# version 15
# baseline (speedup 1.0000x reference)
import contextlib
import numpy as np
import ml_dtypes
import concourse.bass as bass
import concourse.mybir as mybir
from concourse.bass_utils import run_bass_kernel_spmd

F32 = mybir.dt.float32
BF16 = mybir.dt.bfloat16
AF = mybir.ActivationFunctionType
ALU = mybir.AluOpType
AX = mybir.AxisListType

T = 2048
D = 2048
FOUT = 8512
I4 = 4096
NBC = 4352
NH = 64
PP = 64
NS = 128
CH = 128
NCH = T // CH
EXP = 8
HE = 1024
KT = D // 128
NTT = T // 128
EPS_LN = 1e-5
EPS_RMS = 1e-5

ENGS = ("pe", "dve", "act", "pool", "sp")
NLANES = 8
QSEMS = tuple(ENGS) + tuple(
    f"{e}q{i}" for e in ("pool", "sp", "act") for i in range(NLANES))


class Res:
    __slots__ = ("name", "writer", "readers")

    def __init__(self, name):
        self.name = name
        self.writer = None
        self.readers = []


class Op:
    __slots__ = ("fn", "eng", "qsem", "inc_amt", "waits", "need_inc",
                 "sem_val_after", "dbg")

    def __init__(self, fn, eng, qsem, inc_amt, dbg):
        self.fn = fn
        self.eng = eng
        self.qsem = qsem
        self.inc_amt = inc_amt
        self.waits = []
        self.need_inc = False
        self.sem_val_after = 0
        self.dbg = dbg


class EM:

    def __init__(self):
        self.streams = {e: [] for e in ENGS}
        self.nres = 0
        self.dma_count = {e: 0 for e in ("pool", "sp", "act")}
        self.dma_lane_prev = {e: [None] * NLANES for e in ("pool", "sp", "act")}

    def res(self, name=None):
        self.nres += 1
        return Res(name or f"r{self.nres}")

    def op(self, eng, fn, reads=(), writes=(), dma=False, dbg=""):
        lane_prev = None
        if dma:
            j = self.dma_count[eng]
            self.dma_count[eng] = j + 1
            lane = j % NLANES
            qsem = f"{eng}q{lane}"
            lane_prev = self.dma_lane_prev[eng][lane]
        else:
            qsem = eng
        o = Op(fn, eng, qsem, 16 if dma else 1, dbg)
        if dma:
            self.dma_lane_prev[eng][lane] = o
        tgt = []
        if lane_prev is not None:
            tgt.append(lane_prev)
        for r in reads:
            if r.writer is not None:
                tgt.append(r.writer)
        for r in writes:
            if r.writer is not None:
                tgt.append(r.writer)
            tgt.extend(r.readers)
        seen = set()
        for t in tgt:
            if t is o or id(t) in seen:
                continue
            seen.add(id(t))
            if o.qsem == "pe" and t.qsem == "pe":
                continue
            o.waits.append(t)
        for r in reads:
            r.readers.append(o)
        for r in writes:
            r.writer = o
            r.readers = []
        self.streams[eng].append(o)
        return o

    def barrier(self):
        lasts = []
        for s in self.streams.values():
            for t in reversed(s):
                if t.fn is not None and not t.qsem[-1].isdigit():
                    lasts.append(t)
                    break
        for e in ("pool", "sp", "act"):
            for prev in self.dma_lane_prev[e]:
                if prev is not None:
                    lasts.append(prev)
        for eng in ENGS:
            o = Op(None, eng, eng, 1, "barrier")
            o.waits.extend(lasts)
            self.streams[eng].append(o)

    def finalize(self, nc, sems):
        for s in self.streams.values():
            for o in s:
                if o.qsem.endswith("q"):
                    o.need_inc = True
                for t in o.waits:
                    t.need_inc = True
        cum = {q: 0 for q in QSEMS}
        for eng in ENGS:
            for o in self.streams[eng]:
                if o.need_inc:
                    cum[o.qsem] += o.inc_amt
                o.sem_val_after = cum[o.qsem]
        knows = {e: {q: 0 for q in QSEMS} for e in ENGS}
        stats = {"waits": 0, "incs": 0, "ops": 0}

        def emit_stream(engname):
            def run(eh):
                kn = knows[engname]
                for o in self.streams[engname]:
                    stats["ops"] += 1
                    need = {}
                    for t in o.waits:
                        v = t.sem_val_after
                        if v > need.get(t.qsem, 0):
                            need[t.qsem] = v
                    for q in QSEMS:
                        if q in need and kn[q] < need[q]:
                            eh.wait_ge(sems[q], need[q])
                            kn[q] = need[q]
                            stats["waits"] += 1
                    if o.fn is not None:
                        inst = o.fn(eh)
                        if o.need_inc:
                            assert inst is not None, (o.eng, o.dbg)
                            inst.then_inc(sems[o.qsem], o.inc_amt)
                            stats["incs"] += 1
            return run

        with nc.Block() as block:
            block.tensor(emit_stream("pe"))
            block.vector(emit_stream("dve"))
            block.scalar(emit_stream("act"))
            block.gpsimd(emit_stream("pool"))
            block.sync(emit_stream("sp"))
        return stats


class KB:

    def __init__(self, dbg=()):
        self.dbg = set(dbg)
        self.nc = bass.Bass("TRN2", target_bir_lowering=False)
        self.em = EM()
        self.stack = contextlib.ExitStack()
        self.sems = {}
        for q in QSEMS:
            self.sems[q] = self.stack.enter_context(self.nc.semaphore("sem_" + q))
        self.dr = {}

    def din(self, name, shape, dt=F32):
        ap = self.nc.dram_tensor(name, list(shape), dt, kind="ExternalInput").ap()
        self.dr[name] = ap
        return ap

    def dout(self, name, shape, dt=F32):
        ap = self.nc.dram_tensor(name, list(shape), dt, kind="ExternalOutput").ap()
        self.dr[name] = ap
        return ap

    def dint(self, name, shape, dt):
        ap = self.nc.dram_tensor(name, list(shape), dt, kind="Internal").ap()
        self.dr[name] = ap
        return ap

    def sb(self, name, shape, dt, scope=None, side=None):
        st = scope if scope is not None else self.stack
        if side is None:
            return st.enter_context(self.nc.sbuf_tensor(name, list(shape), dt))
        return st.enter_context(
            self.nc.sbuf_tensor(name, list(shape), dt, side=side))

    def close_scope(self, scope):
        self.em.barrier()
        scope.close()

    def ps(self, name, shape, dt=F32):
        return self.stack.enter_context(self.nc.psum_tensor(name, list(shape), dt))

    def build(self):
        em = self.em
        self.x_h = self.din("x", [T, D])
        self.winT_h = self.din("winT", [D, FOUT])
        self.woutT_h = self.din("woutT", [I4, D])
        self.wupT_h = self.din("wupT", [D, EXP * HE])
        self.wdT_h = self.din("wdT", [EXP, HE, D])
        self.wg1T_h = self.din("wg1T", [D, EXP])
        self.wg2T_h = self.din("wg2T", [EXP, EXP])
        self.convw_h = self.din("convw", [128, (NBC // 128) * 4])
        self.convb_h = self.din("convb", [128, NBC // 128])
        self.dtb_h = self.din("dtb", [NH, 1])
        self.alog_h = self.din("alog", [NH, 1])
        self.drep_h = self.din("drep", [1, I4])
        self.nw_h = self.din("nw", [128, I4 // 128])
        self.idf_h = self.din("idf", [128, 128])
        self.tril_h = self.din("tril", [128, 128])
        self.ones_h = self.din("onesrow", [1, 128])
        self.out_h = self.dout("out", [T, D], F32)
        self.zs_h = self.dint("zs", [T, I4], BF16)
        self.xts_h = self.dint("xts", [NCH, I4, CH], BF16)
        self.ffn_h = self.dint("ffn", [T, D], BF16)
        self.ygt_h = self.dint("ygt", [NCH, I4, CH], BF16)
        self.acr_h = self.dint("acr", [NCH, 3, NH * CH], BF16)

        self.r_out = em.res("out")
        self.r_zs = [em.res(f"zs{t}") for t in range(NTT)]
        self.r_xts = [em.res(f"xts{c}") for c in range(NCH)]
        self.r_ffn = [em.res(f"ffn{t}") for t in range(NTT)]
        self.r_ygt = [em.res(f"ygt{c}") for c in range(NCH)]
        self.r_acr = [em.res(f"acr{c}") for c in range(NCH)]

        self.idf = self.sb("idf_s", [128, 128], F32)
        self.id16 = self.sb("id16", [128, 128], BF16)
        self.tril = self.sb("tril_s", [128, 128], F32)
        self.ones1 = self.sb("ones1", [1, 128], F32)
        self.ones16 = self.sb("ones16", [1, 128], BF16)
        self.r_idf = em.res("idf")
        self.r_id16 = em.res("id16")
        self.r_tril = em.res("tril")
        self.r_ones1 = em.res("ones1")
        self.r_ones16 = em.res("ones16")
        em.op("sp", lambda e: e.dma_start(self.idf[:, :], self.idf_h),
              writes=[self.r_idf], dma=True)
        em.op("pool", lambda e: e.dma_start(self.id16[:, :], self.idf_h),
              writes=[self.r_id16], dma=True)
        em.op("sp", lambda e: e.dma_start(self.tril[:, :], self.tril_h),
              writes=[self.r_tril], dma=True)
        em.op("sp", lambda e: e.dma_start(self.ones1[:, :], self.ones_h),
              writes=[self.r_ones1], dma=True)
        em.op("pool", lambda e: e.dma_start(self.ones16[:, :], self.ones_h),
              writes=[self.r_ones16], dma=True)

        self.psumAll = self.ps("psumAll", [128, 4096], F32)
        self.pbank = [self.psumAll[:, i * 512:(i + 1) * 512] for i in range(8)]
        self.r_pb = [em.res(f"pb{i}") for i in range(8)]
        self.pview = self.psumAll

        self.cconst = self.sb("cconst", [128, 8], F32)
        self.r_cconst = em.res("cconst")
        self._cmap = {}
        return self.build_phases()

    def constcol(self, val):
        if val not in self._cmap:
            i = len(self._cmap)
            assert i < 8
            self._cmap[val] = i
            self.em.op("dve", lambda e, i=i, val=val: e.memset(
                self.cconst[:, i:i + 1], float(val)),
                writes=[self.r_cconst], dbg=f"const{val}")
        i = self._cmap[val]
        return self.cconst[:, i:i + 1]

    def build_phases(self):
        import contextlib as _ctx
        self.xns = _ctx.ExitStack()
        self.rscope = _ctx.ExitStack()
        self.phase0()
        if "skipF" not in self.dbg:
            self.phaseF()
        self.phaseA()
        self.close_scope(self.xns)
        if "stopA" in self.dbg:
            self.stub_out()
            self.em.barrier()
            return self
        self.phaseB()
        self.close_scope(self.rscope)
        if "stopB" in self.dbg:
            self.stub_out()
            self.em.barrier()
            return self
        self.phaseD()
        self.em.barrier()
        return self

    def phaseA(self):
        em = self.em
        import contextlib as _ctx
        pas = _ctx.ExitStack()
        pb, r_pb = self.pbank, self.r_pb

        self.bct = self.sb("bct", [128, 2 * T], BF16, self.rscope, side="right")
        self.r_bct = [em.res("bT"), em.res("cT")]
        self.aT = self.sb("aT", [NH, T], F32, self.rscope, side="right")
        self.r_aT = em.res("aT")
        self.dt_tok = self.sb("dt_tok", [128, NCH * NH], BF16, self.rscope,
                              side="right")
        self.r_dtok = em.res("dt_tok")

        wz = [self.sb(f"wz{b}", [128, KT * 512], BF16, pas) for b in range(2)]
        r_wz = [em.res(f"wz{b}") for b in range(2)]
        wx = [self.sb(f"wx{b}", [128, KT * 128], BF16, pas) for b in range(2)]
        r_wx = [em.res(f"wx{b}") for b in range(2)]
        z16 = [self.sb(f"z16_{b}", [128, 512], BF16, pas) for b in range(2)]
        r_z16 = [em.res(f"z16{b}") for b in range(2)]
        cacc = [self.sb(f"cacc{b}", [128, T], F32, pas) for b in range(2)]
        r_cacc = [em.res(f"cacc{b}") for b in range(2)]
        xt16 = [self.sb(f"xt16_{b}", [128, T], BF16, pas) for b in range(2)]
        r_xt16 = [em.res(f"xt16{b}") for b in range(2)]
        convw = self.sb("convw_s", [128, 34 * 4], F32, pas)
        convb = self.sb("convb_s", [128, 34], F32, pas)
        r_cw, r_cb = em.res(), em.res()
        dtr = self.sb("dtr", [NH, T], F32, pas)
        dtt = self.sb("dtt", [NH, T], F32, pas)
        dtv = self.sb("dtv", [NH, T], F32, pas)
        r_dtr, r_dtt, r_dtv = em.res(), em.res(), em.res()
        acol = self.sb("acol", [NH, 2], F32, pas)
        r_acol = em.res()

        em.op("sp", lambda e: e.dma_start(convw[:, :], self.convw_h),
              writes=[r_cw], dma=True)
        em.op("sp", lambda e: e.dma_start(convb[:, :], self.convb_h),
              writes=[r_cb], dma=True)

        for fs in range(8):
            wb, r_wb = wz[fs % 2], r_wz[fs % 2]
            em.op("pool", lambda e, fs=fs, wb=wb: e.dma_start(
                wb[:, :].rearrange("p (k c) -> p k c", k=KT),
                self.winT_h.rearrange("(k p) f -> p k f", k=KT)[
                    :, :, fs * 512:(fs + 1) * 512]),
                writes=[r_wb], dma=True, dbg=f"wz{fs}")
            for tt in range(NTT):
                bi = tt % 2
                def zmm(e, wb=wb, tt=tt, bi=bi):
                    last = None
                    for k in range(KT):
                        last = e.matmul(
                            pb[bi][:, :],
                            self.xnTb[:, :].rearrange("p (k t) -> p k t", k=KT)[
                                :, k, tt * 128:(tt + 1) * 128],
                            wb[:, :].rearrange("p (k c) -> p k c", k=KT)[:, k],
                            start=(k == 0), stop=(k == KT - 1))
                    return last
                em.op("pe", zmm, reads=[r_wb] + self.r_xnT, writes=[r_pb[bi]],
                      dbg=f"zmm{fs}_{tt}")
                zb, r_zb = z16[tt % 2], r_z16[tt % 2]
                em.op("act", lambda e, bi=bi, zb=zb: e.activation(
                    zb[:, :], pb[bi][:, :], AF.Silu),
                    reads=[r_pb[bi]], writes=[r_zb], dbg=f"zsilu{fs}_{tt}")
                em.op("sp", lambda e, fs=fs, tt=tt, zb=zb: e.dma_start(
                    self.zs_h[tt * 128:(tt + 1) * 128,
                              fs * 512:(fs + 1) * 512], zb[:, :]),
                    reads=[r_zb], writes=[self.r_zs[tt]], dma=True,
                    dbg=f"zst{fs}_{tt}")

        for j in range(34):
            wb, r_wb = wx[j % 2], r_wx[j % 2]
            f0 = I4 + j * 128
            em.op("pool", lambda e, f0=f0, wb=wb: e.dma_start(
                wb[:, :].rearrange("p (k c) -> p k c", k=KT),
                self.winT_h.rearrange("(k p) f -> p k f", k=KT)[
                    :, :, f0:f0 + 128]),
                writes=[r_wb], dma=True, dbg=f"wx{j}")
            qoff = (j % 2) * 4
            r_quad = [r_pb[qoff + n] for n in range(4)]
            for n in range(4):
                bi = qoff + n
                def xmm(e, wb=wb, n=n, bi=bi):
                    last = None
                    for k in range(KT):
                        last = e.matmul(
                            pb[bi][:, :],
                            wb[:, :].rearrange("p (k c) -> p k c", k=KT)[:, k],
                            self.xnTb[:, :].rearrange("p (k t) -> p k t", k=KT)[
                                :, k, n * 512:(n + 1) * 512],
                            start=(k == 0), stop=(k == KT - 1))
                    return last
                em.op("pe", xmm, reads=[r_wb] + self.r_xnT, writes=[r_pb[bi]],
                      dbg=f"xmm{j}_{n}")
            cb_, r_cc = cacc[j % 2], r_cacc[j % 2]
            pvw = self.pview[:, qoff * 512:(qoff + 4) * 512]
            em.op("act", lambda e, j=j, cb_=cb_, pvw=pvw: e.activation(
                cb_[:, :], pvw, AF.Copy, scale=convw[:, j * 4 + 3:j * 4 + 4]),
                reads=r_quad + [r_cw], writes=[r_cc], dbg=f"cv3_{j}")
            for kk in (2, 1, 0):
                sft = 3 - kk
                em.op("dve", lambda e, j=j, kk=kk, sft=sft, cb_=cb_, pvw=pvw:
                      e.scalar_tensor_tensor(
                          cb_[:, sft:T], pvw[:, 0:T - sft],
                          convw[:, j * 4 + kk:j * 4 + kk + 1],
                          cb_[:, sft:T], ALU.mult, ALU.add),
                      reads=r_quad + [r_cw, r_cc], writes=[r_cc],
                      dbg=f"cv{kk}_{j}")
            if j < 32:
                ob, r_ob = xt16[j % 2], r_xt16[j % 2]
                em.op("act", lambda e, j=j, cb_=cb_, ob=ob: e.activation(
                    ob[:, :], cb_[:, :], AF.Silu,
                    bias=convb[:, j:j + 1]),
                    reads=[r_cc, r_cb], writes=[r_ob], dbg=f"cvs{j}")
                em.op("sp", lambda e, j=j, ob=ob: e.dma_start(
                    self.xts_h[:, j * 128:(j + 1) * 128, :].transpose([1, 0, 2]),
                    ob[:, :].rearrange("p (c t) -> p c t", c=NCH)),
                    reads=[r_ob], writes=self.r_xts, dma=True, dbg=f"xst{j}")
            else:
                bi2 = j - 32
                em.op("act", lambda e, j=j, bi2=bi2, cb_=cb_: e.activation(
                    self.bct[:, bi2 * T:(bi2 + 1) * T], cb_[:, :], AF.Silu,
                    bias=convb[:, j:j + 1]),
                    reads=[r_cc, r_cb], writes=[self.r_bct[bi2]], dbg=f"cvs{j}")

        def dtmm(e, n=None):
            last = None
            for k in range(KT):
                last = e.matmul(
                    pb[6][0:NH, 0:512],
                    self.winT_h_dt[:, :].rearrange("p (k c) -> p k c", k=KT)[:, k],
                    self.xnTb[:, :].rearrange("p (k t) -> p k t", k=KT)[
                        :, k, n * 512:(n + 1) * 512],
                    start=(k == 0), stop=(k == KT - 1))
            return last
        wdt = self.sb("wdt", [128, KT * NH], BF16, pas)
        r_wdt = em.res()
        self.winT_h_dt = wdt
        em.op("pool", lambda e: e.dma_start(
            wdt[:, :].rearrange("p (k c) -> p k c", k=KT),
            self.winT_h.rearrange("(k p) f -> p k f", k=KT)[
                :, :, FOUT - NH:FOUT]),
            writes=[r_wdt], dma=True, dbg="wdt")
        for n in range(4):
            em.op("pe", lambda e, n=n: dtmm(e, n=n),
                  reads=[r_wdt] + self.r_xnT, writes=[r_pb[6]], dbg=f"dtmm{n}")
            em.op("dve", lambda e, n=n: e.tensor_copy(
                dtr[:, n * 512:(n + 1) * 512], pb[6][0:NH, 0:512]),
                reads=[r_pb[6]], writes=[r_dtr], dbg=f"dtev{n}")

        dtb = self.sb("dtb_s", [NH, 1], F32, pas)
        r_dtb = em.res()
        em.op("sp", lambda e: e.dma_start(dtb[:, :], self.dtb_h),
              writes=[r_dtb], dma=True)
        em.op("sp", lambda e: e.dma_start(acol[:, 0:1], self.alog_h),
              writes=[r_acol], dma=True)
        onec = self.constcol(1.0)
        rcc = self.r_cconst
        em.op("dve", lambda e: e.tensor_scalar(
            dtr[:, :], dtr[:, :], dtb[:, :], None, ALU.add),
            reads=[r_dtr, r_dtb], writes=[r_dtr])
        em.op("act", lambda e: e.activation(dtt[:, :], dtr[:, :], AF.Abs),
              reads=[r_dtr], writes=[r_dtt], dbg="abs")
        em.op("act", lambda e: e.activation(dtt[:, :], dtt[:, :], AF.Exp,
                                            scale=-1.0),
              reads=[r_dtt], writes=[r_dtt])
        em.op("act", lambda e: e.activation(dtt[:, :], dtt[:, :], AF.Ln,
                                            bias=onec[0:NH, :]),
              reads=[r_dtt, rcc], writes=[r_dtt])
        em.op("dve", lambda e: e.tensor_scalar(
            dtv[:, :], dtr[:, :], 0.0, None, ALU.max),
            reads=[r_dtr], writes=[r_dtv])
        em.op("dve", lambda e: e.tensor_tensor(
            dtv[:, :], dtv[:, :], dtt[:, :], ALU.add),
            reads=[r_dtv, r_dtt], writes=[r_dtv])
        em.op("dve", lambda e: e.tensor_scalar(
            dtv[:, :], dtv[:, :], 100.0, None, ALU.min),
            reads=[r_dtv], writes=[r_dtv], dbg="clip")
        em.op("act", lambda e: e.activation(acol[:, 1:2], acol[:, 0:1], AF.Exp),
              reads=[r_acol], writes=[r_acol])
        em.op("dve", lambda e: e.tensor_scalar(
            acol[:, 1:2], acol[:, 1:2], -1.0, None, ALU.mult),
            reads=[r_acol], writes=[r_acol])
        em.op("dve", lambda e: e.tensor_scalar(
            self.aT[:, :], dtv[:, :], acol[:, 1:2], None, ALU.mult),
            reads=[r_dtv, r_acol], writes=[self.r_aT], dbg="aT")
        for c in range(NCH):
            em.op("pe", lambda e, c=c: e.transpose(
                pb[7][0:128, 0:NH], dtv[:, c * CH:(c + 1) * CH],
                self.idf[0:NH, 0:NH]),
                reads=[r_dtv, self.r_idf], writes=[r_pb[7]], dbg=f"trdt{c}")
            em.op("dve", lambda e, c=c: e.tensor_copy(
                self.dt_tok[:, c * NH:(c + 1) * NH], pb[7][0:128, 0:NH]),
                reads=[r_pb[7]], writes=[self.r_dtok], dbg=f"dtokev{c}")

        if "zxbcdt" in self.dbg:
            oz = self.dout("dbg_z", [T, I4], BF16)
            for tt in range(NTT):
                em.op("sp", lambda e, tt=tt, oz=oz: e.dma_start(
                    oz[tt * 128:(tt + 1) * 128, :],
                    self.zs_h[tt * 128:(tt + 1) * 128, :]),
                    reads=[self.r_zs[tt]], dma=True)
            ox = self.dout("dbg_xts", [NCH, I4, CH], BF16)
            em.op("sp", lambda e, ox=ox: e.dma_start(ox, self.xts_h),
                  reads=self.r_xts, dma=True)
            ob = self.dout("dbg_bct", [128, 2 * T], BF16)
            em.op("sp", lambda e, ob=ob: e.dma_start(ob, self.bct[:, :]),
                  reads=self.r_bct, dma=True)
            od = self.dout("dbg_dtv", [NH, T], F32)
            em.op("sp", lambda e, od=od: e.dma_start(od, dtv[:, :]),
                  reads=[r_dtv], dma=True)
        self.close_scope(pas)

    def phaseB(self):
        em = self.em
        import contextlib as _ctx
        pbs = _ctx.ExitStack()
        pb, r_pb = self.pbank, self.r_pb
        epsc = self.constcol(EPS_RMS)
        rcc = self.r_cconst

        def dup(mk, n=2):
            return [mk(i) for i in range(n)]

        xtc = dup(lambda i: self.sb(f"xtc{i}", [128, 32 * CH], BF16, pbs))
        r_xtc = dup(lambda i: em.res(f"xtc{i}"))
        zc0 = self.sb("zc", [128, I4], BF16, pbs)
        zcb = [zc0, zc0]
        r_zc0 = em.res("zc")
        r_zcb = [r_zc0, r_zc0]
        Xcb = dup(lambda i: self.sb(f"Xc{i}", [128, I4], BF16, pbs))
        r_Xcb = dup(lambda i: em.res(f"Xc{i}"))
        Xdb = dup(lambda i: self.sb(f"Xd{i}", [128, I4], BF16, pbs))
        r_Xdb = dup(lambda i: em.res(f"Xd{i}"))
        XDs = self.sb("XD", [128, I4], BF16, pbs)
        XDb = [XDs, XDs]
        r_XDs = em.res("XD")
        r_XDb = [r_XDs, r_XDs]
        P32 = self.sb("P32", [128, I4], F32, pbs)
        P16 = self.sb("P16", [128, I4], BF16, pbs)
        r_P32h = [em.res("P32a"), em.res("P32b")]
        r_P16h = [em.res("P16a"), em.res("P16b")]
        y32 = self.sb("y32", [128, I4], BF16, pbs)
        r_y32 = em.res()
        yg0 = self.sb("yg16", [128, I4], BF16, pbs)
        yg16b = [yg0, yg0]
        r_yg0 = em.res("yg0")
        r_ygb = [r_yg0, r_yg0]
        Dq0 = self.sb("Dq0", [128, 16 * CH], F32, pbs)
        Dq = [Dq0, Dq0]
        r_Dq0 = em.res("Dq0")
        r_Dq = [r_Dq0, r_Dq0]
        Mq = dup(lambda i: self.sb(f"Mq{i}", [128, 16 * CH], BF16, pbs), 8)
        r_Mq = dup(lambda i: em.res(f"Mq{i}"), 8)
        dl1 = dup(lambda i: self.sb(f"dl1_{i}", [128, CH], BF16, pbs), 4)
        r_dl1 = dup(lambda i: em.res(f"dl1_{i}"), 4)
        dr1 = dup(lambda i: self.sb(f"dr1_{i}", [128, 16 * CH], BF16, pbs), 4)
        r_dr1 = dup(lambda i: em.res(f"dr1_{i}"), 4)
        Gmb = dup(lambda i: self.sb(f"Gm{i}", [128, CH], BF16, pbs))
        r_Gmb = dup(lambda i: em.res(f"Gm{i}"))
        stkb = dup(lambda i: self.sb(f"stk{i}", [128, CH], F32, pbs))
        r_stkb = dup(lambda i: em.res(f"stk{i}"))
        atokb = dup(lambda i: self.sb(f"atok{i}", [128, NH], F32, pbs))
        dstokb = dup(lambda i: self.sb(f"dstok{i}", [128, NH], BF16, pbs))
        eatokb = dup(lambda i: self.sb(f"eatok{i}", [128, NH], F32, pbs))
        r_atokb = dup(lambda i: em.res(f"atok{i}"))
        r_dstokb = dup(lambda i: em.res(f"dst{i}"))
        r_eatokb = dup(lambda i: em.res(f"eat{i}"))
        dtdsb = dup(lambda i: self.sb(f"dtds{i}", [128, NH], BF16, pbs))
        r_dtdsb = dup(lambda i: em.res(f"dtds{i}"))
        gbcb = dup(lambda i: self.sb(f"gbc{i}", [128, NH], F32, pbs))
        r_gbcb = dup(lambda i: em.res(f"gbc{i}"))
        asplitb = dup(lambda i: self.sb(f"asplit{i}", [NH, 3 * CH], BF16, pbs))
        r_aspb = dup(lambda i: em.res(f"asplit{i}"))
        rres = self.sb("rres", [NH, CH], F32, pbs)
        r_rres = em.res("rres")
        onesf = self.sb("onesf", [NH, CH], F32, pbs)
        r_onesf = em.res()
        dfull = self.sb("dfull", [128, I4], BF16, pbs)
        r_dfull = em.res("dfull")
        ygs0 = self.sb("ygs0", [128, 1024], BF16, pbs)
        ygstage = [ygs0, ygs0]
        r_ygs0 = em.res("ygs0")
        r_ygst = [r_ygs0, r_ygs0]
        btokb = dup(lambda i: self.sb(f"btok{i}", [128, NS], BF16, pbs))
        r_btokb = dup(lambda i: em.res(f"btok{i}"))
        self.gscr_h = self.dint("gscr", [NCH, NH], F32)
        r_gscr = [em.res(f"g{c}") for c in range(NCH)]
        self.negHE_h = self.din("negHE", [32, 32 * CH])

        em.op("dve", lambda e: e.memset(onesf[:, :], 1.0), writes=[r_onesf])
        em.op("dve", lambda e: e.memset(P32[:, :], 0.0), writes=r_P32h)
        em.op("dve", lambda e: e.memset(P16[:, :], 0.0), writes=r_P16h)
        self.one3_h = self.din("one3", [32, CH])
        for i in range(4):
            em.op("dve", lambda e, i=i: e.memset(dl1[i][:, :], 0.0),
                  writes=[r_dl1[i]])
            em.op("pool", lambda e, i=i: e.dma_start(
                dl1[i][96:128, :], self.one3_h), writes=[r_dl1[i]], dma=True)
        for i in range(4):
            qp = i % 2
            em.op("dve", lambda e, i=i: e.memset(dr1[i][:, :], 0.0),
                  writes=[r_dr1[i]])
            for base in (0, 32, 64):
                em.op("pool", lambda e, i=i, qp=qp, base=base: e.dma_start(
                    dr1[i][base:base + 32, :],
                    self.negHE_h[:, qp * 2048:(qp + 1) * 2048]),
                    writes=[r_dr1[i]], dma=True)
        em.op("pool", lambda e: e.dma_start(
            dfull[:, :], bass.AP(self.drep_h.tensor, 0, [[0, 128], [1, I4]])),
            writes=[r_dfull], dma=True, dbg="dfull")
        em.op("sp", lambda e: e.dma_start(
            xtc[0][:, :].rearrange("p (j t) -> p j t", j=32),
            self.xts_h[0].rearrange("(j p) t -> p j t", j=32)),
            reads=self.r_xts, writes=[r_xtc[0]], dma=True, dbg="ldx0")

        def stageA(c):
            P = c % 2
            stk, r_stk = stkb[P], r_stkb[P]
            atok, dstok, eatok = atokb[P], dstokb[P], eatokb[P]
            r_atok, r_dstok, r_eatok = r_atokb[P], r_dstokb[P], r_eatokb[P]
            asplit, r_asp = asplitb[P], r_aspb[P]
            if c + 1 < NCH:
                em.op("sp", lambda e, c=c, P=P: e.dma_start(
                    xtc[1 - P][:, :].rearrange("p (j t) -> p j t", j=32),
                    self.xts_h[c + 1].rearrange("(j p) t -> p j t", j=32)),
                    reads=self.r_xts, writes=[r_xtc[1 - P]], dma=True,
                    dbg=f"ldx{c+1}")
            em.op("dve", lambda e, c=c, stk=stk: e.tensor_tensor_scan(
                stk[0:NH, :], onesf[:, :], self.aT[:, c * CH:(c + 1) * CH],
                0.0, ALU.mult, ALU.add),
                reads=[r_onesf, self.r_aT], writes=[r_stk], dbg=f"cums{c}")
            em.op("dve", lambda e, stk=stk: e.tensor_scalar(
                stk[64:128, :], stk[0:NH, :], stk[0:NH, CH - 1:CH], None,
                ALU.subtract),
                reads=[r_stk], writes=[r_stk], dbg="dsT")
            em.op("dve", lambda e, stk=stk, asplit=asplit: e.tensor_copy(
                asplit[:, 0:CH], stk[0:NH, :]),
                reads=[r_stk], writes=[r_asp], dbg=f"a1_{c}")
            em.op("dve", lambda e, stk=stk, asplit=asplit: e.tensor_tensor(
                rres[:, :], stk[0:NH, :], asplit[:, 0:CH], ALU.subtract),
                reads=[r_stk, r_asp], writes=[r_rres], dbg=f"r2_{c}")
            em.op("dve", lambda e, asplit=asplit: e.tensor_copy(
                asplit[:, CH:2 * CH], rres[:, :]),
                reads=[r_rres], writes=[r_asp], dbg=f"a2_{c}")
            em.op("dve", lambda e, asplit=asplit: e.tensor_tensor(
                rres[:, :], rres[:, :], asplit[:, CH:2 * CH], ALU.subtract),
                reads=[r_rres, r_asp], writes=[r_rres], dbg=f"r3_{c}")
            em.op("dve", lambda e, asplit=asplit: e.tensor_copy(
                asplit[:, 2 * CH:3 * CH], rres[:, :]),
                reads=[r_rres], writes=[r_asp], dbg=f"a3_{c}")
            em.op("sp", lambda e, c=c, asplit=asplit: e.dma_start(
                self.acr_h[c].rearrange("i (h l) -> h i l", h=NH),
                asplit[:, :].rearrange("p (i l) -> p i l", i=3)),
                reads=[r_asp], writes=[self.r_acr[c]], dma=True, dbg=f"acr{c}")
            em.op("pe", lambda e, stk=stk: e.transpose(
                pb[0][:, 0:128], stk[:, :], self.idf[:, :]),
                reads=[r_stk, self.r_idf], writes=[r_pb[0]], dbg=f"trstk{c}")
            em.op("act", lambda e, atok=atok: e.activation(
                atok[:, :], pb[0][:, 0:NH], AF.Copy),
                reads=[r_pb[0]], writes=[r_atok], dbg="atok")
            em.op("act", lambda e, dstok=dstok: e.activation(
                dstok[:, :], pb[0][:, NH:128], AF.Exp, scale=-1.0),
                reads=[r_pb[0], r_atok], writes=[r_dstok], dbg="dstok")
            em.op("act", lambda e, eatok=eatok, atok=atok: e.activation(
                eatok[:, :], atok[:, :], AF.Exp),
                reads=[r_atok], writes=[r_eatok], dbg="eatok")
            em.op("dve", lambda e, c=c, P=P, dstok=dstok: e.tensor_tensor(
                dtdsb[P][:, :], self.dt_tok[:, c * NH:(c + 1) * NH],
                dstok[:, :], ALU.mult),
                reads=[self.r_dtok, r_dstok], writes=[r_dtdsb[P]],
                dbg=f"dtds{c}")
            em.op("pool", lambda e, c=c, stk=stk: e.dma_start(
                self.gscr_h[c:c + 1, :].rearrange("o h -> h o"),
                stk[0:NH, CH - 1:CH]),
                reads=[r_stk], writes=[r_gscr[c]], dma=True, dbg=f"gst{c}")
            em.op("pool", lambda e, c=c, P=P: e.dma_start(
                gbcb[P][:, :],
                bass.AP(self.gscr_h.tensor, c * NH, [[0, 128], [1, NH]])),
                reads=[r_gscr[c]], writes=[r_gbcb[P]], dma=True, dbg=f"gbc{c}")
            em.op("act", lambda e, P=P: e.activation(
                gbcb[P][:, :], gbcb[P][:, :], AF.Exp),
                reads=[r_gbcb[P]], writes=[r_gbcb[P]], dbg=f"gexp{c}")
            em.op("pe", lambda e, c=c: e.transpose(
                pb[0][:, 0:128].bitcast(BF16)[:, 0:128],
                self.bct[:, c * CH:(c + 1) * CH], self.id16[:, :]),
                reads=[self.r_bct[0], self.r_id16], writes=[r_pb[0]],
                dbg=f"trB{c}")
            em.op("act", lambda e, P=P: e.activation(
                btokb[P][:, :], pb[0][:, 0:128].bitcast(BF16)[:, 0:128],
                AF.Copy),
                reads=[r_pb[0]], writes=[r_btokb[P]], dbg="btok")
            em.op("pe", lambda e, c=c: e.matmul(
                pb[1][:, 0:128], self.bct[:, c * CH:(c + 1) * CH],
                self.bct[:, T + c * CH:T + (c + 1) * CH],
                start=True, stop=True),
                reads=self.r_bct, writes=[r_pb[1]], dbg=f"G{c}")
            em.op("dve", lambda e, P=P: e.tensor_tensor(
                Gmb[P][:, :], pb[1][:, 0:128], self.tril[:, :], ALU.mult),
                reads=[r_pb[1], self.r_tril], writes=[r_Gmb[P]], dbg="Gm")

        def stageC(c):
            P = c % 2
            xb, r_xb = xtc[P], r_xtc[P]
            Xc, r_Xc = Xcb[P], r_Xcb[P]
            Xd, r_Xd = Xdb[P], r_Xdb[P]
            XD, r_XD = XDb[P], r_XDb[P]
            asplit, r_asp = asplitb[P], r_aspb[P]
            dstok, r_dstok = dstokb[P], r_dstokb[P]
            for gg in range(4):
                bp = (gg % 2) * 2
                for sub in range(2):
                    g = gg * 2 + sub
                    pv = pb[bp + sub][:, :].bitcast(BF16)

                    def trx(e, g=g, pv=pv, xb=xb):
                        last = None
                        for j in range(4):
                            last = e.transpose(
                                pv[:, j * 128:(j + 1) * 128],
                                xb[:, (g * 4 + j) * CH:(g * 4 + j + 1) * CH],
                                self.id16[:, :])
                        return last
                    em.op("pe", trx, reads=[r_xb, self.r_id16],
                          writes=[r_pb[bp + sub]], dbg=f"trx{c}_{g}")
                pvb = self.pview[:, :].bitcast(BF16).rearrange(
                    "p (b x) -> p b x", b=8)[:, bp:bp + 2, 0:512]
                wide4 = pvb.rearrange("p b (h q) -> p b h q", h=8)
                sl = slice(gg * 1024, (gg + 1) * 1024)
                em.op("dve", lambda e, gg=gg, sl=sl, wide4=wide4, c=c, Xc=Xc:
                      e.tensor_tensor(
                      Xc[:, sl].rearrange("p (b h q) -> p b h q", b=2, h=8),
                      wide4,
                      self.dt_tok[:, c * NH + gg * 16:c * NH + (gg + 1) * 16]
                      .rearrange("p (b h) -> p b h", b=2)
                      .unsqueeze(3).broadcast_to([128, 2, 8, PP]),
                      ALU.mult),
                      reads=[r_pb[bp], r_pb[bp + 1], self.r_dtok],
                      writes=[r_Xc], dbg=f"xc{c}_{gg}")
                em.op("dve", lambda e, sl=sl, pvb=pvb, XD=XD: e.tensor_tensor(
                    XD[:, sl].rearrange("p (b x) -> p b x", b=2),
                    pvb, dfull[:, sl].rearrange("p (b x) -> p b x", b=2),
                    ALU.mult),
                    reads=[r_pb[bp], r_pb[bp + 1], r_dfull], writes=[r_XD],
                    dbg=f"xD{c}_{gg}")
                em.op("dve", lambda e, gg=gg, sl=sl, Xd=Xd, Xc=Xc, P=P:
                      e.tensor_tensor(
                      Xd[:, sl].rearrange("p (h q) -> p h q", h=16),
                      Xc[:, sl].rearrange("p (h q) -> p h q", h=16),
                      dstokb[P][:, gg * 16:(gg + 1) * 16]
                      .unsqueeze(2).broadcast_to([128, 16, PP]),
                      ALU.mult),
                      reads=[r_Xc, r_dstokb[P]], writes=[r_Xd],
                      dbg=f"xd{c}_{gg}")


        def decay_chunk(c):
            P = c % 2
            asplit, r_asp = asplitb[P], r_aspb[P]
            for q in range(4):
                half = q // 2
                l1, r_l1 = dl1[P * 2 + half], r_dl1[P * 2 + half]
                if q % 2 == 0:
                    h0 = half * 32
                    for i3, base in ((0, 0), (1, 32), (2, 64)):
                        em.op("dve", lambda e, h0=h0, l1=l1, i3=i3, base=base,
                              asplit=asplit: e.tensor_copy(
                              l1[base:base + 32, :],
                              asplit[h0:h0 + 32, i3 * CH:(i3 + 1) * CH]),
                              reads=[r_asp], writes=[r_l1],
                              dbg=f"dl{i3}_{c}_{half}")
                di = (c % 2) * 2 + (q % 2)
                rb1, r_rb1 = dr1[di], r_dr1[di]
                for i3 in range(3):
                    em.op("pool", lambda e, c=c, q=q, rb1=rb1, i3=i3:
                          e.dma_start(
                          rb1[96 + i3:97 + i3, :],
                          self.acr_h[c, i3, q * 16 * CH:(q + 1) * 16 * CH]
                          .unsqueeze(0)),
                          reads=[self.r_acr[c]], writes=[r_rb1], dma=True,
                          dbg=f"ar{i3}_{c}_{q}")
                db_, r_db = Dq[q % 2], r_Dq[q % 2]
                mb = Mq[P * 4 + q]
                r_mb = r_Mq[P * 4 + q]
                for dd in range(2):
                    for sub in range(2):
                        ssl = dd * 2 + sub
                        bk = 2 + sub
                        cs = slice(ssl * 512, (ssl + 1) * 512)

                        def dmm(e, cs=cs, bk=bk, l1=l1, rb1=rb1):
                            return e.matmul(pb[bk][:, :], l1[:, :],
                                            rb1[:, cs], start=True, stop=True)
                        em.op("pe", dmm,
                              reads=[r_l1, r_rb1],
                              writes=[r_pb[bk]], dbg=f"dmm{c}_{q}_{ssl}")
                    em.op("act", lambda e, dd=dd, db_=db_: e.activation(
                        db_[:, dd * 1024:(dd + 1) * 1024],
                        self.pview[:, 1024:2048], AF.Relu, scale=-1.0),
                        reads=[r_pb[2], r_pb[3]], writes=[r_db],
                        dbg=f"dev{c}_{q}_{dd}")
                em.op("act", lambda e, db_=db_, mb=mb: e.activation(
                    mb[:, :], db_[:, :], AF.Exp, scale=-1.0),
                    reads=[r_db], writes=[r_mb], dbg=f"E{c}_{q}")
                em.op("dve",
                      lambda e, mb=mb, P=P: e.tensor_tensor(
                      mb[:, :].rearrange("p (h l) -> p h l", h=16),
                      mb[:, :].rearrange("p (h l) -> p h l", h=16),
                      Gmb[P][:, :].unsqueeze(1).broadcast_to([128, 16, CH]),
                      ALU.mult),
                      reads=[r_mb, r_Gmb[P]], writes=[r_mb], dbg=f"M{c}_{q}")

        def stageB(c):
            P = c % 2
            Xc, r_Xc = Xcb[P], r_Xcb[P]
            XD, r_XD = XDb[P], r_XDb[P]
            eatok, r_eatok = eatokb[P], r_eatokb[P]
            pbank_h = [pb[4 + i] for i in range(4)]
            r_pbh = [r_pb[4 + i] for i in range(4)]
            em.op("sp", lambda e, c=c: e.dma_start(
                zcb[0][:, :], self.zs_h[c * CH:(c + 1) * CH, :]),
                reads=[self.r_zs[c]], writes=[r_zcb[0]], dma=True,
                dbg=f"ldz{c}")
            for half in range(2):
                def yoff(e, c=c, half=half, pbank_h=pbank_h):
                    last = None
                    for n in range(4):
                        last = e.matmul(
                            pbank_h[n][:, :],
                            self.bct[:, T + c * CH:T + (c + 1) * CH],
                            P16[:, half * 2048 + n * 512:
                                half * 2048 + (n + 1) * 512],
                            start=True, stop=True)
                    return last
                em.op("pe", yoff, reads=[self.r_bct[1], r_P16h[half]],
                      writes=r_pbh, dbg=f"yoff{c}_{half}")
                em.op("dve", lambda e, half=half, P=P: e.tensor_tensor(
                    P32[:, half * 2048:(half + 1) * 2048]
                    .rearrange("p (h q) -> p h q", h=32),
                    P32[:, half * 2048:(half + 1) * 2048]
                    .rearrange("p (h q) -> p h q", h=32),
                    gbcb[P][:, half * 32:(half + 1) * 32].unsqueeze(2)
                    .broadcast_to([128, 32, PP]),
                    ALU.mult),
                    reads=[r_P32h[half], r_gbcb[P]], writes=[r_P32h[half]],
                    dbg=f"Pg{c}_{half}")
                em.op("dve", lambda e, half=half, eatok=eatok:
                      e.tensor_tensor(
                      y32[:, half * 2048:(half + 1) * 2048]
                      .rearrange("p (h q) -> p h q", h=32),
                      self.pview[:, 2048:4096]
                      .rearrange("p (h q) -> p h q", h=32),
                      eatok[:, half * 32:(half + 1) * 32]
                      .unsqueeze(2).broadcast_to([128, 32, PP]),
                      ALU.mult),
                      reads=r_pbh + [r_eatok], writes=[r_y32],
                      dbg=f"ev1_{c}_{half}")

                def yd(e, c=c, half=half, pbank_h=pbank_h, Xc=Xc, XD=XD, P=P):
                    last = None
                    for n in range(4):
                        last = e.matmul(
                            pbank_h[n][:, :], self.id16[:, :],
                            XD[:, half * 2048 + n * 512:
                               half * 2048 + (n + 1) * 512],
                            start=True, stop=False)
                    for hh in range(32):
                        h = half * 32 + hh
                        qq, hq = divmod(hh, 16)
                        n, off = divmod(hh * PP, 512)
                        last = e.matmul(
                            pbank_h[n][:, off:off + PP],
                            Mq[P * 4 + half * 2 + qq][:, hq * CH:(hq + 1) * CH],
                            Xc[:, h * PP:(h + 1) * PP],
                            start=False, stop=(off + PP == 512))
                    return last
                em.op("pe", yd,
                      reads=[self.r_id16, r_XD, r_Xc,
                             r_Mq[P * 4 + half * 2], r_Mq[P * 4 + half * 2 + 1]],
                      writes=r_pbh, dbg=f"yd{c}_{half}")
                em.op("dve", lambda e, half=half: e.scalar_tensor_tensor(
                    y32[:, half * 2048:(half + 1) * 2048],
                    self.pview[:, 2048:4096], 1.0,
                    y32[:, half * 2048:(half + 1) * 2048],
                    ALU.mult, ALU.add),
                    reads=r_pbh + [r_y32], writes=[r_y32],
                    dbg=f"ev2_{c}_{half}")
                stk_, r_stk_ = stkb[P], r_stkb[P]
                em.op("act", lambda e, half=half, stk_=stk_, P=P: e.activation(
                    yg16b[P][:, half * 2048:(half + 1) * 2048],
                    y32[:, half * 2048:(half + 1) * 2048], AF.Square,
                    accum_out=stk_[0:128, 2 + half:3 + half]),
                    reads=[r_y32], writes=[r_ygb[P], r_stk_],
                    dbg=f"ssh{c}_{half}")

        def stageD(c):
            P = c % 2
            Xd, r_Xd = Xdb[P], r_Xdb[P]
            stk, r_stk = stkb[P], r_stkb[P]
            yg16, r_yg = yg16b[P], r_ygb[P]
            pbank_h = [pb[4 + i] for i in range(4)]
            r_pbh = [r_pb[4 + i] for i in range(4)]
            for half in range(2):
                def stmm(e, half=half, pbank_h=pbank_h, Xd=Xd, P=P):
                    last = None
                    for n in range(4):
                        last = e.matmul(
                            pbank_h[n][:, :], btokb[P][:, :],
                            Xd[:, half * 2048 + n * 512:
                               half * 2048 + (n + 1) * 512],
                            start=True, stop=True)
                    return last
                em.op("pe", stmm, reads=[r_btokb[P], r_Xd], writes=r_pbh,
                      dbg=f"st{c}_{half}")
                em.op("dve", lambda e, half=half: e.scalar_tensor_tensor(
                    P16[:, half * 2048:(half + 1) * 2048],
                    self.pview[:, 2048:4096], 1.0,
                    P32[:, half * 2048:(half + 1) * 2048],
                    ALU.mult, ALU.add),
                    reads=r_pbh + [r_P32h[half]], writes=[r_P16h[half]],
                    dbg=f"P16{c}_{half}")
                em.op("dve", lambda e, half=half: e.tensor_tensor(
                    P32[:, half * 2048:(half + 1) * 2048],
                    P32[:, half * 2048:(half + 1) * 2048],
                    self.pview[:, 2048:4096], ALU.add),
                    reads=r_pbh + [r_P32h[half]], writes=[r_P32h[half]],
                    dbg=f"Pa{c}_{half}")

            em.op("dve", lambda e, stk=stk: e.tensor_scalar(
                stk[:, 1:2], stk[:, 2:3], stk[:, 3:4], 1.0 / I4, ALU.add,
                ALU.mult),
                reads=[r_stk], writes=[r_stk], dbg=f"sscomb{c}")
            em.op("act", lambda e, stk=stk: e.activation(
                stk[:, 1:2], stk[:, 1:2], AF.Ln, bias=epsc),
                reads=[r_stk, rcc], writes=[r_stk])
            em.op("act", lambda e, stk=stk: e.activation(
                stk[:, 1:2], stk[:, 1:2], AF.Exp, scale=-0.5),
                reads=[r_stk], writes=[r_stk], dbg=f"rstd{c}")
            em.op("dve", lambda e, yg16=yg16, stk=stk, P=P:
                  e.scalar_tensor_tensor(
                  yg16[:, :], y32[:, :], stk[:, 1:2], zcb[P][:, :],
                  ALU.mult, ALU.mult),
                  reads=[r_y32, r_stk, r_zcb[P]], writes=[r_yg],
                  dbg=f"ygfuse{c}")
            for gg in range(4):
                for sub in range(2):
                    g = gg * 2 + sub
                    pv = pb[2 + sub][:, :].bitcast(BF16)

                    def tryg(e, g=g, pv=pv, yg16=yg16):
                        last = None
                        for j in range(4):
                            last = e.transpose(
                                pv[:, j * 128:(j + 1) * 128],
                                yg16[:, (g * 4 + j) * 128:
                                     (g * 4 + j + 1) * 128],
                                self.id16[:, :])
                        return last
                    em.op("pe", tryg, reads=[r_yg, self.r_id16],
                          writes=[r_pb[2 + sub]], dbg=f"tryg{c}_{g}")
                pvb = self.pview[:, :].bitcast(BF16).rearrange(
                    "p (b x) -> p b x", b=8)[:, 2:4, 0:512]
                sb_, r_sb = ygstage[gg % 2], r_ygst[gg % 2]
                em.op("act", lambda e, pvb=pvb, sb_=sb_: e.activation(
                    sb_[:, :].rearrange("p (b x) -> p b x", b=2), pvb,
                    AF.Copy),
                    reads=[r_pb[2], r_pb[3]], writes=[r_sb],
                    dbg=f"evyg{c}_{gg}")
                em.op("sp", lambda e, c=c, gg=gg, sb_=sb_: e.dma_start(
                    self.ygt_h[c, gg * 1024:(gg + 1) * 1024, :]
                    .rearrange("(j p) t -> p j t", j=8),
                    sb_[:, :].rearrange("p (j t) -> p j t", j=8)),
                    reads=[r_sb], writes=[self.r_ygt[c]], dma=True,
                    dbg=f"styg{c}_{gg}")

        nch = NCH
        for d in self.dbg:
            if d.startswith("nch"):
                nch = int(d[3:])
        stageA(0)
        stageC(0)
        decay_chunk(0)
        for k in range(1, nch):
            stageA(k)
            stageB(k - 1)
            stageC(k)
            decay_chunk(k)
            stageD(k - 1)
        stageB(nch - 1)
        stageD(nch - 1)

        if "yg" in self.dbg:
            o = self.dout("dbg_ygt", [NCH, I4, CH], BF16)
            em.op("sp", lambda e, o=o: e.dma_start(o, self.ygt_h),
                  reads=self.r_ygt, dma=True)
        self.close_scope(pbs)

    def phaseD(self):
        em = self.em
        import contextlib as _ctx
        pds = _ctx.ExitStack()
        pb, r_pb = self.pbank, self.r_pb

        wo16 = self.sb("wo16", [128, 32 * D], BF16, pds)
        r_wo = [em.res("wo16a"), em.res("wo16b")]
        nwcol = self.sb("nwcol", [128, 32], F32, pds)
        r_nw = em.res("nw")
        ygcol = [self.sb(f"ygc{b}", [128, 32 * 128], BF16, pds) for b in range(2)]
        r_ygc = [em.res(f"ygc{b}") for b in range(2)]
        x_t = [self.sb(f"xd{b}", [128, D], F32, pds) for b in range(2)]
        r_xt = [em.res(f"xd{b}") for b in range(2)]
        f_t = [self.sb(f"fd{b}", [128, D], BF16, pds) for b in range(2)]
        r_ft = [em.res(f"fd{b}") for b in range(2)]
        o_t = [self.sb(f"od{b}", [128, D], F32, pds) for b in range(2)]
        r_ot = [em.res(f"od{b}") for b in range(2)]

        em.op("sp", lambda e: e.dma_start(nwcol[:, :], self.nw_h),
              writes=[r_nw], dma=True)
        for dh in range(2):
            em.op("pool", lambda e, dh=dh: e.dma_start(
                wo16[:, :].rearrange("p (j x) -> p j x", j=32)[
                    :, :, dh * 1024:(dh + 1) * 1024],
                self.woutT_h.rearrange("(j p) d -> p j d", j=32)[
                    :, :, dh * 1024:(dh + 1) * 1024]),
                writes=[r_wo[dh]], dma=True, dbg=f"ldwo{dh}")
        for j in range(32):
            for dh in range(2):
                em.op("dve", lambda e, j=j, dh=dh: e.tensor_scalar(
                    wo16[:, j * D + dh * 1024:j * D + (dh + 1) * 1024],
                    wo16[:, j * D + dh * 1024:j * D + (dh + 1) * 1024],
                    nwcol[:, j:j + 1], None, ALU.mult),
                    reads=[r_wo[dh], r_nw], writes=[r_wo[dh]],
                    dbg=f"nwfold{j}_{dh}")

        em.op("sp", lambda e: e.dma_start(
            ygcol[0][:, :].rearrange("p (j t) -> p j t", j=32),
            self.ygt_h[0].rearrange("(j p) t -> p j t", j=32)),
            reads=[self.r_ygt[0]], writes=[r_ygc[0]], dma=True)
        em.op("sp", lambda e: e.dma_start(x_t[0][:, :], self.x_h[0:128, :]),
              writes=[r_xt[0]], dma=True)
        em.op("sp", lambda e: e.dma_start(f_t[0][:, :], self.ffn_h[0:128, :]),
              reads=[self.r_ffn[0]], writes=[r_ft[0]], dma=True)

        for tt in range(NTT):
            bi = tt % 2
            yb, r_yb = ygcol[bi], r_ygc[bi]
            xb, r_xb = x_t[bi], r_xt[bi]
            fb, r_fb = f_t[bi], r_ft[bi]
            ob, r_ob = o_t[bi], r_ot[bi]
            if tt + 1 < NTT:
                em.op("sp", lambda e, tt=tt, bi=bi: e.dma_start(
                    ygcol[1 - bi][:, :].rearrange("p (j t) -> p j t", j=32),
                    self.ygt_h[tt + 1].rearrange("(j p) t -> p j t", j=32)),
                    reads=[self.r_ygt[tt + 1]], writes=[r_ygc[1 - bi]],
                    dma=True)
                em.op("sp", lambda e, tt=tt, bi=bi: e.dma_start(
                    x_t[1 - bi][:, :],
                    self.x_h[(tt + 1) * 128:(tt + 2) * 128, :]),
                    writes=[r_xt[1 - bi]], dma=True)
                em.op("sp", lambda e, tt=tt, bi=bi: e.dma_start(
                    f_t[1 - bi][:, :],
                    self.ffn_h[(tt + 1) * 128:(tt + 2) * 128, :]),
                    reads=[self.r_ffn[tt + 1]], writes=[r_ft[1 - bi]],
                    dma=True)
            for dsl in range(4):
                bk = dsl
                dh = dsl // 2
                def omm(e, yb=yb, dsl=dsl, bk=bk):
                    last = None
                    for j in range(32):
                        last = e.matmul(
                            pb[bk][:, :],
                            yb[:, j * 128:(j + 1) * 128],
                            wo16[:, :].rearrange("p (j d) -> p j d", j=32)[
                                :, j, dsl * 512:(dsl + 1) * 512],
                            start=(j == 0), stop=(j == 31))
                    return last
                em.op("pe", omm, reads=[r_yb, r_wo[dh]], writes=[r_pb[bk]],
                      dbg=f"omm{tt}_{dsl}")
                sl = slice(dsl * 512, (dsl + 1) * 512)
                em.op("dve", lambda e, sl=sl, bk=bk, ob=ob, xb=xb:
                      e.scalar_tensor_tensor(
                          ob[:, sl], pb[bk][:, :], 1.0, xb[:, sl],
                          ALU.mult, ALU.add),
                      reads=[r_pb[bk], r_xb], writes=[r_ob],
                      dbg=f"oev{tt}_{dsl}")
            em.op("dve", lambda e, ob=ob, fb=fb: e.tensor_tensor(
                ob[:, :], ob[:, :], fb[:, :], ALU.add),
                reads=[r_ob, r_fb], writes=[r_ob], dbg=f"offn{tt}")
            em.op("sp", lambda e, tt=tt, ob=ob: e.dma_start(
                self.out_h[tt * 128:(tt + 1) * 128, :], ob[:, :]),
                reads=[r_ob], writes=[self.r_out], dma=True, dbg=f"out{tt}")
        self.close_scope(pds)

    def phaseF(self):
        em = self.em
        import contextlib as _ctx
        pfs = _ctx.ExitStack()
        pb, r_pb = self.pbank, self.r_pb
        HT = T // 2
        NTH = NTT // 2

        acc16 = self.sb("acc16", [128, NTH * D], BF16, pfs)
        r_acc = [em.res(f"acc{t}") for t in range(NTH)]
        h16b = [self.sb(f"h16_{b}", [128, 8 * HT], BF16, pfs) for b in range(2)]
        r_h16b = [[em.res(f"h16_{b}_{j}") for j in range(8)] for b in range(2)]
        srowb = [self.sb(f"srow{b}", [128, HT], BF16, pfs) for b in range(2)]
        r_srowb = [em.res(f"srow{b}") for b in range(2)]
        wd16 = self.sb("wd16", [128, 8 * D], BF16, pfs)
        r_wd = em.res("wd16")
        wupb = [self.sb(f"wupb{b}", [128, KT * 128], BF16, pfs) for b in range(2)]
        r_wupb = [em.res(f"wupb{b}") for b in range(2)]
        bupc = self.sb("bupc_s", [128, EXP * 8], F32, pfs)
        r_bup = em.res("bupc")
        bdn16 = self.sb("bdn16", [EXP, D], BF16, pfs)
        r_bdn = em.res("bdn16")

        self.bupc_h = self.din("bupc", [128, EXP * 8])
        self.bdn_h = self.din("bdn", [EXP, D])
        em.op("sp", lambda e: e.dma_start(bupc[:, :], self.bupc_h),
              writes=[r_bup], dma=True)
        em.op("pool", lambda e: e.dma_start(bdn16[:, :], self.bdn_h),
              writes=[r_bdn], dma=True)

        for th in range(2):
            t_base = th * HT
            for e_i in range(EXP):
                h16 = h16b[e_i % 2]
                r_h16 = r_h16b[e_i % 2]
                srow, r_srow = srowb[e_i % 2], r_srowb[e_i % 2]
                em.op("sp", lambda e, e_i=e_i, t_base=t_base, srow=srow:
                    e.dma_start(
                    srow[:, :],
                    bass.AP(self.scoh_h.tensor, e_i * T + t_base,
                            [[0, 128], [1, HT]])),
                    reads=[self.r_scoh], writes=[r_srow], dma=True,
                    dbg=f"srow{th}_{e_i}")
                for ht in range(8):
                    wb, r_wb = wupb[ht % 2], r_wupb[ht % 2]
                    c0 = e_i * HE + ht * 128
                    em.op("pool", lambda e, c0=c0, wb=wb: e.dma_start(
                        wb[:, :].rearrange("p (k c) -> p k c", k=KT),
                        self.wupT_h.rearrange("(k p) c -> p k c", k=KT)[
                            :, :, c0:c0 + 128]),
                        writes=[r_wb], dma=True, dbg=f"wup{th}_{e_i}_{ht}")
                    if ht == 2:
                        em.op("pool", lambda e, e_i=e_i: e.dma_start(
                            wd16[:, :].rearrange("p (j d) -> p j d", j=8),
                            self.wdT_h[e_i].rearrange("(j p) d -> p j d", j=8)),
                            writes=[r_wd], dma=True, dbg=f"wd{th}_{e_i}")
                    for n in range(2):
                        bi = n % 2
                        def up_fn(e, wb=wb, n=n, bi=bi, t_base=t_base):
                            last = None
                            for k in range(KT):
                                last = e.matmul(
                                    pb[bi][:, :],
                                    wb[:, :].rearrange(
                                        "p (k c) -> p k c", k=KT)[:, k],
                                    self.xnTb[:, :].rearrange(
                                        "p (k t) -> p k t", k=KT)[
                                        :, k,
                                        t_base + n * 512:t_base + (n + 1) * 512],
                                    start=(k == 0), stop=(k == KT - 1))
                            return last
                        em.op("pe", up_fn, reads=[r_wb] + self.r_xnT,
                              writes=[r_pb[bi]], dbg=f"up{th}_{e_i}_{ht}_{n}")
                        em.op("act", lambda e, ht=ht, n=n, e_i=e_i, bi=bi,
                              h16=h16: e.activation(
                                  h16[:, ht * HT + n * 512:
                                      ht * HT + (n + 1) * 512],
                                  pb[bi][:, :], AF.Silu,
                                  bias=bupc[:, e_i * 8 + ht:e_i * 8 + ht + 1]),
                              reads=[r_pb[bi], r_bup], writes=[r_h16[ht]],
                              dbg=f"silu{th}_{e_i}_{ht}_{n}")
                    em.op("dve", lambda e, ht=ht, h16=h16, srow=srow:
                        e.tensor_tensor(
                        h16[:, ht * HT:(ht + 1) * HT],
                        h16[:, ht * HT:(ht + 1) * HT],
                        srow[:, :], ALU.mult),
                        reads=[r_h16[ht], r_srow], writes=[r_h16[ht]],
                        dbg=f"hsc{th}_{e_i}_{ht}")

                for lt in range(NTH):
                    tt = th * NTH + lt
                    for dsl in range(4):
                        bi = 2 + dsl
                        def dn_fn(e, lt=lt, tt=tt, dsl=dsl, e_i=e_i, bi=bi,
                                  h16=h16):
                            last = None
                            if e_i == 0:
                                last = e.matmul(
                                    pb[bi][:, :],
                                    self.scoT[0:8, tt * 128:(tt + 1) * 128],
                                    bdn16[0:8, dsl * 512:(dsl + 1) * 512],
                                    start=True, stop=False)
                            for j in range(8):
                                last = e.matmul(
                                    pb[bi][:, :],
                                    h16[:, j * HT + lt * 128:
                                        j * HT + (lt + 1) * 128],
                                    wd16[:, :].rearrange(
                                        "p (j d) -> p j d", j=8)[
                                        :, j, dsl * 512:(dsl + 1) * 512],
                                    start=(e_i != 0 and j == 0), stop=(j == 7))
                            return last
                        em.op("pe", dn_fn,
                              reads=[r_wd, self.r_scoT, r_bdn] + r_h16,
                              writes=[r_pb[bi]], dbg=f"dn{th}_{e_i}_{lt}_{dsl}")
                    dst = acc16[:, lt * D:(lt + 1) * D]
                    rquad = [r_pb[2 + i] for i in range(4)]
                    if e_i == 0:
                        em.op("dve", lambda e, dst=dst: e.tensor_copy(
                            dst, self.pview[:, 1024:3072]),
                            reads=rquad, writes=[r_acc[lt]],
                            dbg=f"ev0_{th}_{lt}")
                    else:
                        em.op("dve", lambda e, dst=dst:
                              e.scalar_tensor_tensor(
                                  dst, self.pview[:, 1024:3072], 1.0, dst,
                                  ALU.mult, ALU.add),
                              reads=rquad + [r_acc[lt]],
                              writes=[r_acc[lt]],
                              dbg=f"ev{e_i}_{th}_{lt}")

            for lt in range(NTH):
                tt = th * NTH + lt
                em.op("sp", lambda e, lt=lt, tt=tt: e.dma_start(
                    self.ffn_h[tt * 128:(tt + 1) * 128, :],
                    acc16[:, lt * D:(lt + 1) * D]),
                    reads=[r_acc[lt]], writes=[self.r_ffn[tt]], dma=True,
                    dbg=f"stffn{tt}")

        if "ffn" in self.dbg:
            o = self.dout("dbg_ffn", [T, D], BF16)
            for tt in range(NTT):
                em.op("sp", lambda e, tt=tt, o=o: e.dma_start(
                    o[tt * 128:(tt + 1) * 128, :],
                    self.ffn_h[tt * 128:(tt + 1) * 128, :]),
                    reads=[self.r_ffn[tt]], dma=True)
        self.close_scope(pfs)

    def phase0(self):
        em = self.em
        import contextlib as _ctx
        p0s = _ctx.ExitStack()
        self.xnTb = self.sb("xnTb", [128, KT * T], BF16, self.xns)
        self.r_xnT = [em.res(f"xnT{k}") for k in range(KT)]
        self.scoT = self.sb("scoT", [EXP, T], BF16, self.xns)
        self.r_scoT = em.res("scoT")

        def xnT(k):
            return self.xnTb[:, k * T:(k + 1) * T]

        self.xnT = xnT

        x_tile = [self.sb(f"xt{b}", [128, D], F32, p0s) for b in range(2)]
        r_xtile = [em.res(f"xt{b}") for b in range(2)]
        sq_scr = self.sb("sqscr", [128, D], BF16, p0s)
        r_sq = em.res("sq")
        statsb = [self.sb(f"stats{b}", [128, 8], F32, p0s) for b in range(2)]
        r_statsb = [em.res(f"stats{b}") for b in range(2)]
        xn_stageb = [self.sb(f"xn_stage{b}", [128, D], BF16, p0s)
                     for b in range(2)]
        lo_stageb = [self.sb(f"lo_stage{b}", [128, D], BF16, p0s)
                     for b in range(2)]
        loTb = [self.sb(f"loT{b}", [128, KT * 128], BF16, p0s)
                for b in range(2)]
        r_xnstb = [em.res(f"xnst{b}") for b in range(2)]
        r_lostb = [em.res(f"lost{b}") for b in range(2)]
        r_loTb = [em.res(f"loT{b}") for b in range(2)]

        wg1f = self.sb("wg1f", [128, KT * EXP], F32, p0s)
        wg1hi = self.sb("wg1hi", [128, KT * EXP], BF16, p0s)
        wg1lo = self.sb("wg1lo", [128, KT * EXP], BF16, p0s)
        r_wg1, r_wg1h, r_wg1l = em.res(), em.res(), em.res()
        wg2f = self.sb("wg2f", [EXP, EXP], F32, p0s)
        r_wg2 = em.res()
        em.op("sp", lambda e: e.dma_start(
            wg1f[:, :].rearrange("p (k e) -> p k e", k=KT),
            self.wg1T_h.rearrange("(k p) e -> p k e", k=KT)),
            writes=[r_wg1], dma=True)
        em.op("sp", lambda e: e.dma_start(wg2f[:, :], self.wg2T_h),
              writes=[r_wg2], dma=True)
        em.op("dve", lambda e: e.tensor_copy(wg1hi[:, :], wg1f[:, :]),
              reads=[r_wg1], writes=[r_wg1h])
        em.op("dve", lambda e: e.tensor_tensor(
            wg1lo[:, :], wg1f[:, :], wg1hi[:, :], ALU.subtract),
            reads=[r_wg1, r_wg1h], writes=[r_wg1l])

        l1T = self.sb("l1T", [EXP, T], F32, p0s)
        r_l1 = em.res("l1T")
        l2T = self.sb("l2T", [EXP, T], F32, p0s)
        r_l2 = em.res("l2T")
        lgt = self.sb("lgt", [128, 8], F32, p0s)
        r_lgt = em.res("lgt")
        tkb = self.sb("tkb", [128, 40], F32, p0s)
        r_tk = em.res("tk")

        pb, r_pb = self.pbank, self.r_pb
        epsc = self.constcol(EPS_LN)
        rcc = self.r_cconst

        for tt in range(NTT):
            xb, r_xb = x_tile[tt % 2], r_xtile[tt % 2]
            stats, r_stats = statsb[tt % 2], r_statsb[tt % 2]
            xn_stage, r_xnst = xn_stageb[tt % 2], r_xnstb[tt % 2]
            lo_stage, r_lost = lo_stageb[tt % 2], r_lostb[tt % 2]
            loT, r_loT = loTb[tt % 2], r_loTb[tt % 2]
            em.op("sp", lambda e, tt=tt, xb=xb: e.dma_start(
                xb[:, :], self.x_h[tt * 128:(tt + 1) * 128, :]),
                writes=[r_xb], dma=True, dbg=f"ldx{tt}")
            em.op("dve", lambda e, xb=xb, stats=stats: e.tensor_reduce(
                stats[:, 0:1], xb[:, :], AX.X, ALU.add),
                reads=[r_xb], writes=[r_stats], dbg="lnsum")
            em.op("act", lambda e, xb=xb, stats=stats: e.activation(
                sq_scr[:, :], xb[:, :], AF.Square, accum_out=stats[:, 2:3]),
                reads=[r_xb], writes=[r_sq, r_stats], dbg="lnsq")
            em.op("dve", lambda e, stats=stats: e.tensor_scalar(
                stats[:, 1:2], stats[:, 0:1], 1.0 / D, None, ALU.mult),
                reads=[r_stats], writes=[r_stats])
            em.op("act", lambda e, stats=stats: e.activation(
                stats[:, 3:4], stats[:, 1:2], AF.Square),
                reads=[r_stats], writes=[r_stats])
            em.op("dve", lambda e, stats=stats: e.tensor_scalar(
                stats[:, 3:4], stats[:, 2:3], 1.0 / D, stats[:, 3:4], ALU.mult,
                ALU.subtract), reads=[r_stats], writes=[r_stats], dbg="var")
            em.op("act", lambda e, stats=stats: e.activation(
                stats[:, 4:5], stats[:, 3:4], AF.Ln, bias=epsc),
                reads=[r_stats, rcc], writes=[r_stats])
            em.op("act", lambda e, stats=stats: e.activation(
                stats[:, 4:5], stats[:, 4:5], AF.Exp, scale=-0.5),
                reads=[r_stats], writes=[r_stats], dbg="rstd")
            em.op("dve", lambda e, xb=xb, stats=stats, xn_stage=xn_stage:
                  e.tensor_scalar(
                  xn_stage[:, :], xb[:, :], stats[:, 1:2], stats[:, 4:5],
                  ALU.subtract, ALU.mult),
                  reads=[r_xb, r_stats], writes=[r_xnst], dbg="xn16")
            em.op("dve", lambda e, xb=xb, stats=stats: e.tensor_scalar(
                xb[:, :], xb[:, :], stats[:, 1:2], stats[:, 4:5],
                ALU.subtract, ALU.mult),
                reads=[r_xb, r_stats], writes=[r_xb], dbg="xnf32")
            em.op("dve", lambda e, xb=xb, xn_stage=xn_stage, lo_stage=lo_stage:
                  e.tensor_tensor(
                  lo_stage[:, :], xb[:, :], xn_stage[:, :], ALU.subtract),
                  reads=[r_xb, r_xnst], writes=[r_lost], dbg="lo")

            for g in range(4):
                bi = g % 2
                pv = pb[bi][:, :].bitcast(BF16)

                def tr_fn(e, g=g, pv=pv, xn_stage=xn_stage):
                    last = None
                    for j in range(4):
                        k = g * 4 + j
                        last = e.transpose(pv[:, j * 128:(j + 1) * 128],
                                           xn_stage[:, k * 128:(k + 1) * 128],
                                           self.id16[:, :])
                    return last
                em.op("pe", tr_fn, reads=[r_xnst, self.r_id16], writes=[r_pb[bi]],
                      dbg=f"trxn{tt}_{g}")

                def ev_fn(e, g=g, tt=tt, pv=pv):
                    dst = self.xnTb[:, :].rearrange(
                        "p (k t) -> p k t", k=KT)[:, g * 4:(g + 1) * 4,
                                                  tt * 128:(tt + 1) * 128]
                    return e.tensor_copy(
                        dst, pv[:, 0:512].rearrange("p (j c) -> p j c", j=4))
                em.op("dve", ev_fn, reads=[r_pb[bi]],
                      writes=[self.r_xnT[g * 4 + j] for j in range(4)],
                      dbg=f"evxn{tt}_{g}")

                bi2 = 2 + g % 2
                pv2 = pb[bi2][:, :].bitcast(BF16)

                def trlo_fn(e, g=g, pv2=pv2, lo_stage=lo_stage):
                    last = None
                    for j in range(4):
                        k = g * 4 + j
                        last = e.transpose(pv2[:, j * 128:(j + 1) * 128],
                                           lo_stage[:, k * 128:(k + 1) * 128],
                                           self.id16[:, :])
                    return last
                em.op("pe", trlo_fn, reads=[r_lost, self.r_id16],
                      writes=[r_pb[bi2]], dbg=f"trlo{tt}_{g}")

                def evlo_fn(e, g=g, pv2=pv2, loT=loT):
                    dst = loT[:, :].rearrange("p (k t) -> p k t", k=KT)[
                        :, g * 4:(g + 1) * 4, :]
                    return e.tensor_copy(
                        dst, pv2[:, 0:512].rearrange("p (j c) -> p j c", j=4))
                em.op("dve", evlo_fn, reads=[r_pb[bi2]], writes=[r_loT],
                      dbg=f"evlo{tt}_{g}")

            def gmm_fn(e, tt=tt, loT=loT):
                w_hi = wg1hi[:, :].rearrange("p (k e) -> p k e", k=KT)
                w_lo = wg1lo[:, :].rearrange("p (k e) -> p k e", k=KT)
                lov = loT[:, :].rearrange("p (k t) -> p k t", k=KT)
                last = None
                for k in range(KT):
                    xcol = self.xnTb[:, :].rearrange("p (k t) -> p k t", k=KT)[
                        :, k, tt * 128:(tt + 1) * 128]
                    last = e.matmul(pb[4][0:8, 0:128], w_hi[:, k], xcol,
                                    start=(k == 0), stop=False)
                    last = e.matmul(pb[4][0:8, 0:128], w_lo[:, k], xcol,
                                    start=False, stop=False)
                    last = e.matmul(pb[4][0:8, 0:128], w_hi[:, k], lov[:, k],
                                    start=False, stop=(k == KT - 1))
                return last
            em.op("pe", gmm_fn,
                  reads=[r_wg1h, r_wg1l, r_loT] + self.r_xnT, writes=[r_pb[4]],
                  dbg=f"gmm{tt}")
            em.op("act", lambda e, tt=tt: e.activation(
                l1T[:, tt * 128:(tt + 1) * 128], pb[4][0:8, 0:128], AF.Tanh),
                reads=[r_pb[4]], writes=[r_l1], dbg=f"tanh{tt}")

        for j in range(4):
            em.op("pe", lambda e, j=j: e.matmul(
                pb[5][0:8, 0:512], wg2f[:, :], l1T[:, j * 512:(j + 1) * 512],
                start=True, stop=True),
                reads=[r_wg2, r_l1], writes=[r_pb[5]], dbg=f"g2_{j}")
            em.op("dve", lambda e, j=j: e.tensor_copy(
                l2T[:, j * 512:(j + 1) * 512], pb[5][0:8, 0:512]),
                reads=[r_pb[5]], writes=[r_l2], dbg=f"g2ev{j}")

        for tt in range(NTT):
            em.op("pe", lambda e, tt=tt: e.transpose(
                self.pbank[6][0:128, 0:8], l2T[0:8, tt * 128:(tt + 1) * 128],
                self.idf[0:8, 0:8]),
                reads=[r_l2, self.r_idf], writes=[r_pb[6]], dbg=f"trl2{tt}")
            em.op("dve", lambda e: e.tensor_copy(lgt[:, 0:8],
                                                 self.pbank[6][0:128, 0:8]),
                  reads=[r_pb[6]], writes=[r_lgt])
            em.op("dve", lambda e: e.reduce_max(tkb[:, 0:1], lgt[:, 0:8], AX.X),
                  reads=[r_lgt], writes=[r_tk])
            em.op("dve", lambda e: e.tensor_scalar(
                tkb[:, 8:16], lgt[:, 0:8], tkb[:, 0:1], None, ALU.is_ge),
                reads=[r_lgt, r_tk], writes=[r_tk], dbg="m1")
            em.op("dve", lambda e: e.scalar_tensor_tensor(
                tkb[:, 16:24], tkb[:, 8:16], -1e30, lgt[:, 0:8], ALU.mult,
                ALU.add), reads=[r_tk, r_lgt], writes=[r_tk], dbg="l2m")
            em.op("dve", lambda e: e.reduce_max(tkb[:, 1:2], tkb[:, 16:24], AX.X),
                  reads=[r_tk], writes=[r_tk])
            em.op("dve", lambda e: e.tensor_scalar(
                tkb[:, 24:32], tkb[:, 16:24], tkb[:, 1:2], None, ALU.is_ge),
                reads=[r_tk], writes=[r_tk], dbg="m2")
            em.op("dve", lambda e: e.tensor_tensor(
                tkb[:, 2:3], tkb[:, 0:1], tkb[:, 1:2], ALU.subtract),
                reads=[r_tk], writes=[r_tk])
            em.op("act", lambda e: e.activation(tkb[:, 3:4], tkb[:, 2:3],
                                                AF.Sigmoid),
                  reads=[r_tk], writes=[r_tk], dbg="sig")
            em.op("dve", lambda e: e.tensor_scalar(
                tkb[:, 4:5], tkb[:, 3:4], 1.0, -1.0, ALU.subtract, ALU.mult),
                reads=[r_tk], writes=[r_tk], dbg="p2")
            em.op("dve", lambda e: e.tensor_scalar(
                tkb[:, 8:16], tkb[:, 8:16], tkb[:, 3:4], None, ALU.mult),
                reads=[r_tk], writes=[r_tk])
            em.op("dve", lambda e: e.scalar_tensor_tensor(
                tkb[:, 32:40], tkb[:, 24:32], tkb[:, 4:5], tkb[:, 8:16],
                ALU.mult, ALU.add),
                reads=[r_tk], writes=[r_tk], dbg="scores")
            em.op("pe", lambda e: e.transpose(
                self.pbank[7][0:8, 0:128], tkb[:, 32:40], self.idf[:, :]),
                reads=[r_tk, self.r_idf], writes=[r_pb[7]], dbg=f"trsc{tt}")
            em.op("dve", lambda e, tt=tt: e.tensor_copy(
                self.scoT[0:8, tt * 128:(tt + 1) * 128],
                self.pbank[7][0:8, 0:128]),
                reads=[r_pb[7]], writes=[self.r_scoT])

        self.scoh_h = self.dint("scoh", [EXP, T], BF16)
        self.r_scoh = em.res("scoh")
        em.op("sp", lambda e: e.dma_start(self.scoh_h, self.scoT[:, :]),
              reads=[self.r_scoT], writes=[self.r_scoh], dma=True)

        if "xnT" in self.dbg:
            o = self.dout("dbg_xnT", [KT * 128, T], BF16)
            for k in range(KT):
                em.op("sp", lambda e, k=k, o=o: e.dma_start(
                    o[k * 128:(k + 1) * 128, :], self.xnT(k)),
                    reads=[self.r_xnT[k]], dma=True)
        if "scoT" in self.dbg:
            o = self.dout("dbg_scoT", [EXP, T], BF16)
            em.op("sp", lambda e, o=o: e.dma_start(o, self.scoT[:, :]),
                  reads=[self.r_scoT], dma=True)
        if "l2T" in self.dbg:
            o = self.dout("dbg_l2T", [EXP, T], F32)
            em.op("sp", lambda e, o=o: e.dma_start(o, l2T[:, :]),
                  reads=[r_l2], dma=True)
        self.close_scope(p0s)

    def stub_out(self):
        em = self.em
        import contextlib as _ctx
        sos = _ctx.ExitStack()
        x_tile = [self.sb(f"so{b}", [128, D], F32, sos) for b in range(2)]
        r_xtile = [em.res(f"so{b}") for b in range(2)]
        for tt in range(NTT):
            xb, r_xb = x_tile[tt % 2], r_xtile[tt % 2]
            em.op("sp", lambda e, tt=tt, xb=xb: e.dma_start(
                xb[:, :], self.x_h[tt * 128:(tt + 1) * 128, :]),
                writes=[r_xb], dma=True)
            em.op("sp", lambda e, tt=tt, xb=xb: e.dma_start(
                self.out_h[tt * 128:(tt + 1) * 128, :], xb[:, :]),
                reads=[r_xb], writes=[self.r_out], dma=True)
        self.close_scope(sos)

    def finish(self):
        st = self.em.finalize(self.nc, self.sems)
        self.stack.close()
        return st


def host_inputs(core_x, w):
    f32 = np.float32
    d = {
        "x": np.ascontiguousarray(core_x, dtype=f32),
        "winT": w["winT"], "woutT": w["woutT"], "wupT": w["wupT"],
        "wdT": w["wdT"], "wg1T": w["wg1T"], "wg2T": w["wg2T"],
        "convw": w["convw"], "convb": w["convb"], "dtb": w["dtb"],
        "alog": w["alog"], "drep": w["drep"], "nw": w["nw"],
        "idf": w["idf"], "tril": w["tril"], "onesrow": w["onesrow"],
        "bupc": w["bupc"], "bdn": w["bdn"], "negHE": w["negHE"],
        "one3": w["one3"],
    }
    return d


def shared_weights(inputs):
    f32 = np.float32
    asc = np.ascontiguousarray
    w = {}
    w["winT"] = asc(np.asarray(inputs["L0_attn_in_proj_w"], f32).T)
    w["woutT"] = asc(np.asarray(inputs["L0_attn_out_proj_w"], f32).T)
    wup = np.asarray(inputs["L0_ffn_Wup"], f32)
    w["wupT"] = asc(wup.transpose(2, 0, 1).reshape(D, EXP * HE))
    wdn = np.asarray(inputs["L0_ffn_Wdown"], f32)
    w["wdT"] = asc(wdn.transpose(0, 2, 1))
    w["wg1T"] = asc(np.asarray(inputs["L0_ffn_Wg1"], f32).T)
    w["wg2T"] = asc(np.asarray(inputs["L0_ffn_Wg2"], f32).T)
    cw = np.asarray(inputs["L0_attn_conv_w"], f32)[:, 0, :]
    w["convw"] = asc(cw.reshape(NBC // 128, 128, 4).transpose(1, 0, 2)
                     .reshape(128, (NBC // 128) * 4))
    cb = np.asarray(inputs["L0_attn_conv_b"], f32)
    w["convb"] = asc(cb.reshape(NBC // 128, 128).T)
    w["dtb"] = asc(np.asarray(inputs["L0_attn_dt_bias"], f32).reshape(NH, 1))
    w["alog"] = asc(np.asarray(inputs["L0_attn_A_log"], f32).reshape(NH, 1))
    w["drep"] = asc(np.repeat(np.asarray(inputs["L0_attn_D"], f32), PP)
                    .reshape(1, I4))
    nw = np.asarray(inputs["L0_attn_norm_w"], f32)
    w["nw"] = asc(nw.reshape(I4 // 128, 128).T)
    bup = np.asarray(inputs["L0_ffn_bup"], f32)
    w["bupc"] = asc(np.concatenate(
        [bup[e].reshape(8, 128).T for e in range(EXP)], axis=1))
    w["bdn"] = asc(np.asarray(inputs["L0_ffn_bdown"], f32))
    he = np.zeros((32, 32, 128), f32)
    for h in range(32):
        he[h, h, :] = -1.0
    w["negHE"] = asc(he.reshape(32, 32 * 128))
    o3 = np.zeros((32, 128), f32)
    o3[0:3, :] = 1.0
    w["one3"] = o3
    w["idf"] = np.eye(128, dtype=f32)
    w["tril"] = np.triu(np.ones((128, 128), f32))
    w["onesrow"] = np.ones((1, 128), f32)
    return w


def kernel(**inputs):
    kb = KB().build()
    st = kb.finish()
    w = shared_weights(inputs)
    x = np.asarray(inputs["L0_attn_hidden_states"], np.float32)
    in_maps = [host_inputs(x[i], w) for i in range(8)]
    res = run_bass_kernel_spmd(kb.nc, in_maps, core_ids=list(range(8)))
    out = np.stack([np.asarray(r["out"]) for r in res.results], axis=0)
    return out


# version 16
# speedup vs baseline: 1.1860x; 1.1860x over previous
import contextlib
import numpy as np
import ml_dtypes
import concourse.bass as bass
import concourse.mybir as mybir
from concourse.bass_utils import run_bass_kernel_spmd

F32 = mybir.dt.float32
BF16 = mybir.dt.bfloat16
AF = mybir.ActivationFunctionType
ALU = mybir.AluOpType
AX = mybir.AxisListType

T = 2048
D = 2048
FOUT = 8512
I4 = 4096
NBC = 4352
NH = 64
PP = 64
NS = 128
CH = 128
NCH = T // CH
EXP = 8
HE = 1024
KT = D // 128
NTT = T // 128
EPS_LN = 1e-5
EPS_RMS = 1e-5

ENGS = ("pe", "dve", "act", "pool", "sp")
NLANES = 8
QSEMS = tuple(ENGS) + tuple(
    f"{e}q{i}" for e in ("pool", "sp", "act") for i in range(NLANES))


class Res:
    __slots__ = ("name", "writer", "readers")

    def __init__(self, name):
        self.name = name
        self.writer = None
        self.readers = []


class Op:
    __slots__ = ("fn", "eng", "qsem", "inc_amt", "waits", "need_inc",
                 "sem_val_after", "dbg")

    def __init__(self, fn, eng, qsem, inc_amt, dbg):
        self.fn = fn
        self.eng = eng
        self.qsem = qsem
        self.inc_amt = inc_amt
        self.waits = []
        self.need_inc = False
        self.sem_val_after = 0
        self.dbg = dbg


class EM:

    def __init__(self):
        self.streams = {e: [] for e in ENGS}
        self.nres = 0
        self.dma_count = {e: 0 for e in ("pool", "sp", "act")}
        self.dma_lane_prev = {e: [None] * NLANES for e in ("pool", "sp", "act")}

    def res(self, name=None):
        self.nres += 1
        return Res(name or f"r{self.nres}")

    def op(self, eng, fn, reads=(), writes=(), dma=False, dbg=""):
        lane_prev = None
        if dma:
            j = self.dma_count[eng]
            self.dma_count[eng] = j + 1
            lane = j % NLANES
            qsem = f"{eng}q{lane}"
            lane_prev = self.dma_lane_prev[eng][lane]
        else:
            qsem = eng
        o = Op(fn, eng, qsem, 16 if dma else 1, dbg)
        if dma:
            self.dma_lane_prev[eng][lane] = o
        tgt = []
        if lane_prev is not None:
            tgt.append(lane_prev)
        for r in reads:
            if r.writer is not None:
                tgt.append(r.writer)
        for r in writes:
            if r.writer is not None:
                tgt.append(r.writer)
            tgt.extend(r.readers)
        seen = set()
        for t in tgt:
            if t is o or id(t) in seen:
                continue
            seen.add(id(t))
            if o.qsem == "pe" and t.qsem == "pe":
                continue
            o.waits.append(t)
        for r in reads:
            r.readers.append(o)
        for r in writes:
            r.writer = o
            r.readers = []
        self.streams[eng].append(o)
        return o

    def barrier(self):
        lasts = []
        for s in self.streams.values():
            for t in reversed(s):
                if t.fn is not None and not t.qsem[-1].isdigit():
                    lasts.append(t)
                    break
        for e in ("pool", "sp", "act"):
            for prev in self.dma_lane_prev[e]:
                if prev is not None:
                    lasts.append(prev)
        for eng in ENGS:
            o = Op(None, eng, eng, 1, "barrier")
            o.waits.extend(lasts)
            self.streams[eng].append(o)

    def finalize(self, nc, sems):
        for s in self.streams.values():
            for o in s:
                if o.qsem.endswith("q"):
                    o.need_inc = True
                for t in o.waits:
                    t.need_inc = True
        cum = {q: 0 for q in QSEMS}
        for eng in ENGS:
            for o in self.streams[eng]:
                if o.need_inc:
                    cum[o.qsem] += o.inc_amt
                o.sem_val_after = cum[o.qsem]
        knows = {e: {q: 0 for q in QSEMS} for e in ENGS}
        stats = {"waits": 0, "incs": 0, "ops": 0}

        def emit_stream(engname):
            def run(eh):
                kn = knows[engname]
                for o in self.streams[engname]:
                    stats["ops"] += 1
                    need = {}
                    for t in o.waits:
                        v = t.sem_val_after
                        if v > need.get(t.qsem, 0):
                            need[t.qsem] = v
                    for q in QSEMS:
                        if q in need and kn[q] < need[q]:
                            eh.wait_ge(sems[q], need[q])
                            kn[q] = need[q]
                            stats["waits"] += 1
                    if o.fn is not None:
                        inst = o.fn(eh)
                        if o.need_inc:
                            assert inst is not None, (o.eng, o.dbg)
                            inst.then_inc(sems[o.qsem], o.inc_amt)
                            stats["incs"] += 1
            return run

        with nc.Block() as block:
            block.tensor(emit_stream("pe"))
            block.vector(emit_stream("dve"))
            block.scalar(emit_stream("act"))
            block.gpsimd(emit_stream("pool"))
            block.sync(emit_stream("sp"))
        return stats


class KB:

    def __init__(self, dbg=()):
        self.dbg = set(dbg)
        self.nc = bass.Bass("TRN2", target_bir_lowering=False)
        self.em = EM()
        self.stack = contextlib.ExitStack()
        self.sems = {}
        for q in QSEMS:
            self.sems[q] = self.stack.enter_context(self.nc.semaphore("sem_" + q))
        self.dr = {}

    def din(self, name, shape, dt=F32):
        ap = self.nc.dram_tensor(name, list(shape), dt, kind="ExternalInput").ap()
        self.dr[name] = ap
        return ap

    def dout(self, name, shape, dt=F32):
        ap = self.nc.dram_tensor(name, list(shape), dt, kind="ExternalOutput").ap()
        self.dr[name] = ap
        return ap

    def dint(self, name, shape, dt):
        ap = self.nc.dram_tensor(name, list(shape), dt, kind="Internal").ap()
        self.dr[name] = ap
        return ap

    def sb(self, name, shape, dt, scope=None, side=None):
        st = scope if scope is not None else self.stack
        if side is None:
            return st.enter_context(self.nc.sbuf_tensor(name, list(shape), dt))
        return st.enter_context(
            self.nc.sbuf_tensor(name, list(shape), dt, side=side))

    def close_scope(self, scope):
        self.em.barrier()
        scope.close()

    def ps(self, name, shape, dt=F32):
        return self.stack.enter_context(self.nc.psum_tensor(name, list(shape), dt))

    def build(self):
        em = self.em
        self.x_h = self.din("x", [T, D])
        self.winT_h = self.din("winT", [D, FOUT])
        self.woutT_h = self.din("woutT", [I4, D])
        self.wupT_h = self.din("wupT", [D, EXP * HE])
        self.wdT_h = self.din("wdT", [EXP, HE, D])
        self.wg1T_h = self.din("wg1T", [D, EXP])
        self.wg2T_h = self.din("wg2T", [EXP, EXP])
        self.convw_h = self.din("convw", [128, (NBC // 128) * 4])
        self.convb_h = self.din("convb", [128, NBC // 128])
        self.dtb_h = self.din("dtb", [NH, 1])
        self.alog_h = self.din("alog", [NH, 1])
        self.drep_h = self.din("drep", [1, I4])
        self.nw_h = self.din("nw", [128, I4 // 128])
        self.idf_h = self.din("idf", [128, 128])
        self.tril_h = self.din("tril", [128, 128])
        self.ones_h = self.din("onesrow", [1, 128])
        self.out_h = self.dout("out", [T, D], F32)
        self.zs_h = self.dint("zs", [T, I4], BF16)
        self.xts_h = self.dint("xts", [NCH, I4, CH], BF16)
        self.ffn_h = self.dint("ffn", [T, D], BF16)
        self.ygt_h = self.dint("ygt", [NCH, I4, CH], BF16)
        self.acr_h = self.dint("acr", [NCH, 3, NH * CH], BF16)

        self.r_out = em.res("out")
        self.r_zs = [em.res(f"zs{t}") for t in range(NTT)]
        self.r_xts = [em.res(f"xts{c}") for c in range(NCH)]
        self.r_ffn = [em.res(f"ffn{t}") for t in range(NTT)]
        self.r_ygt = [em.res(f"ygt{c}") for c in range(NCH)]
        self.r_acr = [em.res(f"acr{c}") for c in range(NCH)]

        self.idf = self.sb("idf_s", [128, 128], F32)
        self.id16 = self.sb("id16", [128, 128], BF16)
        self.tril = self.sb("tril_s", [128, 128], F32)
        self.ones1 = self.sb("ones1", [1, 128], F32)
        self.ones16 = self.sb("ones16", [1, 128], BF16)
        self.r_idf = em.res("idf")
        self.r_id16 = em.res("id16")
        self.r_tril = em.res("tril")
        self.r_ones1 = em.res("ones1")
        self.r_ones16 = em.res("ones16")
        em.op("sp", lambda e: e.dma_start(self.idf[:, :], self.idf_h),
              writes=[self.r_idf], dma=True)
        em.op("pool", lambda e: e.dma_start(self.id16[:, :], self.idf_h),
              writes=[self.r_id16], dma=True)
        em.op("sp", lambda e: e.dma_start(self.tril[:, :], self.tril_h),
              writes=[self.r_tril], dma=True)
        em.op("sp", lambda e: e.dma_start(self.ones1[:, :], self.ones_h),
              writes=[self.r_ones1], dma=True)
        em.op("pool", lambda e: e.dma_start(self.ones16[:, :], self.ones_h),
              writes=[self.r_ones16], dma=True)

        self.psumAll = self.ps("psumAll", [128, 4096], F32)
        self.pbank = [self.psumAll[:, i * 512:(i + 1) * 512] for i in range(8)]
        self.r_pb = [em.res(f"pb{i}") for i in range(8)]
        self.pview = self.psumAll

        self.cconst = self.sb("cconst", [128, 8], F32)
        self.r_cconst = em.res("cconst")
        self._cmap = {}
        return self.build_phases()

    def constcol(self, val):
        if val not in self._cmap:
            i = len(self._cmap)
            assert i < 8
            self._cmap[val] = i
            self.em.op("dve", lambda e, i=i, val=val: e.memset(
                self.cconst[:, i:i + 1], float(val)),
                writes=[self.r_cconst], dbg=f"const{val}")
        i = self._cmap[val]
        return self.cconst[:, i:i + 1]

    def build_phases(self):
        import contextlib as _ctx
        self.xns = _ctx.ExitStack()
        self.rscope = _ctx.ExitStack()
        self.phase0()
        if "skipF" not in self.dbg:
            self.phaseF()
        self.phaseA()
        self.close_scope(self.xns)
        if "stopA" in self.dbg:
            self.stub_out()
            self.em.barrier()
            return self
        self.phaseB()
        self.close_scope(self.rscope)
        if "stopB" in self.dbg:
            self.stub_out()
            self.em.barrier()
            return self
        self.phaseD()
        self.em.barrier()
        return self

    def phaseA(self):
        em = self.em
        import contextlib as _ctx
        pas = _ctx.ExitStack()
        pb, r_pb = self.pbank, self.r_pb

        self.bct = self.sb("bct", [128, 2 * T], BF16, self.rscope, side="right")
        self.r_bct = [em.res("bT"), em.res("cT")]
        self.aT = self.sb("aT", [NH, T], F32, self.rscope, side="right")
        self.r_aT = em.res("aT")
        self.dt_tok = self.sb("dt_tok", [128, NCH * NH], BF16, self.rscope,
                              side="right")
        self.r_dtok = em.res("dt_tok")

        wz = [self.sb(f"wz{b}", [128, KT * 512], BF16, pas) for b in range(2)]
        r_wz = [em.res(f"wz{b}") for b in range(2)]
        wx = [self.sb(f"wx{b}", [128, KT * 128], BF16, pas) for b in range(2)]
        r_wx = [em.res(f"wx{b}") for b in range(2)]
        z16 = [self.sb(f"z16_{b}", [128, 512], BF16, pas) for b in range(2)]
        r_z16 = [em.res(f"z16{b}") for b in range(2)]
        cacc = [self.sb(f"cacc{b}", [128, T], F32, pas) for b in range(2)]
        r_cacc = [em.res(f"cacc{b}") for b in range(2)]
        xt16 = [self.sb(f"xt16_{b}", [128, T], BF16, pas) for b in range(2)]
        r_xt16 = [em.res(f"xt16{b}") for b in range(2)]
        convw = self.sb("convw_s", [128, 34 * 4], F32, pas)
        convb = self.sb("convb_s", [128, 34], F32, pas)
        r_cw, r_cb = em.res(), em.res()
        dtr = self.sb("dtr", [NH, T], F32, pas)
        dtt = self.sb("dtt", [NH, T], F32, pas)
        dtv = self.sb("dtv", [NH, T], F32, pas)
        r_dtr, r_dtt, r_dtv = em.res(), em.res(), em.res()
        acol = self.sb("acol", [NH, 2], F32, pas)
        r_acol = em.res()

        em.op("sp", lambda e: e.dma_start(convw[:, :], self.convw_h),
              writes=[r_cw], dma=True)
        em.op("sp", lambda e: e.dma_start(convb[:, :], self.convb_h),
              writes=[r_cb], dma=True)

        for fs in range(8):
            wb, r_wb = wz[fs % 2], r_wz[fs % 2]
            em.op("pool", lambda e, fs=fs, wb=wb: e.dma_start(
                wb[:, :].rearrange("p (k c) -> p k c", k=KT),
                self.winT_h.rearrange("(k p) f -> p k f", k=KT)[
                    :, :, fs * 512:(fs + 1) * 512]),
                writes=[r_wb], dma=True, dbg=f"wz{fs}")
            for tt in range(NTT):
                bi = tt % 2
                def zmm(e, wb=wb, tt=tt, bi=bi):
                    last = None
                    for k in range(KT):
                        last = e.matmul(
                            pb[bi][:, :],
                            self.xnTb[:, :].rearrange("p (k t) -> p k t", k=KT)[
                                :, k, tt * 128:(tt + 1) * 128],
                            wb[:, :].rearrange("p (k c) -> p k c", k=KT)[:, k],
                            start=(k == 0), stop=(k == KT - 1))
                    return last
                em.op("pe", zmm, reads=[r_wb] + self.r_xnT, writes=[r_pb[bi]],
                      dbg=f"zmm{fs}_{tt}")
                zb, r_zb = z16[tt % 2], r_z16[tt % 2]
                em.op("act", lambda e, bi=bi, zb=zb: e.activation(
                    zb[:, :], pb[bi][:, :], AF.Silu),
                    reads=[r_pb[bi]], writes=[r_zb], dbg=f"zsilu{fs}_{tt}")
                em.op("sp", lambda e, fs=fs, tt=tt, zb=zb: e.dma_start(
                    self.zs_h[tt * 128:(tt + 1) * 128,
                              fs * 512:(fs + 1) * 512], zb[:, :]),
                    reads=[r_zb], writes=[self.r_zs[tt]], dma=True,
                    dbg=f"zst{fs}_{tt}")

        for j in range(34):
            wb, r_wb = wx[j % 2], r_wx[j % 2]
            f0 = I4 + j * 128
            em.op("pool", lambda e, f0=f0, wb=wb: e.dma_start(
                wb[:, :].rearrange("p (k c) -> p k c", k=KT),
                self.winT_h.rearrange("(k p) f -> p k f", k=KT)[
                    :, :, f0:f0 + 128]),
                writes=[r_wb], dma=True, dbg=f"wx{j}")
            qoff = (j % 2) * 4
            r_quad = [r_pb[qoff + n] for n in range(4)]
            for n in range(4):
                bi = qoff + n
                def xmm(e, wb=wb, n=n, bi=bi):
                    last = None
                    for k in range(KT):
                        last = e.matmul(
                            pb[bi][:, :],
                            wb[:, :].rearrange("p (k c) -> p k c", k=KT)[:, k],
                            self.xnTb[:, :].rearrange("p (k t) -> p k t", k=KT)[
                                :, k, n * 512:(n + 1) * 512],
                            start=(k == 0), stop=(k == KT - 1))
                    return last
                em.op("pe", xmm, reads=[r_wb] + self.r_xnT, writes=[r_pb[bi]],
                      dbg=f"xmm{j}_{n}")
            cb_, r_cc = cacc[j % 2], r_cacc[j % 2]
            pvw = self.pview[:, qoff * 512:(qoff + 4) * 512]
            em.op("act", lambda e, j=j, cb_=cb_, pvw=pvw: e.activation(
                cb_[:, :], pvw, AF.Copy, scale=convw[:, j * 4 + 3:j * 4 + 4]),
                reads=r_quad + [r_cw], writes=[r_cc], dbg=f"cv3_{j}")
            for kk in (2, 1, 0):
                sft = 3 - kk
                em.op("dve", lambda e, j=j, kk=kk, sft=sft, cb_=cb_, pvw=pvw:
                      e.scalar_tensor_tensor(
                          cb_[:, sft:T], pvw[:, 0:T - sft],
                          convw[:, j * 4 + kk:j * 4 + kk + 1],
                          cb_[:, sft:T], ALU.mult, ALU.add),
                      reads=r_quad + [r_cw, r_cc], writes=[r_cc],
                      dbg=f"cv{kk}_{j}")
            if j < 32:
                ob, r_ob = xt16[j % 2], r_xt16[j % 2]
                em.op("act", lambda e, j=j, cb_=cb_, ob=ob: e.activation(
                    ob[:, :], cb_[:, :], AF.Silu,
                    bias=convb[:, j:j + 1]),
                    reads=[r_cc, r_cb], writes=[r_ob], dbg=f"cvs{j}")
                em.op("sp", lambda e, j=j, ob=ob: e.dma_start(
                    self.xts_h[:, j * 128:(j + 1) * 128, :].transpose([1, 0, 2]),
                    ob[:, :].rearrange("p (c t) -> p c t", c=NCH)),
                    reads=[r_ob], writes=self.r_xts, dma=True, dbg=f"xst{j}")
            else:
                bi2 = j - 32
                em.op("act", lambda e, j=j, bi2=bi2, cb_=cb_: e.activation(
                    self.bct[:, bi2 * T:(bi2 + 1) * T], cb_[:, :], AF.Silu,
                    bias=convb[:, j:j + 1]),
                    reads=[r_cc, r_cb], writes=[self.r_bct[bi2]], dbg=f"cvs{j}")

        def dtmm(e, n=None):
            last = None
            for k in range(KT):
                last = e.matmul(
                    pb[6][0:NH, 0:512],
                    self.winT_h_dt[:, :].rearrange("p (k c) -> p k c", k=KT)[:, k],
                    self.xnTb[:, :].rearrange("p (k t) -> p k t", k=KT)[
                        :, k, n * 512:(n + 1) * 512],
                    start=(k == 0), stop=(k == KT - 1))
            return last
        wdt = self.sb("wdt", [128, KT * NH], BF16, pas)
        r_wdt = em.res()
        self.winT_h_dt = wdt
        em.op("pool", lambda e: e.dma_start(
            wdt[:, :].rearrange("p (k c) -> p k c", k=KT),
            self.winT_h.rearrange("(k p) f -> p k f", k=KT)[
                :, :, FOUT - NH:FOUT]),
            writes=[r_wdt], dma=True, dbg="wdt")
        for n in range(4):
            em.op("pe", lambda e, n=n: dtmm(e, n=n),
                  reads=[r_wdt] + self.r_xnT, writes=[r_pb[6]], dbg=f"dtmm{n}")
            em.op("dve", lambda e, n=n: e.tensor_copy(
                dtr[:, n * 512:(n + 1) * 512], pb[6][0:NH, 0:512]),
                reads=[r_pb[6]], writes=[r_dtr], dbg=f"dtev{n}")

        dtb = self.sb("dtb_s", [NH, 1], F32, pas)
        r_dtb = em.res()
        em.op("sp", lambda e: e.dma_start(dtb[:, :], self.dtb_h),
              writes=[r_dtb], dma=True)
        em.op("sp", lambda e: e.dma_start(acol[:, 0:1], self.alog_h),
              writes=[r_acol], dma=True)
        onec = self.constcol(1.0)
        rcc = self.r_cconst
        em.op("dve", lambda e: e.tensor_scalar(
            dtr[:, :], dtr[:, :], dtb[:, :], None, ALU.add),
            reads=[r_dtr, r_dtb], writes=[r_dtr])
        em.op("act", lambda e: e.activation(dtt[:, :], dtr[:, :], AF.Abs),
              reads=[r_dtr], writes=[r_dtt], dbg="abs")
        em.op("act", lambda e: e.activation(dtt[:, :], dtt[:, :], AF.Exp,
                                            scale=-1.0),
              reads=[r_dtt], writes=[r_dtt])
        em.op("act", lambda e: e.activation(dtt[:, :], dtt[:, :], AF.Ln,
                                            bias=onec[0:NH, :]),
              reads=[r_dtt, rcc], writes=[r_dtt])
        em.op("dve", lambda e: e.tensor_scalar(
            dtv[:, :], dtr[:, :], 0.0, None, ALU.max),
            reads=[r_dtr], writes=[r_dtv])
        em.op("dve", lambda e: e.tensor_tensor(
            dtv[:, :], dtv[:, :], dtt[:, :], ALU.add),
            reads=[r_dtv, r_dtt], writes=[r_dtv])
        em.op("dve", lambda e: e.tensor_scalar(
            dtv[:, :], dtv[:, :], 100.0, None, ALU.min),
            reads=[r_dtv], writes=[r_dtv], dbg="clip")
        em.op("act", lambda e: e.activation(acol[:, 1:2], acol[:, 0:1], AF.Exp),
              reads=[r_acol], writes=[r_acol])
        em.op("dve", lambda e: e.tensor_scalar(
            acol[:, 1:2], acol[:, 1:2], -1.0, None, ALU.mult),
            reads=[r_acol], writes=[r_acol])
        em.op("dve", lambda e: e.tensor_scalar(
            self.aT[:, :], dtv[:, :], acol[:, 1:2], None, ALU.mult),
            reads=[r_dtv, r_acol], writes=[self.r_aT], dbg="aT")
        for c in range(NCH):
            em.op("pe", lambda e, c=c: e.transpose(
                pb[7][0:128, 0:NH], dtv[:, c * CH:(c + 1) * CH],
                self.idf[0:NH, 0:NH]),
                reads=[r_dtv, self.r_idf], writes=[r_pb[7]], dbg=f"trdt{c}")
            em.op("dve", lambda e, c=c: e.tensor_copy(
                self.dt_tok[:, c * NH:(c + 1) * NH], pb[7][0:128, 0:NH]),
                reads=[r_pb[7]], writes=[self.r_dtok], dbg=f"dtokev{c}")

        if "zxbcdt" in self.dbg:
            oz = self.dout("dbg_z", [T, I4], BF16)
            for tt in range(NTT):
                em.op("sp", lambda e, tt=tt, oz=oz: e.dma_start(
                    oz[tt * 128:(tt + 1) * 128, :],
                    self.zs_h[tt * 128:(tt + 1) * 128, :]),
                    reads=[self.r_zs[tt]], dma=True)
            ox = self.dout("dbg_xts", [NCH, I4, CH], BF16)
            em.op("sp", lambda e, ox=ox: e.dma_start(ox, self.xts_h),
                  reads=self.r_xts, dma=True)
            ob = self.dout("dbg_bct", [128, 2 * T], BF16)
            em.op("sp", lambda e, ob=ob: e.dma_start(ob, self.bct[:, :]),
                  reads=self.r_bct, dma=True)
            od = self.dout("dbg_dtv", [NH, T], F32)
            em.op("sp", lambda e, od=od: e.dma_start(od, dtv[:, :]),
                  reads=[r_dtv], dma=True)
        self.close_scope(pas)

    def phaseB(self):
        em = self.em
        import contextlib as _ctx
        pbs = _ctx.ExitStack()
        pb, r_pb = self.pbank, self.r_pb
        epsc = self.constcol(EPS_RMS)
        rcc = self.r_cconst

        def dup(mk, n=2):
            return [mk(i) for i in range(n)]

        xtc = dup(lambda i: self.sb(f"xtc{i}", [128, 32 * CH], BF16, pbs))
        r_xtc = dup(lambda i: em.res(f"xtc{i}"))
        zc0 = self.sb("zc", [128, I4], BF16, pbs)
        zcb = [zc0, zc0]
        r_zc0 = em.res("zc")
        r_zcb = [r_zc0, r_zc0]
        Xcb = dup(lambda i: self.sb(f"Xc{i}", [128, I4], BF16, pbs))
        r_Xcb = dup(lambda i: em.res(f"Xc{i}"))
        Xdb = dup(lambda i: self.sb(f"Xd{i}", [128, I4], BF16, pbs))
        r_Xdb = dup(lambda i: em.res(f"Xd{i}"))
        XDs = self.sb("XD", [128, I4], BF16, pbs)
        XDb = [XDs, XDs]
        r_XDs = em.res("XD")
        r_XDb = [r_XDs, r_XDs]
        P32 = self.sb("P32", [128, I4], F32, pbs)
        P16 = self.sb("P16", [128, I4], BF16, pbs)
        r_P32h = [em.res("P32a"), em.res("P32b")]
        r_P16h = [em.res("P16a"), em.res("P16b")]
        y32 = self.sb("y32", [128, I4], BF16, pbs)
        r_y32 = em.res()
        yg0 = self.sb("yg16", [128, I4], BF16, pbs)
        yg16b = [yg0, yg0]
        r_yg0 = em.res("yg0")
        r_ygb = [r_yg0, r_yg0]
        Dq0 = self.sb("Dq0", [128, 16 * CH], F32, pbs)
        Dq = [Dq0, Dq0]
        r_Dq0 = em.res("Dq0")
        r_Dq = [r_Dq0, r_Dq0]
        Mq = dup(lambda i: self.sb(f"Mq{i}", [128, 16 * CH], BF16, pbs), 8)
        r_Mq = dup(lambda i: em.res(f"Mq{i}"), 8)
        dl1 = dup(lambda i: self.sb(f"dl1_{i}", [128, CH], BF16, pbs), 4)
        r_dl1 = dup(lambda i: em.res(f"dl1_{i}"), 4)
        dr1 = dup(lambda i: self.sb(f"dr1_{i}", [128, 16 * CH], BF16, pbs), 4)
        r_dr1 = dup(lambda i: em.res(f"dr1_{i}"), 4)
        Gmb = dup(lambda i: self.sb(f"Gm{i}", [128, CH], BF16, pbs))
        r_Gmb = dup(lambda i: em.res(f"Gm{i}"))
        stkb = dup(lambda i: self.sb(f"stk{i}", [128, CH], F32, pbs))
        r_stkb = dup(lambda i: em.res(f"stk{i}"))
        atokb = dup(lambda i: self.sb(f"atok{i}", [128, NH], F32, pbs))
        dstokb = dup(lambda i: self.sb(f"dstok{i}", [128, NH], BF16, pbs))
        eatokb = dup(lambda i: self.sb(f"eatok{i}", [128, NH], F32, pbs))
        r_atokb = dup(lambda i: em.res(f"atok{i}"))
        r_dstokb = dup(lambda i: em.res(f"dst{i}"))
        r_eatokb = dup(lambda i: em.res(f"eat{i}"))
        dtdsb = dup(lambda i: self.sb(f"dtds{i}", [128, NH], BF16, pbs))
        r_dtdsb = dup(lambda i: em.res(f"dtds{i}"))
        gbcb = dup(lambda i: self.sb(f"gbc{i}", [128, NH], F32, pbs))
        r_gbcb = dup(lambda i: em.res(f"gbc{i}"))
        asplitb = dup(lambda i: self.sb(f"asplit{i}", [NH, 3 * CH], BF16, pbs))
        r_aspb = dup(lambda i: em.res(f"asplit{i}"))
        rres = self.sb("rres", [NH, CH], F32, pbs)
        r_rres = em.res("rres")
        onesf = self.sb("onesf", [NH, CH], F32, pbs)
        r_onesf = em.res()
        dfull = self.sb("dfull", [128, I4], BF16, pbs)
        r_dfull = em.res("dfull")
        ygs0 = self.sb("ygs0", [128, 1024], BF16, pbs)
        ygstage = [ygs0, ygs0]
        r_ygs0 = em.res("ygs0")
        r_ygst = [r_ygs0, r_ygs0]
        btokb = dup(lambda i: self.sb(f"btok{i}", [128, NS], BF16, pbs))
        r_btokb = dup(lambda i: em.res(f"btok{i}"))
        self.gscr_h = self.dint("gscr", [NCH, NH], F32)
        r_gscr = [em.res(f"g{c}") for c in range(NCH)]
        self.negHE_h = self.din("negHE", [32, 32 * CH])

        em.op("dve", lambda e: e.memset(onesf[:, :], 1.0), writes=[r_onesf])
        em.op("dve", lambda e: e.memset(P32[:, :], 0.0), writes=r_P32h)
        em.op("dve", lambda e: e.memset(P16[:, :], 0.0), writes=r_P16h)
        self.one3_h = self.din("one3", [32, CH])
        for i in range(4):
            em.op("dve", lambda e, i=i: e.memset(dl1[i][:, :], 0.0),
                  writes=[r_dl1[i]])
            em.op("pool", lambda e, i=i: e.dma_start(
                dl1[i][96:128, :], self.one3_h), writes=[r_dl1[i]], dma=True)
        for i in range(4):
            qp = i % 2
            em.op("dve", lambda e, i=i: e.memset(dr1[i][:, :], 0.0),
                  writes=[r_dr1[i]])
            for base in (0, 32, 64):
                em.op("pool", lambda e, i=i, qp=qp, base=base: e.dma_start(
                    dr1[i][base:base + 32, :],
                    self.negHE_h[:, qp * 2048:(qp + 1) * 2048]),
                    writes=[r_dr1[i]], dma=True)
        em.op("pool", lambda e: e.dma_start(
            dfull[:, :], bass.AP(self.drep_h.tensor, 0, [[0, 128], [1, I4]])),
            writes=[r_dfull], dma=True, dbg="dfull")
        em.op("sp", lambda e: e.dma_start(
            xtc[0][:, :].rearrange("p (j t) -> p j t", j=32),
            self.xts_h[0].rearrange("(j p) t -> p j t", j=32)),
            reads=self.r_xts, writes=[r_xtc[0]], dma=True, dbg="ldx0")

        def stageA(c):
            P = c % 2
            stk, r_stk = stkb[P], r_stkb[P]
            atok, dstok, eatok = atokb[P], dstokb[P], eatokb[P]
            r_atok, r_dstok, r_eatok = r_atokb[P], r_dstokb[P], r_eatokb[P]
            asplit, r_asp = asplitb[P], r_aspb[P]
            if c + 1 < NCH:
                em.op("sp", lambda e, c=c, P=P: e.dma_start(
                    xtc[1 - P][:, :].rearrange("p (j t) -> p j t", j=32),
                    self.xts_h[c + 1].rearrange("(j p) t -> p j t", j=32)),
                    reads=self.r_xts, writes=[r_xtc[1 - P]], dma=True,
                    dbg=f"ldx{c+1}")
            em.op("dve", lambda e, c=c, stk=stk: e.tensor_tensor_scan(
                stk[0:NH, :], onesf[:, :], self.aT[:, c * CH:(c + 1) * CH],
                0.0, ALU.mult, ALU.add),
                reads=[r_onesf, self.r_aT], writes=[r_stk], dbg=f"cums{c}")
            em.op("dve", lambda e, stk=stk: e.tensor_scalar(
                stk[64:128, :], stk[0:NH, :], stk[0:NH, CH - 1:CH], None,
                ALU.subtract),
                reads=[r_stk], writes=[r_stk], dbg="dsT")
            em.op("dve", lambda e, stk=stk, asplit=asplit: e.tensor_copy(
                asplit[:, 0:CH], stk[0:NH, :]),
                reads=[r_stk], writes=[r_asp], dbg=f"a1_{c}")
            em.op("dve", lambda e, stk=stk, asplit=asplit: e.tensor_tensor(
                rres[:, :], stk[0:NH, :], asplit[:, 0:CH], ALU.subtract),
                reads=[r_stk, r_asp], writes=[r_rres], dbg=f"r2_{c}")
            em.op("dve", lambda e, asplit=asplit: e.tensor_copy(
                asplit[:, CH:2 * CH], rres[:, :]),
                reads=[r_rres], writes=[r_asp], dbg=f"a2_{c}")
            em.op("dve", lambda e, asplit=asplit: e.tensor_tensor(
                rres[:, :], rres[:, :], asplit[:, CH:2 * CH], ALU.subtract),
                reads=[r_rres, r_asp], writes=[r_rres], dbg=f"r3_{c}")
            em.op("dve", lambda e, asplit=asplit: e.tensor_copy(
                asplit[:, 2 * CH:3 * CH], rres[:, :]),
                reads=[r_rres], writes=[r_asp], dbg=f"a3_{c}")
            em.op("sp", lambda e, c=c, asplit=asplit: e.dma_start(
                self.acr_h[c].rearrange("i (h l) -> h i l", h=NH),
                asplit[:, :].rearrange("p (i l) -> p i l", i=3)),
                reads=[r_asp], writes=[self.r_acr[c]], dma=True, dbg=f"acr{c}")
            em.op("pe", lambda e, stk=stk: e.transpose(
                pb[0][:, 0:128], stk[:, :], self.idf[:, :]),
                reads=[r_stk, self.r_idf], writes=[r_pb[0]], dbg=f"trstk{c}")
            em.op("act", lambda e, atok=atok: e.activation(
                atok[:, :], pb[0][:, 0:NH], AF.Copy),
                reads=[r_pb[0]], writes=[r_atok], dbg="atok")
            em.op("act", lambda e, dstok=dstok: e.activation(
                dstok[:, :], pb[0][:, NH:128], AF.Exp, scale=-1.0),
                reads=[r_pb[0], r_atok], writes=[r_dstok], dbg="dstok")
            em.op("act", lambda e, eatok=eatok, atok=atok: e.activation(
                eatok[:, :], atok[:, :], AF.Exp),
                reads=[r_atok], writes=[r_eatok], dbg="eatok")
            em.op("dve", lambda e, c=c, P=P, dstok=dstok: e.tensor_tensor(
                dtdsb[P][:, :], self.dt_tok[:, c * NH:(c + 1) * NH],
                dstok[:, :], ALU.mult),
                reads=[self.r_dtok, r_dstok], writes=[r_dtdsb[P]],
                dbg=f"dtds{c}")
            em.op("pool", lambda e, c=c, stk=stk: e.dma_start(
                self.gscr_h[c:c + 1, :].rearrange("o h -> h o"),
                stk[0:NH, CH - 1:CH]),
                reads=[r_stk], writes=[r_gscr[c]], dma=True, dbg=f"gst{c}")
            em.op("pool", lambda e, c=c, P=P: e.dma_start(
                gbcb[P][:, :],
                bass.AP(self.gscr_h.tensor, c * NH, [[0, 128], [1, NH]])),
                reads=[r_gscr[c]], writes=[r_gbcb[P]], dma=True, dbg=f"gbc{c}")
            em.op("act", lambda e, P=P: e.activation(
                gbcb[P][:, :], gbcb[P][:, :], AF.Exp),
                reads=[r_gbcb[P]], writes=[r_gbcb[P]], dbg=f"gexp{c}")
            em.op("pe", lambda e, c=c: e.transpose(
                pb[0][:, 0:128].bitcast(BF16)[:, 0:128],
                self.bct[:, c * CH:(c + 1) * CH], self.id16[:, :]),
                reads=[self.r_bct[0], self.r_id16], writes=[r_pb[0]],
                dbg=f"trB{c}")
            em.op("act", lambda e, P=P: e.activation(
                btokb[P][:, :], pb[0][:, 0:128].bitcast(BF16)[:, 0:128],
                AF.Copy),
                reads=[r_pb[0]], writes=[r_btokb[P]], dbg="btok")
            em.op("pe", lambda e, c=c: e.matmul(
                pb[1][:, 0:128], self.bct[:, c * CH:(c + 1) * CH],
                self.bct[:, T + c * CH:T + (c + 1) * CH],
                start=True, stop=True),
                reads=self.r_bct, writes=[r_pb[1]], dbg=f"G{c}")
            em.op("dve", lambda e, P=P: e.tensor_tensor(
                Gmb[P][:, :], pb[1][:, 0:128], self.tril[:, :], ALU.mult),
                reads=[r_pb[1], self.r_tril], writes=[r_Gmb[P]], dbg="Gm")

        def stageC(c):
            P = c % 2
            xb, r_xb = xtc[P], r_xtc[P]
            Xc, r_Xc = Xcb[P], r_Xcb[P]
            Xd, r_Xd = Xdb[P], r_Xdb[P]
            XD, r_XD = XDb[P], r_XDb[P]
            asplit, r_asp = asplitb[P], r_aspb[P]
            dstok, r_dstok = dstokb[P], r_dstokb[P]
            for gg in range(4):
                bp = (gg % 2) * 2
                for sub in range(2):
                    g = gg * 2 + sub
                    pv = pb[bp + sub][:, :].bitcast(BF16)

                    def trx(e, g=g, pv=pv, xb=xb):
                        last = None
                        for j in range(4):
                            last = e.transpose(
                                pv[:, j * 128:(j + 1) * 128],
                                xb[:, (g * 4 + j) * CH:(g * 4 + j + 1) * CH],
                                self.id16[:, :])
                        return last
                    em.op("pe", trx, reads=[r_xb, self.r_id16],
                          writes=[r_pb[bp + sub]], dbg=f"trx{c}_{g}")
                pvb = self.pview[:, :].bitcast(BF16).rearrange(
                    "p (b x) -> p b x", b=8)[:, bp:bp + 2, 0:512]
                wide4 = pvb.rearrange("p b (h q) -> p b h q", h=8)
                sl = slice(gg * 1024, (gg + 1) * 1024)
                em.op("dve", lambda e, gg=gg, sl=sl, wide4=wide4, c=c, Xc=Xc:
                      e.tensor_tensor(
                      Xc[:, sl].rearrange("p (b h q) -> p b h q", b=2, h=8),
                      wide4,
                      self.dt_tok[:, c * NH + gg * 16:c * NH + (gg + 1) * 16]
                      .rearrange("p (b h) -> p b h", b=2)
                      .unsqueeze(3).broadcast_to([128, 2, 8, PP]),
                      ALU.mult),
                      reads=[r_pb[bp], r_pb[bp + 1], self.r_dtok],
                      writes=[r_Xc], dbg=f"xc{c}_{gg}")
                em.op("dve", lambda e, sl=sl, pvb=pvb, XD=XD: e.tensor_tensor(
                    XD[:, sl].rearrange("p (b x) -> p b x", b=2),
                    pvb, dfull[:, sl].rearrange("p (b x) -> p b x", b=2),
                    ALU.mult),
                    reads=[r_pb[bp], r_pb[bp + 1], r_dfull], writes=[r_XD],
                    dbg=f"xD{c}_{gg}")
                em.op("dve", lambda e, gg=gg, sl=sl, Xd=Xd, Xc=Xc, P=P:
                      e.tensor_tensor(
                      Xd[:, sl].rearrange("p (h q) -> p h q", h=16),
                      Xc[:, sl].rearrange("p (h q) -> p h q", h=16),
                      dstokb[P][:, gg * 16:(gg + 1) * 16]
                      .unsqueeze(2).broadcast_to([128, 16, PP]),
                      ALU.mult),
                      reads=[r_Xc, r_dstokb[P]], writes=[r_Xd],
                      dbg=f"xd{c}_{gg}")


        def decay_chunk(c):
            P = c % 2
            asplit, r_asp = asplitb[P], r_aspb[P]
            for q in range(4):
                half = q // 2
                l1, r_l1 = dl1[P * 2 + half], r_dl1[P * 2 + half]
                if q % 2 == 0:
                    h0 = half * 32
                    for i3, base in ((0, 0), (1, 32), (2, 64)):
                        em.op("dve", lambda e, h0=h0, l1=l1, i3=i3, base=base,
                              asplit=asplit: e.tensor_copy(
                              l1[base:base + 32, :],
                              asplit[h0:h0 + 32, i3 * CH:(i3 + 1) * CH]),
                              reads=[r_asp], writes=[r_l1],
                              dbg=f"dl{i3}_{c}_{half}")
                di = (c % 2) * 2 + (q % 2)
                rb1, r_rb1 = dr1[di], r_dr1[di]
                for i3 in range(3):
                    em.op("pool", lambda e, c=c, q=q, rb1=rb1, i3=i3:
                          e.dma_start(
                          rb1[96 + i3:97 + i3, :],
                          self.acr_h[c, i3, q * 16 * CH:(q + 1) * 16 * CH]
                          .unsqueeze(0)),
                          reads=[self.r_acr[c]], writes=[r_rb1], dma=True,
                          dbg=f"ar{i3}_{c}_{q}")
                db_, r_db = Dq[q % 2], r_Dq[q % 2]
                mb = Mq[P * 4 + q]
                r_mb = r_Mq[P * 4 + q]
                for dd in range(2):
                    for sub in range(2):
                        ssl = dd * 2 + sub
                        bk = 2 + sub
                        cs = slice(ssl * 512, (ssl + 1) * 512)

                        def dmm(e, cs=cs, bk=bk, l1=l1, rb1=rb1):
                            return e.matmul(pb[bk][:, :], l1[:, :],
                                            rb1[:, cs], start=True, stop=True)
                        em.op("pe", dmm,
                              reads=[r_l1, r_rb1],
                              writes=[r_pb[bk]], dbg=f"dmm{c}_{q}_{ssl}")
                    em.op("act", lambda e, dd=dd, db_=db_: e.activation(
                        db_[:, dd * 1024:(dd + 1) * 1024],
                        self.pview[:, 1024:2048], AF.Relu, scale=-1.0),
                        reads=[r_pb[2], r_pb[3]], writes=[r_db],
                        dbg=f"dev{c}_{q}_{dd}")
                em.op("act", lambda e, db_=db_, mb=mb: e.activation(
                    mb[:, :], db_[:, :], AF.Exp, scale=-1.0),
                    reads=[r_db], writes=[r_mb], dbg=f"E{c}_{q}")
                em.op("dve",
                      lambda e, mb=mb, P=P: e.tensor_tensor(
                      mb[:, :].rearrange("p (h l) -> p h l", h=16),
                      mb[:, :].rearrange("p (h l) -> p h l", h=16),
                      Gmb[P][:, :].unsqueeze(1).broadcast_to([128, 16, CH]),
                      ALU.mult),
                      reads=[r_mb, r_Gmb[P]], writes=[r_mb], dbg=f"M{c}_{q}")

        def stageB(c):
            P = c % 2
            Xc, r_Xc = Xcb[P], r_Xcb[P]
            XD, r_XD = XDb[P], r_XDb[P]
            eatok, r_eatok = eatokb[P], r_eatokb[P]
            pbank_h = [pb[4 + i] for i in range(4)]
            r_pbh = [r_pb[4 + i] for i in range(4)]
            em.op("sp", lambda e, c=c: e.dma_start(
                zcb[0][:, :], self.zs_h[c * CH:(c + 1) * CH, :]),
                reads=[self.r_zs[c]], writes=[r_zcb[0]], dma=True,
                dbg=f"ldz{c}")
            for half in range(2):
                def yoff(e, c=c, half=half, pbank_h=pbank_h):
                    last = None
                    for n in range(4):
                        last = e.matmul(
                            pbank_h[n][:, :],
                            self.bct[:, T + c * CH:T + (c + 1) * CH],
                            P16[:, half * 2048 + n * 512:
                                half * 2048 + (n + 1) * 512],
                            start=True, stop=True)
                    return last
                em.op("pe", yoff, reads=[self.r_bct[1], r_P16h[half]],
                      writes=r_pbh, dbg=f"yoff{c}_{half}")
                em.op("dve", lambda e, half=half, P=P: e.tensor_tensor(
                    P32[:, half * 2048:(half + 1) * 2048]
                    .rearrange("p (h q) -> p h q", h=32),
                    P32[:, half * 2048:(half + 1) * 2048]
                    .rearrange("p (h q) -> p h q", h=32),
                    gbcb[P][:, half * 32:(half + 1) * 32].unsqueeze(2)
                    .broadcast_to([128, 32, PP]),
                    ALU.mult),
                    reads=[r_P32h[half], r_gbcb[P]], writes=[r_P32h[half]],
                    dbg=f"Pg{c}_{half}")
                em.op("dve", lambda e, half=half, eatok=eatok:
                      e.tensor_tensor(
                      y32[:, half * 2048:(half + 1) * 2048]
                      .rearrange("p (h q) -> p h q", h=32),
                      self.pview[:, 2048:4096]
                      .rearrange("p (h q) -> p h q", h=32),
                      eatok[:, half * 32:(half + 1) * 32]
                      .unsqueeze(2).broadcast_to([128, 32, PP]),
                      ALU.mult),
                      reads=r_pbh + [r_eatok], writes=[r_y32],
                      dbg=f"ev1_{c}_{half}")

                def yd(e, c=c, half=half, pbank_h=pbank_h, Xc=Xc, XD=XD, P=P):
                    last = None
                    for n in range(4):
                        last = e.matmul(
                            pbank_h[n][:, :], self.id16[:, :],
                            XD[:, half * 2048 + n * 512:
                               half * 2048 + (n + 1) * 512],
                            start=True, stop=False)
                    for hh in range(32):
                        h = half * 32 + hh
                        qq, hq = divmod(hh, 16)
                        n, off = divmod(hh * PP, 512)
                        last = e.matmul(
                            pbank_h[n][:, off:off + PP],
                            Mq[P * 4 + half * 2 + qq][:, hq * CH:(hq + 1) * CH],
                            Xc[:, h * PP:(h + 1) * PP],
                            start=False, stop=(off + PP == 512))
                    return last
                em.op("pe", yd,
                      reads=[self.r_id16, r_XD, r_Xc,
                             r_Mq[P * 4 + half * 2], r_Mq[P * 4 + half * 2 + 1]],
                      writes=r_pbh, dbg=f"yd{c}_{half}")
                em.op("dve", lambda e, half=half: e.scalar_tensor_tensor(
                    y32[:, half * 2048:(half + 1) * 2048],
                    self.pview[:, 2048:4096], 1.0,
                    y32[:, half * 2048:(half + 1) * 2048],
                    ALU.mult, ALU.add),
                    reads=r_pbh + [r_y32], writes=[r_y32],
                    dbg=f"ev2_{c}_{half}")
                stk_, r_stk_ = stkb[P], r_stkb[P]
                em.op("act", lambda e, half=half, stk_=stk_, P=P: e.activation(
                    yg16b[P][:, half * 2048:(half + 1) * 2048],
                    y32[:, half * 2048:(half + 1) * 2048], AF.Square,
                    accum_out=stk_[0:128, 2 + half:3 + half]),
                    reads=[r_y32], writes=[r_ygb[P], r_stk_],
                    dbg=f"ssh{c}_{half}")

        def stageD(c):
            P = c % 2
            Xd, r_Xd = Xdb[P], r_Xdb[P]
            stk, r_stk = stkb[P], r_stkb[P]
            yg16, r_yg = yg16b[P], r_ygb[P]
            pbank_h = [pb[4 + i] for i in range(4)]
            r_pbh = [r_pb[4 + i] for i in range(4)]
            for half in range(2):
                def stmm(e, half=half, pbank_h=pbank_h, Xd=Xd, P=P):
                    last = None
                    for n in range(4):
                        last = e.matmul(
                            pbank_h[n][:, :], btokb[P][:, :],
                            Xd[:, half * 2048 + n * 512:
                               half * 2048 + (n + 1) * 512],
                            start=True, stop=True)
                    return last
                em.op("pe", stmm, reads=[r_btokb[P], r_Xd], writes=r_pbh,
                      dbg=f"st{c}_{half}")
                em.op("dve", lambda e, half=half: e.scalar_tensor_tensor(
                    P16[:, half * 2048:(half + 1) * 2048],
                    self.pview[:, 2048:4096], 1.0,
                    P32[:, half * 2048:(half + 1) * 2048],
                    ALU.mult, ALU.add),
                    reads=r_pbh + [r_P32h[half]], writes=[r_P16h[half]],
                    dbg=f"P16{c}_{half}")
                em.op("dve", lambda e, half=half: e.tensor_tensor(
                    P32[:, half * 2048:(half + 1) * 2048],
                    P32[:, half * 2048:(half + 1) * 2048],
                    self.pview[:, 2048:4096], ALU.add),
                    reads=r_pbh + [r_P32h[half]], writes=[r_P32h[half]],
                    dbg=f"Pa{c}_{half}")

            em.op("dve", lambda e, stk=stk: e.tensor_scalar(
                stk[:, 1:2], stk[:, 2:3], stk[:, 3:4], 1.0 / I4, ALU.add,
                ALU.mult),
                reads=[r_stk], writes=[r_stk], dbg=f"sscomb{c}")
            em.op("act", lambda e, stk=stk: e.activation(
                stk[:, 1:2], stk[:, 1:2], AF.Ln, bias=epsc),
                reads=[r_stk, rcc], writes=[r_stk])
            em.op("act", lambda e, stk=stk: e.activation(
                stk[:, 1:2], stk[:, 1:2], AF.Exp, scale=-0.5),
                reads=[r_stk], writes=[r_stk], dbg=f"rstd{c}")
            em.op("dve", lambda e, yg16=yg16, stk=stk, P=P:
                  e.scalar_tensor_tensor(
                  yg16[:, :], y32[:, :], stk[:, 1:2], zcb[P][:, :],
                  ALU.mult, ALU.mult),
                  reads=[r_y32, r_stk, r_zcb[P]], writes=[r_yg],
                  dbg=f"ygfuse{c}")
            for gg in range(4):
                for sub in range(2):
                    g = gg * 2 + sub
                    pv = pb[2 + sub][:, :].bitcast(BF16)

                    def tryg(e, g=g, pv=pv, yg16=yg16):
                        last = None
                        for j in range(4):
                            last = e.transpose(
                                pv[:, j * 128:(j + 1) * 128],
                                yg16[:, (g * 4 + j) * 128:
                                     (g * 4 + j + 1) * 128],
                                self.id16[:, :])
                        return last
                    em.op("pe", tryg, reads=[r_yg, self.r_id16],
                          writes=[r_pb[2 + sub]], dbg=f"tryg{c}_{g}")
                pvb = self.pview[:, :].bitcast(BF16).rearrange(
                    "p (b x) -> p b x", b=8)[:, 2:4, 0:512]
                sb_, r_sb = ygstage[gg % 2], r_ygst[gg % 2]
                em.op("act", lambda e, pvb=pvb, sb_=sb_: e.activation(
                    sb_[:, :].rearrange("p (b x) -> p b x", b=2), pvb,
                    AF.Copy),
                    reads=[r_pb[2], r_pb[3]], writes=[r_sb],
                    dbg=f"evyg{c}_{gg}")
                em.op("sp", lambda e, c=c, gg=gg, sb_=sb_: e.dma_start(
                    self.ygt_h[c, gg * 1024:(gg + 1) * 1024, :]
                    .rearrange("(j p) t -> p j t", j=8),
                    sb_[:, :].rearrange("p (j t) -> p j t", j=8)),
                    reads=[r_sb], writes=[self.r_ygt[c]], dma=True,
                    dbg=f"styg{c}_{gg}")

        nch = NCH
        for d in self.dbg:
            if d.startswith("nch"):
                nch = int(d[3:])
        stageA(0)
        stageC(0)
        decay_chunk(0)
        for k in range(1, nch):
            stageA(k)
            stageB(k - 1)
            stageC(k)
            decay_chunk(k)
            stageD(k - 1)
        stageB(nch - 1)
        stageD(nch - 1)

        if "yg" in self.dbg:
            o = self.dout("dbg_ygt", [NCH, I4, CH], BF16)
            em.op("sp", lambda e, o=o: e.dma_start(o, self.ygt_h),
                  reads=self.r_ygt, dma=True)
        self.close_scope(pbs)

    def phaseD(self):
        em = self.em
        import contextlib as _ctx
        pds = _ctx.ExitStack()
        pb, r_pb = self.pbank, self.r_pb

        wo16 = self.sb("wo16", [128, 32 * D], BF16, pds)
        r_wo = [em.res("wo16a"), em.res("wo16b")]
        nwcol = self.sb("nwcol", [128, 32], F32, pds)
        r_nw = em.res("nw")
        ygcol = [self.sb(f"ygc{b}", [128, 32 * 128], BF16, pds) for b in range(2)]
        r_ygc = [em.res(f"ygc{b}") for b in range(2)]
        x_t = [self.sb(f"xd{b}", [128, D], F32, pds) for b in range(2)]
        r_xt = [em.res(f"xd{b}") for b in range(2)]
        f_t = [self.sb(f"fd{b}", [128, D], BF16, pds) for b in range(2)]
        r_ft = [em.res(f"fd{b}") for b in range(2)]
        o_t = [self.sb(f"od{b}", [128, D], F32, pds) for b in range(2)]
        r_ot = [em.res(f"od{b}") for b in range(2)]

        em.op("sp", lambda e: e.dma_start(nwcol[:, :], self.nw_h),
              writes=[r_nw], dma=True)
        for dh in range(2):
            em.op("pool", lambda e, dh=dh: e.dma_start(
                wo16[:, :].rearrange("p (j x) -> p j x", j=32)[
                    :, :, dh * 1024:(dh + 1) * 1024],
                self.woutT_h.rearrange("(j p) d -> p j d", j=32)[
                    :, :, dh * 1024:(dh + 1) * 1024]),
                writes=[r_wo[dh]], dma=True, dbg=f"ldwo{dh}")
        for j in range(32):
            for dh in range(2):
                em.op("dve", lambda e, j=j, dh=dh: e.tensor_scalar(
                    wo16[:, j * D + dh * 1024:j * D + (dh + 1) * 1024],
                    wo16[:, j * D + dh * 1024:j * D + (dh + 1) * 1024],
                    nwcol[:, j:j + 1], None, ALU.mult),
                    reads=[r_wo[dh], r_nw], writes=[r_wo[dh]],
                    dbg=f"nwfold{j}_{dh}")

        em.op("sp", lambda e: e.dma_start(
            ygcol[0][:, :].rearrange("p (j t) -> p j t", j=32),
            self.ygt_h[0].rearrange("(j p) t -> p j t", j=32)),
            reads=[self.r_ygt[0]], writes=[r_ygc[0]], dma=True)
        em.op("sp", lambda e: e.dma_start(x_t[0][:, :], self.x_h[0:128, :]),
              writes=[r_xt[0]], dma=True)
        em.op("sp", lambda e: e.dma_start(f_t[0][:, :], self.ffn_h[0:128, :]),
              reads=[self.r_ffn[0]], writes=[r_ft[0]], dma=True)

        for tt in range(NTT):
            bi = tt % 2
            yb, r_yb = ygcol[bi], r_ygc[bi]
            xb, r_xb = x_t[bi], r_xt[bi]
            fb, r_fb = f_t[bi], r_ft[bi]
            ob, r_ob = o_t[bi], r_ot[bi]
            if tt + 1 < NTT:
                em.op("sp", lambda e, tt=tt, bi=bi: e.dma_start(
                    ygcol[1 - bi][:, :].rearrange("p (j t) -> p j t", j=32),
                    self.ygt_h[tt + 1].rearrange("(j p) t -> p j t", j=32)),
                    reads=[self.r_ygt[tt + 1]], writes=[r_ygc[1 - bi]],
                    dma=True)
                em.op("sp", lambda e, tt=tt, bi=bi: e.dma_start(
                    x_t[1 - bi][:, :],
                    self.x_h[(tt + 1) * 128:(tt + 2) * 128, :]),
                    writes=[r_xt[1 - bi]], dma=True)
                em.op("sp", lambda e, tt=tt, bi=bi: e.dma_start(
                    f_t[1 - bi][:, :],
                    self.ffn_h[(tt + 1) * 128:(tt + 2) * 128, :]),
                    reads=[self.r_ffn[tt + 1]], writes=[r_ft[1 - bi]],
                    dma=True)
            for dsl in range(4):
                bk = dsl
                dh = dsl // 2
                def omm(e, yb=yb, dsl=dsl, bk=bk):
                    last = None
                    for j in range(32):
                        last = e.matmul(
                            pb[bk][:, :],
                            yb[:, j * 128:(j + 1) * 128],
                            wo16[:, :].rearrange("p (j d) -> p j d", j=32)[
                                :, j, dsl * 512:(dsl + 1) * 512],
                            start=(j == 0), stop=(j == 31))
                    return last
                em.op("pe", omm, reads=[r_yb, r_wo[dh]], writes=[r_pb[bk]],
                      dbg=f"omm{tt}_{dsl}")
                sl = slice(dsl * 512, (dsl + 1) * 512)
                em.op("dve", lambda e, sl=sl, bk=bk, ob=ob, xb=xb:
                      e.scalar_tensor_tensor(
                          ob[:, sl], pb[bk][:, :], 1.0, xb[:, sl],
                          ALU.mult, ALU.add),
                      reads=[r_pb[bk], r_xb], writes=[r_ob],
                      dbg=f"oev{tt}_{dsl}")
            em.op("dve", lambda e, ob=ob, fb=fb: e.tensor_tensor(
                ob[:, :], ob[:, :], fb[:, :], ALU.add),
                reads=[r_ob, r_fb], writes=[r_ob], dbg=f"offn{tt}")
            em.op("sp", lambda e, tt=tt, ob=ob: e.dma_start(
                self.out_h[tt * 128:(tt + 1) * 128, :], ob[:, :]),
                reads=[r_ob], writes=[self.r_out], dma=True, dbg=f"out{tt}")
        self.close_scope(pds)

    def phaseF(self):
        em = self.em
        import contextlib as _ctx
        pfs = _ctx.ExitStack()
        pb, r_pb = self.pbank, self.r_pb
        HT = T // 2
        NTH = NTT // 2

        acc16 = self.sb("acc16", [128, NTH * D], BF16, pfs)
        r_acc = [em.res(f"acc{t}") for t in range(NTH)]
        h16b = [self.sb(f"h16_{b}", [128, 8 * HT], BF16, pfs) for b in range(2)]
        r_h16b = [[em.res(f"h16_{b}_{j}") for j in range(8)] for b in range(2)]
        srowb = [self.sb(f"srow{b}", [128, HT], BF16, pfs) for b in range(2)]
        r_srowb = [em.res(f"srow{b}") for b in range(2)]
        wd16 = self.sb("wd16", [128, 8 * D], BF16, pfs)
        r_wd = em.res("wd16")
        wupb = [self.sb(f"wupb{b}", [128, KT * 128], BF16, pfs) for b in range(2)]
        r_wupb = [em.res(f"wupb{b}") for b in range(2)]
        bupc = self.sb("bupc_s", [128, EXP * 8], F32, pfs)
        r_bup = em.res("bupc")
        bdn16 = self.sb("bdn16", [EXP, D], BF16, pfs)
        r_bdn = em.res("bdn16")

        self.bupc_h = self.din("bupc", [128, EXP * 8])
        self.bdn_h = self.din("bdn", [EXP, D])
        em.op("sp", lambda e: e.dma_start(bupc[:, :], self.bupc_h),
              writes=[r_bup], dma=True)
        em.op("pool", lambda e: e.dma_start(bdn16[:, :], self.bdn_h),
              writes=[r_bdn], dma=True)

        for th in range(2):
            t_base = th * HT
            for e_i in range(EXP):
                h16 = h16b[e_i % 2]
                r_h16 = r_h16b[e_i % 2]
                srow, r_srow = srowb[e_i % 2], r_srowb[e_i % 2]
                em.op("sp", lambda e, e_i=e_i, t_base=t_base, srow=srow:
                    e.dma_start(
                    srow[:, :],
                    bass.AP(self.scoh_h.tensor, e_i * T + t_base,
                            [[0, 128], [1, HT]])),
                    reads=[self.r_scoh], writes=[r_srow], dma=True,
                    dbg=f"srow{th}_{e_i}")
                for ht in range(8):
                    wb, r_wb = wupb[ht % 2], r_wupb[ht % 2]
                    c0 = e_i * HE + ht * 128
                    em.op("pool", lambda e, c0=c0, wb=wb: e.dma_start(
                        wb[:, :].rearrange("p (k c) -> p k c", k=KT),
                        self.wupT_h.rearrange("(k p) c -> p k c", k=KT)[
                            :, :, c0:c0 + 128]),
                        writes=[r_wb], dma=True, dbg=f"wup{th}_{e_i}_{ht}")
                    if ht == 2:
                        em.op("pool", lambda e, e_i=e_i: e.dma_start(
                            wd16[:, :].rearrange("p (j d) -> p j d", j=8),
                            self.wdT_h[e_i].rearrange("(j p) d -> p j d", j=8)),
                            writes=[r_wd], dma=True, dbg=f"wd{th}_{e_i}")
                    for n in range(2):
                        bi = 6 + n % 2
                        def up_fn(e, wb=wb, n=n, bi=bi, t_base=t_base):
                            last = None
                            for k in range(KT):
                                last = e.matmul(
                                    pb[bi][:, :],
                                    wb[:, :].rearrange(
                                        "p (k c) -> p k c", k=KT)[:, k],
                                    self.xnTb[:, :].rearrange(
                                        "p (k t) -> p k t", k=KT)[
                                        :, k,
                                        t_base + n * 512:t_base + (n + 1) * 512],
                                    start=(k == 0), stop=(k == KT - 1))
                            return last
                        em.op("pe", up_fn, reads=[r_wb] + self.r_xnT,
                              writes=[r_pb[bi]], dbg=f"up{th}_{e_i}_{ht}_{n}")
                        em.op("act", lambda e, ht=ht, n=n, e_i=e_i, bi=bi,
                              h16=h16: e.activation(
                                  h16[:, ht * HT + n * 512:
                                      ht * HT + (n + 1) * 512],
                                  pb[bi][:, :], AF.Silu,
                                  bias=bupc[:, e_i * 8 + ht:e_i * 8 + ht + 1]),
                              reads=[r_pb[bi], r_bup], writes=[r_h16[ht]],
                              dbg=f"silu{th}_{e_i}_{ht}_{n}")
                    em.op("dve", lambda e, ht=ht, h16=h16, srow=srow:
                        e.tensor_tensor(
                        h16[:, ht * HT:(ht + 1) * HT],
                        h16[:, ht * HT:(ht + 1) * HT],
                        srow[:, :], ALU.mult),
                        reads=[r_h16[ht], r_srow], writes=[r_h16[ht]],
                        dbg=f"hsc{th}_{e_i}_{ht}")

                for lt in range(NTH):
                    tt = th * NTH + lt
                    for pr in range(2):
                        for sub in range(2):
                            dsl = pr * 2 + sub
                            bi = 2 + pr * 2 + sub
                            def dn_fn(e, lt=lt, tt=tt, dsl=dsl, e_i=e_i,
                                      bi=bi, h16=h16):
                                last = None
                                if e_i == 0:
                                    last = e.matmul(
                                        pb[bi][:, :],
                                        self.scoT[0:8, tt * 128:(tt + 1) * 128],
                                        bdn16[0:8, dsl * 512:(dsl + 1) * 512],
                                        start=True, stop=False)
                                for j in range(8):
                                    last = e.matmul(
                                        pb[bi][:, :],
                                        h16[:, j * HT + lt * 128:
                                            j * HT + (lt + 1) * 128],
                                        wd16[:, :].rearrange(
                                            "p (j d) -> p j d", j=8)[
                                            :, j, dsl * 512:(dsl + 1) * 512],
                                        start=(e_i != 0 and j == 0),
                                        stop=(j == 7))
                                return last
                            em.op("pe", dn_fn,
                                  reads=[r_wd, self.r_scoT, r_bdn] + r_h16,
                                  writes=[r_pb[bi]],
                                  dbg=f"dn{th}_{e_i}_{lt}_{dsl}")
                        dst = acc16[:, lt * D + pr * 1024:
                                    lt * D + (pr + 1) * 1024]
                        rpair = [r_pb[2 + pr * 2], r_pb[3 + pr * 2]]
                        src = self.pview[:, 1024 + pr * 1024:
                                         2048 + pr * 1024]
                        if e_i == 0:
                            em.op("dve", lambda e, dst=dst, src=src:
                                  e.tensor_copy(dst, src),
                                  reads=rpair, writes=[r_acc[lt]],
                                  dbg=f"ev0_{th}_{lt}_{pr}")
                        else:
                            em.op("dve", lambda e, dst=dst, src=src:
                                  e.scalar_tensor_tensor(
                                      dst, src, 1.0, dst, ALU.mult, ALU.add),
                                  reads=rpair + [r_acc[lt]],
                                  writes=[r_acc[lt]],
                                  dbg=f"ev{e_i}_{th}_{lt}_{pr}")

            for lt in range(NTH):
                tt = th * NTH + lt
                em.op("sp", lambda e, lt=lt, tt=tt: e.dma_start(
                    self.ffn_h[tt * 128:(tt + 1) * 128, :],
                    acc16[:, lt * D:(lt + 1) * D]),
                    reads=[r_acc[lt]], writes=[self.r_ffn[tt]], dma=True,
                    dbg=f"stffn{tt}")

        if "ffn" in self.dbg:
            o = self.dout("dbg_ffn", [T, D], BF16)
            for tt in range(NTT):
                em.op("sp", lambda e, tt=tt, o=o: e.dma_start(
                    o[tt * 128:(tt + 1) * 128, :],
                    self.ffn_h[tt * 128:(tt + 1) * 128, :]),
                    reads=[self.r_ffn[tt]], dma=True)
        self.close_scope(pfs)

    def phase0(self):
        em = self.em
        import contextlib as _ctx
        p0s = _ctx.ExitStack()
        self.xnTb = self.sb("xnTb", [128, KT * T], BF16, self.xns)
        self.r_xnT = [em.res(f"xnT{k}") for k in range(KT)]
        self.scoT = self.sb("scoT", [EXP, T], BF16, self.xns)
        self.r_scoT = em.res("scoT")

        def xnT(k):
            return self.xnTb[:, k * T:(k + 1) * T]

        self.xnT = xnT

        x_tile = [self.sb(f"xt{b}", [128, D], F32, p0s) for b in range(2)]
        r_xtile = [em.res(f"xt{b}") for b in range(2)]
        sq_scr = self.sb("sqscr", [128, D], BF16, p0s)
        r_sq = em.res("sq")
        statsb = [self.sb(f"stats{b}", [128, 8], F32, p0s) for b in range(2)]
        r_statsb = [em.res(f"stats{b}") for b in range(2)]
        xn_stageb = [self.sb(f"xn_stage{b}", [128, D], BF16, p0s)
                     for b in range(2)]
        lo_stageb = [self.sb(f"lo_stage{b}", [128, D], BF16, p0s)
                     for b in range(2)]
        loTb = [self.sb(f"loT{b}", [128, KT * 128], BF16, p0s)
                for b in range(2)]
        r_xnstb = [em.res(f"xnst{b}") for b in range(2)]
        r_lostb = [em.res(f"lost{b}") for b in range(2)]
        r_loTb = [em.res(f"loT{b}") for b in range(2)]

        wg1f = self.sb("wg1f", [128, KT * EXP], F32, p0s)
        wg1hi = self.sb("wg1hi", [128, KT * EXP], BF16, p0s)
        wg1lo = self.sb("wg1lo", [128, KT * EXP], BF16, p0s)
        r_wg1, r_wg1h, r_wg1l = em.res(), em.res(), em.res()
        wg2f = self.sb("wg2f", [EXP, EXP], F32, p0s)
        r_wg2 = em.res()
        em.op("sp", lambda e: e.dma_start(
            wg1f[:, :].rearrange("p (k e) -> p k e", k=KT),
            self.wg1T_h.rearrange("(k p) e -> p k e", k=KT)),
            writes=[r_wg1], dma=True)
        em.op("sp", lambda e: e.dma_start(wg2f[:, :], self.wg2T_h),
              writes=[r_wg2], dma=True)
        em.op("dve", lambda e: e.tensor_copy(wg1hi[:, :], wg1f[:, :]),
              reads=[r_wg1], writes=[r_wg1h])
        em.op("dve", lambda e: e.tensor_tensor(
            wg1lo[:, :], wg1f[:, :], wg1hi[:, :], ALU.subtract),
            reads=[r_wg1, r_wg1h], writes=[r_wg1l])

        l1T = self.sb("l1T", [EXP, T], F32, p0s)
        r_l1 = em.res("l1T")
        l2T = self.sb("l2T", [EXP, T], F32, p0s)
        r_l2 = em.res("l2T")
        lgt = self.sb("lgt", [128, 8], F32, p0s)
        r_lgt = em.res("lgt")
        tkb = self.sb("tkb", [128, 40], F32, p0s)
        r_tk = em.res("tk")

        pb, r_pb = self.pbank, self.r_pb
        epsc = self.constcol(EPS_LN)
        rcc = self.r_cconst

        for tt in range(NTT):
            xb, r_xb = x_tile[tt % 2], r_xtile[tt % 2]
            stats, r_stats = statsb[tt % 2], r_statsb[tt % 2]
            xn_stage, r_xnst = xn_stageb[tt % 2], r_xnstb[tt % 2]
            lo_stage, r_lost = lo_stageb[tt % 2], r_lostb[tt % 2]
            loT, r_loT = loTb[tt % 2], r_loTb[tt % 2]
            em.op("sp", lambda e, tt=tt, xb=xb: e.dma_start(
                xb[:, :], self.x_h[tt * 128:(tt + 1) * 128, :]),
                writes=[r_xb], dma=True, dbg=f"ldx{tt}")
            em.op("dve", lambda e, xb=xb, stats=stats: e.tensor_reduce(
                stats[:, 0:1], xb[:, :], AX.X, ALU.add),
                reads=[r_xb], writes=[r_stats], dbg="lnsum")
            em.op("act", lambda e, xb=xb, stats=stats: e.activation(
                sq_scr[:, :], xb[:, :], AF.Square, accum_out=stats[:, 2:3]),
                reads=[r_xb], writes=[r_sq, r_stats], dbg="lnsq")
            em.op("dve", lambda e, stats=stats: e.tensor_scalar(
                stats[:, 1:2], stats[:, 0:1], 1.0 / D, None, ALU.mult),
                reads=[r_stats], writes=[r_stats])
            em.op("act", lambda e, stats=stats: e.activation(
                stats[:, 3:4], stats[:, 1:2], AF.Square),
                reads=[r_stats], writes=[r_stats])
            em.op("dve", lambda e, stats=stats: e.tensor_scalar(
                stats[:, 3:4], stats[:, 2:3], 1.0 / D, stats[:, 3:4], ALU.mult,
                ALU.subtract), reads=[r_stats], writes=[r_stats], dbg="var")
            em.op("act", lambda e, stats=stats: e.activation(
                stats[:, 4:5], stats[:, 3:4], AF.Ln, bias=epsc),
                reads=[r_stats, rcc], writes=[r_stats])
            em.op("act", lambda e, stats=stats: e.activation(
                stats[:, 4:5], stats[:, 4:5], AF.Exp, scale=-0.5),
                reads=[r_stats], writes=[r_stats], dbg="rstd")
            em.op("dve", lambda e, xb=xb, stats=stats, xn_stage=xn_stage:
                  e.tensor_scalar(
                  xn_stage[:, :], xb[:, :], stats[:, 1:2], stats[:, 4:5],
                  ALU.subtract, ALU.mult),
                  reads=[r_xb, r_stats], writes=[r_xnst], dbg="xn16")
            em.op("dve", lambda e, xb=xb, stats=stats: e.tensor_scalar(
                xb[:, :], xb[:, :], stats[:, 1:2], stats[:, 4:5],
                ALU.subtract, ALU.mult),
                reads=[r_xb, r_stats], writes=[r_xb], dbg="xnf32")
            em.op("dve", lambda e, xb=xb, xn_stage=xn_stage, lo_stage=lo_stage:
                  e.tensor_tensor(
                  lo_stage[:, :], xb[:, :], xn_stage[:, :], ALU.subtract),
                  reads=[r_xb, r_xnst], writes=[r_lost], dbg="lo")

            for g in range(4):
                bi = g % 2
                pv = pb[bi][:, :].bitcast(BF16)

                def tr_fn(e, g=g, pv=pv, xn_stage=xn_stage):
                    last = None
                    for j in range(4):
                        k = g * 4 + j
                        last = e.transpose(pv[:, j * 128:(j + 1) * 128],
                                           xn_stage[:, k * 128:(k + 1) * 128],
                                           self.id16[:, :])
                    return last
                em.op("pe", tr_fn, reads=[r_xnst, self.r_id16], writes=[r_pb[bi]],
                      dbg=f"trxn{tt}_{g}")

                def ev_fn(e, g=g, tt=tt, pv=pv):
                    dst = self.xnTb[:, :].rearrange(
                        "p (k t) -> p k t", k=KT)[:, g * 4:(g + 1) * 4,
                                                  tt * 128:(tt + 1) * 128]
                    return e.tensor_copy(
                        dst, pv[:, 0:512].rearrange("p (j c) -> p j c", j=4))
                em.op("dve", ev_fn, reads=[r_pb[bi]],
                      writes=[self.r_xnT[g * 4 + j] for j in range(4)],
                      dbg=f"evxn{tt}_{g}")

                bi2 = 2 + g % 2
                pv2 = pb[bi2][:, :].bitcast(BF16)

                def trlo_fn(e, g=g, pv2=pv2, lo_stage=lo_stage):
                    last = None
                    for j in range(4):
                        k = g * 4 + j
                        last = e.transpose(pv2[:, j * 128:(j + 1) * 128],
                                           lo_stage[:, k * 128:(k + 1) * 128],
                                           self.id16[:, :])
                    return last
                em.op("pe", trlo_fn, reads=[r_lost, self.r_id16],
                      writes=[r_pb[bi2]], dbg=f"trlo{tt}_{g}")

                def evlo_fn(e, g=g, pv2=pv2, loT=loT):
                    dst = loT[:, :].rearrange("p (k t) -> p k t", k=KT)[
                        :, g * 4:(g + 1) * 4, :]
                    return e.tensor_copy(
                        dst, pv2[:, 0:512].rearrange("p (j c) -> p j c", j=4))
                em.op("dve", evlo_fn, reads=[r_pb[bi2]], writes=[r_loT],
                      dbg=f"evlo{tt}_{g}")

            def gmm_fn(e, tt=tt, loT=loT):
                w_hi = wg1hi[:, :].rearrange("p (k e) -> p k e", k=KT)
                w_lo = wg1lo[:, :].rearrange("p (k e) -> p k e", k=KT)
                lov = loT[:, :].rearrange("p (k t) -> p k t", k=KT)
                last = None
                for k in range(KT):
                    xcol = self.xnTb[:, :].rearrange("p (k t) -> p k t", k=KT)[
                        :, k, tt * 128:(tt + 1) * 128]
                    last = e.matmul(pb[4][0:8, 0:128], w_hi[:, k], xcol,
                                    start=(k == 0), stop=False)
                    last = e.matmul(pb[4][0:8, 0:128], w_lo[:, k], xcol,
                                    start=False, stop=False)
                    last = e.matmul(pb[4][0:8, 0:128], w_hi[:, k], lov[:, k],
                                    start=False, stop=(k == KT - 1))
                return last
            em.op("pe", gmm_fn,
                  reads=[r_wg1h, r_wg1l, r_loT] + self.r_xnT, writes=[r_pb[4]],
                  dbg=f"gmm{tt}")
            em.op("act", lambda e, tt=tt: e.activation(
                l1T[:, tt * 128:(tt + 1) * 128], pb[4][0:8, 0:128], AF.Tanh),
                reads=[r_pb[4]], writes=[r_l1], dbg=f"tanh{tt}")

        for j in range(4):
            em.op("pe", lambda e, j=j: e.matmul(
                pb[5][0:8, 0:512], wg2f[:, :], l1T[:, j * 512:(j + 1) * 512],
                start=True, stop=True),
                reads=[r_wg2, r_l1], writes=[r_pb[5]], dbg=f"g2_{j}")
            em.op("dve", lambda e, j=j: e.tensor_copy(
                l2T[:, j * 512:(j + 1) * 512], pb[5][0:8, 0:512]),
                reads=[r_pb[5]], writes=[r_l2], dbg=f"g2ev{j}")

        for tt in range(NTT):
            em.op("pe", lambda e, tt=tt: e.transpose(
                self.pbank[6][0:128, 0:8], l2T[0:8, tt * 128:(tt + 1) * 128],
                self.idf[0:8, 0:8]),
                reads=[r_l2, self.r_idf], writes=[r_pb[6]], dbg=f"trl2{tt}")
            em.op("dve", lambda e: e.tensor_copy(lgt[:, 0:8],
                                                 self.pbank[6][0:128, 0:8]),
                  reads=[r_pb[6]], writes=[r_lgt])
            em.op("dve", lambda e: e.reduce_max(tkb[:, 0:1], lgt[:, 0:8], AX.X),
                  reads=[r_lgt], writes=[r_tk])
            em.op("dve", lambda e: e.tensor_scalar(
                tkb[:, 8:16], lgt[:, 0:8], tkb[:, 0:1], None, ALU.is_ge),
                reads=[r_lgt, r_tk], writes=[r_tk], dbg="m1")
            em.op("dve", lambda e: e.scalar_tensor_tensor(
                tkb[:, 16:24], tkb[:, 8:16], -1e30, lgt[:, 0:8], ALU.mult,
                ALU.add), reads=[r_tk, r_lgt], writes=[r_tk], dbg="l2m")
            em.op("dve", lambda e: e.reduce_max(tkb[:, 1:2], tkb[:, 16:24], AX.X),
                  reads=[r_tk], writes=[r_tk])
            em.op("dve", lambda e: e.tensor_scalar(
                tkb[:, 24:32], tkb[:, 16:24], tkb[:, 1:2], None, ALU.is_ge),
                reads=[r_tk], writes=[r_tk], dbg="m2")
            em.op("dve", lambda e: e.tensor_tensor(
                tkb[:, 2:3], tkb[:, 0:1], tkb[:, 1:2], ALU.subtract),
                reads=[r_tk], writes=[r_tk])
            em.op("act", lambda e: e.activation(tkb[:, 3:4], tkb[:, 2:3],
                                                AF.Sigmoid),
                  reads=[r_tk], writes=[r_tk], dbg="sig")
            em.op("dve", lambda e: e.tensor_scalar(
                tkb[:, 4:5], tkb[:, 3:4], 1.0, -1.0, ALU.subtract, ALU.mult),
                reads=[r_tk], writes=[r_tk], dbg="p2")
            em.op("dve", lambda e: e.tensor_scalar(
                tkb[:, 8:16], tkb[:, 8:16], tkb[:, 3:4], None, ALU.mult),
                reads=[r_tk], writes=[r_tk])
            em.op("dve", lambda e: e.scalar_tensor_tensor(
                tkb[:, 32:40], tkb[:, 24:32], tkb[:, 4:5], tkb[:, 8:16],
                ALU.mult, ALU.add),
                reads=[r_tk], writes=[r_tk], dbg="scores")
            em.op("pe", lambda e: e.transpose(
                self.pbank[7][0:8, 0:128], tkb[:, 32:40], self.idf[:, :]),
                reads=[r_tk, self.r_idf], writes=[r_pb[7]], dbg=f"trsc{tt}")
            em.op("dve", lambda e, tt=tt: e.tensor_copy(
                self.scoT[0:8, tt * 128:(tt + 1) * 128],
                self.pbank[7][0:8, 0:128]),
                reads=[r_pb[7]], writes=[self.r_scoT])

        self.scoh_h = self.dint("scoh", [EXP, T], BF16)
        self.r_scoh = em.res("scoh")
        em.op("sp", lambda e: e.dma_start(self.scoh_h, self.scoT[:, :]),
              reads=[self.r_scoT], writes=[self.r_scoh], dma=True)

        if "xnT" in self.dbg:
            o = self.dout("dbg_xnT", [KT * 128, T], BF16)
            for k in range(KT):
                em.op("sp", lambda e, k=k, o=o: e.dma_start(
                    o[k * 128:(k + 1) * 128, :], self.xnT(k)),
                    reads=[self.r_xnT[k]], dma=True)
        if "scoT" in self.dbg:
            o = self.dout("dbg_scoT", [EXP, T], BF16)
            em.op("sp", lambda e, o=o: e.dma_start(o, self.scoT[:, :]),
                  reads=[self.r_scoT], dma=True)
        if "l2T" in self.dbg:
            o = self.dout("dbg_l2T", [EXP, T], F32)
            em.op("sp", lambda e, o=o: e.dma_start(o, l2T[:, :]),
                  reads=[r_l2], dma=True)
        self.close_scope(p0s)

    def stub_out(self):
        em = self.em
        import contextlib as _ctx
        sos = _ctx.ExitStack()
        x_tile = [self.sb(f"so{b}", [128, D], F32, sos) for b in range(2)]
        r_xtile = [em.res(f"so{b}") for b in range(2)]
        for tt in range(NTT):
            xb, r_xb = x_tile[tt % 2], r_xtile[tt % 2]
            em.op("sp", lambda e, tt=tt, xb=xb: e.dma_start(
                xb[:, :], self.x_h[tt * 128:(tt + 1) * 128, :]),
                writes=[r_xb], dma=True)
            em.op("sp", lambda e, tt=tt, xb=xb: e.dma_start(
                self.out_h[tt * 128:(tt + 1) * 128, :], xb[:, :]),
                reads=[r_xb], writes=[self.r_out], dma=True)
        self.close_scope(sos)

    def finish(self):
        st = self.em.finalize(self.nc, self.sems)
        self.stack.close()
        return st


def host_inputs(core_x, w):
    f32 = np.float32
    d = {
        "x": np.ascontiguousarray(core_x, dtype=f32),
        "winT": w["winT"], "woutT": w["woutT"], "wupT": w["wupT"],
        "wdT": w["wdT"], "wg1T": w["wg1T"], "wg2T": w["wg2T"],
        "convw": w["convw"], "convb": w["convb"], "dtb": w["dtb"],
        "alog": w["alog"], "drep": w["drep"], "nw": w["nw"],
        "idf": w["idf"], "tril": w["tril"], "onesrow": w["onesrow"],
        "bupc": w["bupc"], "bdn": w["bdn"], "negHE": w["negHE"],
        "one3": w["one3"],
    }
    return d


def shared_weights(inputs):
    f32 = np.float32
    asc = np.ascontiguousarray
    w = {}
    w["winT"] = asc(np.asarray(inputs["L0_attn_in_proj_w"], f32).T)
    w["woutT"] = asc(np.asarray(inputs["L0_attn_out_proj_w"], f32).T)
    wup = np.asarray(inputs["L0_ffn_Wup"], f32)
    w["wupT"] = asc(wup.transpose(2, 0, 1).reshape(D, EXP * HE))
    wdn = np.asarray(inputs["L0_ffn_Wdown"], f32)
    w["wdT"] = asc(wdn.transpose(0, 2, 1))
    w["wg1T"] = asc(np.asarray(inputs["L0_ffn_Wg1"], f32).T)
    w["wg2T"] = asc(np.asarray(inputs["L0_ffn_Wg2"], f32).T)
    cw = np.asarray(inputs["L0_attn_conv_w"], f32)[:, 0, :]
    w["convw"] = asc(cw.reshape(NBC // 128, 128, 4).transpose(1, 0, 2)
                     .reshape(128, (NBC // 128) * 4))
    cb = np.asarray(inputs["L0_attn_conv_b"], f32)
    w["convb"] = asc(cb.reshape(NBC // 128, 128).T)
    w["dtb"] = asc(np.asarray(inputs["L0_attn_dt_bias"], f32).reshape(NH, 1))
    w["alog"] = asc(np.asarray(inputs["L0_attn_A_log"], f32).reshape(NH, 1))
    w["drep"] = asc(np.repeat(np.asarray(inputs["L0_attn_D"], f32), PP)
                    .reshape(1, I4))
    nw = np.asarray(inputs["L0_attn_norm_w"], f32)
    w["nw"] = asc(nw.reshape(I4 // 128, 128).T)
    bup = np.asarray(inputs["L0_ffn_bup"], f32)
    w["bupc"] = asc(np.concatenate(
        [bup[e].reshape(8, 128).T for e in range(EXP)], axis=1))
    w["bdn"] = asc(np.asarray(inputs["L0_ffn_bdown"], f32))
    he = np.zeros((32, 32, 128), f32)
    for h in range(32):
        he[h, h, :] = -1.0
    w["negHE"] = asc(he.reshape(32, 32 * 128))
    o3 = np.zeros((32, 128), f32)
    o3[0:3, :] = 1.0
    w["one3"] = o3
    w["idf"] = np.eye(128, dtype=f32)
    w["tril"] = np.triu(np.ones((128, 128), f32))
    w["onesrow"] = np.ones((1, 128), f32)
    return w


def kernel(**inputs):
    kb = KB().build()
    st = kb.finish()
    w = shared_weights(inputs)
    x = np.asarray(inputs["L0_attn_hidden_states"], np.float32)
    in_maps = [host_inputs(x[i], w) for i in range(8)]
    res = run_bass_kernel_spmd(kb.nc, in_maps, core_ids=list(range(8)))
    out = np.stack([np.asarray(r["out"]) for r in res.results], axis=0)
    return out


# version 18
# speedup vs baseline: 1.2071x; 1.0178x over previous
import contextlib
import numpy as np
import ml_dtypes
import concourse.bass as bass
import concourse.mybir as mybir
from concourse.bass_utils import run_bass_kernel_spmd

F32 = mybir.dt.float32
BF16 = mybir.dt.bfloat16
AF = mybir.ActivationFunctionType
ALU = mybir.AluOpType
AX = mybir.AxisListType

T = 2048
D = 2048
FOUT = 8512
I4 = 4096
NBC = 4352
NH = 64
PP = 64
NS = 128
CH = 128
NCH = T // CH
EXP = 8
HE = 1024
KT = D // 128
NTT = T // 128
EPS_LN = 1e-5
EPS_RMS = 1e-5

ENGS = ("pe", "dve", "act", "pool", "sp")
NLANES = 8
QSEMS = tuple(ENGS) + tuple(
    f"{e}q{i}" for e in ("pool", "sp", "act") for i in range(NLANES))


class Res:
    __slots__ = ("name", "writer", "readers")

    def __init__(self, name):
        self.name = name
        self.writer = None
        self.readers = []


class Op:
    __slots__ = ("fn", "eng", "qsem", "inc_amt", "waits", "need_inc",
                 "sem_val_after", "dbg")

    def __init__(self, fn, eng, qsem, inc_amt, dbg):
        self.fn = fn
        self.eng = eng
        self.qsem = qsem
        self.inc_amt = inc_amt
        self.waits = []
        self.need_inc = False
        self.sem_val_after = 0
        self.dbg = dbg


class EM:

    def __init__(self):
        self.streams = {e: [] for e in ENGS}
        self.nres = 0
        self.dma_count = {e: 0 for e in ("pool", "sp", "act")}
        self.dma_lane_prev = {e: [None] * NLANES for e in ("pool", "sp", "act")}

    def res(self, name=None):
        self.nres += 1
        return Res(name or f"r{self.nres}")

    def op(self, eng, fn, reads=(), writes=(), dma=False, dbg=""):
        lane_prev = None
        if dma:
            j = self.dma_count[eng]
            self.dma_count[eng] = j + 1
            lane = j % NLANES
            qsem = f"{eng}q{lane}"
            lane_prev = self.dma_lane_prev[eng][lane]
        else:
            qsem = eng
        o = Op(fn, eng, qsem, 16 if dma else 1, dbg)
        if dma:
            self.dma_lane_prev[eng][lane] = o
        tgt = []
        if lane_prev is not None:
            tgt.append(lane_prev)
        for r in reads:
            if r.writer is not None:
                tgt.append(r.writer)
        for r in writes:
            if r.writer is not None:
                tgt.append(r.writer)
            tgt.extend(r.readers)
        seen = set()
        for t in tgt:
            if t is o or id(t) in seen:
                continue
            seen.add(id(t))
            if o.qsem == "pe" and t.qsem == "pe":
                continue
            o.waits.append(t)
        for r in reads:
            r.readers.append(o)
        for r in writes:
            r.writer = o
            r.readers = []
        self.streams[eng].append(o)
        return o

    def barrier(self):
        lasts = []
        for s in self.streams.values():
            for t in reversed(s):
                if t.fn is not None and not t.qsem[-1].isdigit():
                    lasts.append(t)
                    break
        for e in ("pool", "sp", "act"):
            for prev in self.dma_lane_prev[e]:
                if prev is not None:
                    lasts.append(prev)
        for eng in ENGS:
            o = Op(None, eng, eng, 1, "barrier")
            o.waits.extend(lasts)
            self.streams[eng].append(o)

    def finalize(self, nc, sems):
        for s in self.streams.values():
            for o in s:
                if o.qsem.endswith("q"):
                    o.need_inc = True
                for t in o.waits:
                    t.need_inc = True
        cum = {q: 0 for q in QSEMS}
        for eng in ENGS:
            for o in self.streams[eng]:
                if o.need_inc:
                    cum[o.qsem] += o.inc_amt
                o.sem_val_after = cum[o.qsem]
        knows = {e: {q: 0 for q in QSEMS} for e in ENGS}
        stats = {"waits": 0, "incs": 0, "ops": 0}

        def emit_stream(engname):
            def run(eh):
                kn = knows[engname]
                for o in self.streams[engname]:
                    stats["ops"] += 1
                    need = {}
                    for t in o.waits:
                        v = t.sem_val_after
                        if v > need.get(t.qsem, 0):
                            need[t.qsem] = v
                    for q in QSEMS:
                        if q in need and kn[q] < need[q]:
                            eh.wait_ge(sems[q], need[q])
                            kn[q] = need[q]
                            stats["waits"] += 1
                    if o.fn is not None:
                        inst = o.fn(eh)
                        if o.need_inc:
                            assert inst is not None, (o.eng, o.dbg)
                            inst.then_inc(sems[o.qsem], o.inc_amt)
                            stats["incs"] += 1
            return run

        with nc.Block() as block:
            block.tensor(emit_stream("pe"))
            block.vector(emit_stream("dve"))
            block.scalar(emit_stream("act"))
            block.gpsimd(emit_stream("pool"))
            block.sync(emit_stream("sp"))
        return stats


class KB:

    def __init__(self, dbg=()):
        self.dbg = set(dbg)
        self.nc = bass.Bass("TRN2", target_bir_lowering=False)
        self.em = EM()
        self.stack = contextlib.ExitStack()
        self.sems = {}
        for q in QSEMS:
            self.sems[q] = self.stack.enter_context(self.nc.semaphore("sem_" + q))
        self.dr = {}

    def din(self, name, shape, dt=F32):
        ap = self.nc.dram_tensor(name, list(shape), dt, kind="ExternalInput").ap()
        self.dr[name] = ap
        return ap

    def dout(self, name, shape, dt=F32):
        ap = self.nc.dram_tensor(name, list(shape), dt, kind="ExternalOutput").ap()
        self.dr[name] = ap
        return ap

    def dint(self, name, shape, dt):
        ap = self.nc.dram_tensor(name, list(shape), dt, kind="Internal").ap()
        self.dr[name] = ap
        return ap

    def sb(self, name, shape, dt, scope=None, side=None):
        st = scope if scope is not None else self.stack
        if side is None:
            return st.enter_context(self.nc.sbuf_tensor(name, list(shape), dt))
        return st.enter_context(
            self.nc.sbuf_tensor(name, list(shape), dt, side=side))

    def close_scope(self, scope):
        self.em.barrier()
        scope.close()

    def ps(self, name, shape, dt=F32):
        return self.stack.enter_context(self.nc.psum_tensor(name, list(shape), dt))

    def build(self):
        em = self.em
        self.x_h = self.din("x", [T, D])
        self.winT_h = self.din("winT", [D, FOUT])
        self.woutT_h = self.din("woutT", [I4, D])
        self.wupT_h = self.din("wupT", [D, EXP * HE])
        self.wdT_h = self.din("wdT", [EXP, HE, D])
        self.wg1T_h = self.din("wg1T", [D, EXP])
        self.wg2T_h = self.din("wg2T", [EXP, EXP])
        self.convw_h = self.din("convw", [128, (NBC // 128) * 4])
        self.convb_h = self.din("convb", [128, NBC // 128])
        self.dtb_h = self.din("dtb", [NH, 1])
        self.alog_h = self.din("alog", [NH, 1])
        self.drep_h = self.din("drep", [1, I4])
        self.nw_h = self.din("nw", [128, I4 // 128])
        self.idf_h = self.din("idf", [128, 128])
        self.tril_h = self.din("tril", [128, 128])
        self.ones_h = self.din("onesrow", [1, 128])
        self.out_h = self.dout("out", [T, D], F32)
        self.zs_h = self.dint("zs", [T, I4], BF16)
        self.xts_h = self.dint("xts", [NCH, I4, CH], BF16)
        self.ffn_h = self.dint("ffn", [T, D], BF16)
        self.ygt_h = self.dint("ygt", [NCH, I4, CH], BF16)
        self.acr_h = self.dint("acr", [NCH, 3, NH * CH], BF16)

        self.r_out = em.res("out")
        self.r_zs = [em.res(f"zs{t}") for t in range(NTT)]
        self.r_xts = [em.res(f"xts{c}") for c in range(NCH)]
        self.r_ffn = [em.res(f"ffn{t}") for t in range(NTT)]
        self.r_ygt = [em.res(f"ygt{c}") for c in range(NCH)]
        self.r_acr = [em.res(f"acr{c}") for c in range(NCH)]

        self.idf = self.sb("idf_s", [128, 128], F32)
        self.id16 = self.sb("id16", [128, 128], BF16)
        self.tril = self.sb("tril_s", [128, 128], F32)
        self.ones1 = self.sb("ones1", [1, 128], F32)
        self.ones16 = self.sb("ones16", [1, 128], BF16)
        self.r_idf = em.res("idf")
        self.r_id16 = em.res("id16")
        self.r_tril = em.res("tril")
        self.r_ones1 = em.res("ones1")
        self.r_ones16 = em.res("ones16")
        em.op("sp", lambda e: e.dma_start(self.idf[:, :], self.idf_h),
              writes=[self.r_idf], dma=True)
        em.op("pool", lambda e: e.dma_start(self.id16[:, :], self.idf_h),
              writes=[self.r_id16], dma=True)
        em.op("sp", lambda e: e.dma_start(self.tril[:, :], self.tril_h),
              writes=[self.r_tril], dma=True)
        em.op("sp", lambda e: e.dma_start(self.ones1[:, :], self.ones_h),
              writes=[self.r_ones1], dma=True)
        em.op("pool", lambda e: e.dma_start(self.ones16[:, :], self.ones_h),
              writes=[self.r_ones16], dma=True)

        self.psumAll = self.ps("psumAll", [128, 4096], F32)
        self.pbank = [self.psumAll[:, i * 512:(i + 1) * 512] for i in range(8)]
        self.r_pb = [em.res(f"pb{i}") for i in range(8)]
        self.pview = self.psumAll

        self.cconst = self.sb("cconst", [128, 8], F32)
        self.r_cconst = em.res("cconst")
        self._cmap = {}
        return self.build_phases()

    def constcol(self, val):
        if val not in self._cmap:
            i = len(self._cmap)
            assert i < 8
            self._cmap[val] = i
            self.em.op("dve", lambda e, i=i, val=val: e.memset(
                self.cconst[:, i:i + 1], float(val)),
                writes=[self.r_cconst], dbg=f"const{val}")
        i = self._cmap[val]
        return self.cconst[:, i:i + 1]

    def build_phases(self):
        import contextlib as _ctx
        self.xns = _ctx.ExitStack()
        self.rscope = _ctx.ExitStack()
        self.phase0()
        if "skipF" not in self.dbg:
            self.phaseF()
        self.phaseA()
        self.close_scope(self.xns)
        if "stopA" in self.dbg:
            self.stub_out()
            self.em.barrier()
            return self
        self.phaseB()
        self.close_scope(self.rscope)
        if "stopB" in self.dbg:
            self.stub_out()
            self.em.barrier()
            return self
        self.phaseD()
        self.em.barrier()
        return self

    def phaseA(self):
        em = self.em
        import contextlib as _ctx
        pas = _ctx.ExitStack()
        pb, r_pb = self.pbank, self.r_pb

        self.bct = self.sb("bct", [128, 2 * T], BF16, self.rscope, side="right")
        self.r_bct = [em.res("bT"), em.res("cT")]
        self.aT = self.sb("aT", [NH, T], F32, self.rscope, side="right")
        self.r_aT = em.res("aT")
        self.dt_tok = self.sb("dt_tok", [128, NCH * NH], BF16, self.rscope,
                              side="right")
        self.r_dtok = em.res("dt_tok")

        wz = [self.sb(f"wz{b}", [128, KT * 512], BF16, pas) for b in range(2)]
        r_wz = [em.res(f"wz{b}") for b in range(2)]
        wx = [self.sb(f"wx{b}", [128, KT * 128], BF16, pas) for b in range(2)]
        r_wx = [em.res(f"wx{b}") for b in range(2)]
        z16 = [self.sb(f"z16_{b}", [128, 512], BF16, pas) for b in range(2)]
        r_z16 = [em.res(f"z16{b}") for b in range(2)]
        cacc = [self.sb(f"cacc{b}", [128, T], F32, pas) for b in range(2)]
        r_cacc = [em.res(f"cacc{b}") for b in range(2)]
        xt16 = [self.sb(f"xt16_{b}", [128, T], BF16, pas) for b in range(2)]
        r_xt16 = [em.res(f"xt16{b}") for b in range(2)]
        convw = self.sb("convw_s", [128, 34 * 4], F32, pas)
        convb = self.sb("convb_s", [128, 34], F32, pas)
        r_cw, r_cb = em.res(), em.res()
        dtr = self.sb("dtr", [NH, T], F32, pas)
        dtt = self.sb("dtt", [NH, T], F32, pas)
        dtv = self.sb("dtv", [NH, T], F32, pas)
        r_dtr, r_dtt, r_dtv = em.res(), em.res(), em.res()
        acol = self.sb("acol", [NH, 2], F32, pas)
        r_acol = em.res()

        em.op("sp", lambda e: e.dma_start(convw[:, :], self.convw_h),
              writes=[r_cw], dma=True)
        em.op("sp", lambda e: e.dma_start(convb[:, :], self.convb_h),
              writes=[r_cb], dma=True)

        for fs in range(8):
            wb, r_wb = wz[fs % 2], r_wz[fs % 2]
            em.op("pool", lambda e, fs=fs, wb=wb: e.dma_start(
                wb[:, :].rearrange("p (k c) -> p k c", k=KT),
                self.winT_h.rearrange("(k p) f -> p k f", k=KT)[
                    :, :, fs * 512:(fs + 1) * 512]),
                writes=[r_wb], dma=True, dbg=f"wz{fs}")
            for tt in range(NTT):
                bi = tt % 2
                def zmm(e, wb=wb, tt=tt, bi=bi):
                    last = None
                    for k in range(KT):
                        last = e.matmul(
                            pb[bi][:, :],
                            self.xnTb[:, :].rearrange("p (k t) -> p k t", k=KT)[
                                :, k, tt * 128:(tt + 1) * 128],
                            wb[:, :].rearrange("p (k c) -> p k c", k=KT)[:, k],
                            start=(k == 0), stop=(k == KT - 1))
                    return last
                em.op("pe", zmm, reads=[r_wb] + self.r_xnT, writes=[r_pb[bi]],
                      dbg=f"zmm{fs}_{tt}")
                zb, r_zb = z16[tt % 2], r_z16[tt % 2]
                em.op("act", lambda e, bi=bi, zb=zb: e.activation(
                    zb[:, :], pb[bi][:, :], AF.Silu),
                    reads=[r_pb[bi]], writes=[r_zb], dbg=f"zsilu{fs}_{tt}")
                em.op("sp", lambda e, fs=fs, tt=tt, zb=zb: e.dma_start(
                    self.zs_h[tt * 128:(tt + 1) * 128,
                              fs * 512:(fs + 1) * 512], zb[:, :]),
                    reads=[r_zb], writes=[self.r_zs[tt]], dma=True,
                    dbg=f"zst{fs}_{tt}")

        for j in range(34):
            wb, r_wb = wx[j % 2], r_wx[j % 2]
            f0 = I4 + j * 128
            em.op("pool", lambda e, f0=f0, wb=wb: e.dma_start(
                wb[:, :].rearrange("p (k c) -> p k c", k=KT),
                self.winT_h.rearrange("(k p) f -> p k f", k=KT)[
                    :, :, f0:f0 + 128]),
                writes=[r_wb], dma=True, dbg=f"wx{j}")
            qoff = (j % 2) * 4
            r_quad = [r_pb[qoff + n] for n in range(4)]
            for n in range(4):
                bi = qoff + n
                def xmm(e, wb=wb, n=n, bi=bi):
                    last = None
                    for k in range(KT):
                        last = e.matmul(
                            pb[bi][:, :],
                            wb[:, :].rearrange("p (k c) -> p k c", k=KT)[:, k],
                            self.xnTb[:, :].rearrange("p (k t) -> p k t", k=KT)[
                                :, k, n * 512:(n + 1) * 512],
                            start=(k == 0), stop=(k == KT - 1))
                    return last
                em.op("pe", xmm, reads=[r_wb] + self.r_xnT, writes=[r_pb[bi]],
                      dbg=f"xmm{j}_{n}")
            cb_, r_cc = cacc[j % 2], r_cacc[j % 2]
            pvw = self.pview[:, qoff * 512:(qoff + 4) * 512]
            em.op("act", lambda e, j=j, cb_=cb_, pvw=pvw: e.activation(
                cb_[:, :], pvw, AF.Copy, scale=convw[:, j * 4 + 3:j * 4 + 4]),
                reads=r_quad + [r_cw], writes=[r_cc], dbg=f"cv3_{j}")
            for kk in (2, 1, 0):
                sft = 3 - kk
                em.op("dve", lambda e, j=j, kk=kk, sft=sft, cb_=cb_, pvw=pvw:
                      e.scalar_tensor_tensor(
                          cb_[:, sft:T], pvw[:, 0:T - sft],
                          convw[:, j * 4 + kk:j * 4 + kk + 1],
                          cb_[:, sft:T], ALU.mult, ALU.add),
                      reads=r_quad + [r_cw, r_cc], writes=[r_cc],
                      dbg=f"cv{kk}_{j}")
            if j < 32:
                ob, r_ob = xt16[j % 2], r_xt16[j % 2]
                em.op("act", lambda e, j=j, cb_=cb_, ob=ob: e.activation(
                    ob[:, :], cb_[:, :], AF.Silu,
                    bias=convb[:, j:j + 1]),
                    reads=[r_cc, r_cb], writes=[r_ob], dbg=f"cvs{j}")
                em.op("sp", lambda e, j=j, ob=ob: e.dma_start(
                    self.xts_h[:, j * 128:(j + 1) * 128, :].transpose([1, 0, 2]),
                    ob[:, :].rearrange("p (c t) -> p c t", c=NCH)),
                    reads=[r_ob], writes=self.r_xts, dma=True, dbg=f"xst{j}")
            else:
                bi2 = j - 32
                em.op("act", lambda e, j=j, bi2=bi2, cb_=cb_: e.activation(
                    self.bct[:, bi2 * T:(bi2 + 1) * T], cb_[:, :], AF.Silu,
                    bias=convb[:, j:j + 1]),
                    reads=[r_cc, r_cb], writes=[self.r_bct[bi2]], dbg=f"cvs{j}")

        def dtmm(e, n=None):
            last = None
            for k in range(KT):
                last = e.matmul(
                    pb[6][0:NH, 0:512],
                    self.winT_h_dt[:, :].rearrange("p (k c) -> p k c", k=KT)[:, k],
                    self.xnTb[:, :].rearrange("p (k t) -> p k t", k=KT)[
                        :, k, n * 512:(n + 1) * 512],
                    start=(k == 0), stop=(k == KT - 1))
            return last
        wdt = self.sb("wdt", [128, KT * NH], BF16, pas)
        r_wdt = em.res()
        self.winT_h_dt = wdt
        em.op("pool", lambda e: e.dma_start(
            wdt[:, :].rearrange("p (k c) -> p k c", k=KT),
            self.winT_h.rearrange("(k p) f -> p k f", k=KT)[
                :, :, FOUT - NH:FOUT]),
            writes=[r_wdt], dma=True, dbg="wdt")
        for n in range(4):
            em.op("pe", lambda e, n=n: dtmm(e, n=n),
                  reads=[r_wdt] + self.r_xnT, writes=[r_pb[6]], dbg=f"dtmm{n}")
            em.op("dve", lambda e, n=n: e.tensor_copy(
                dtr[:, n * 512:(n + 1) * 512], pb[6][0:NH, 0:512]),
                reads=[r_pb[6]], writes=[r_dtr], dbg=f"dtev{n}")

        dtb = self.sb("dtb_s", [NH, 1], F32, pas)
        r_dtb = em.res()
        em.op("sp", lambda e: e.dma_start(dtb[:, :], self.dtb_h),
              writes=[r_dtb], dma=True)
        em.op("sp", lambda e: e.dma_start(acol[:, 0:1], self.alog_h),
              writes=[r_acol], dma=True)
        onec = self.constcol(1.0)
        rcc = self.r_cconst
        em.op("dve", lambda e: e.tensor_scalar(
            dtr[:, :], dtr[:, :], dtb[:, :], None, ALU.add),
            reads=[r_dtr, r_dtb], writes=[r_dtr])
        em.op("act", lambda e: e.activation(dtt[:, :], dtr[:, :], AF.Abs),
              reads=[r_dtr], writes=[r_dtt], dbg="abs")
        em.op("act", lambda e: e.activation(dtt[:, :], dtt[:, :], AF.Exp,
                                            scale=-1.0),
              reads=[r_dtt], writes=[r_dtt])
        em.op("act", lambda e: e.activation(dtt[:, :], dtt[:, :], AF.Ln,
                                            bias=onec[0:NH, :]),
              reads=[r_dtt, rcc], writes=[r_dtt])
        em.op("dve", lambda e: e.tensor_scalar(
            dtv[:, :], dtr[:, :], 0.0, None, ALU.max),
            reads=[r_dtr], writes=[r_dtv])
        em.op("dve", lambda e: e.tensor_tensor(
            dtv[:, :], dtv[:, :], dtt[:, :], ALU.add),
            reads=[r_dtv, r_dtt], writes=[r_dtv])
        em.op("dve", lambda e: e.tensor_scalar(
            dtv[:, :], dtv[:, :], 100.0, None, ALU.min),
            reads=[r_dtv], writes=[r_dtv], dbg="clip")
        em.op("act", lambda e: e.activation(acol[:, 1:2], acol[:, 0:1], AF.Exp),
              reads=[r_acol], writes=[r_acol])
        em.op("dve", lambda e: e.tensor_scalar(
            acol[:, 1:2], acol[:, 1:2], -1.0, None, ALU.mult),
            reads=[r_acol], writes=[r_acol])
        em.op("dve", lambda e: e.tensor_scalar(
            self.aT[:, :], dtv[:, :], acol[:, 1:2], None, ALU.mult),
            reads=[r_dtv, r_acol], writes=[self.r_aT], dbg="aT")
        for c in range(NCH):
            em.op("pe", lambda e, c=c: e.transpose(
                pb[7][0:128, 0:NH], dtv[:, c * CH:(c + 1) * CH],
                self.idf[0:NH, 0:NH]),
                reads=[r_dtv, self.r_idf], writes=[r_pb[7]], dbg=f"trdt{c}")
            em.op("dve", lambda e, c=c: e.tensor_copy(
                self.dt_tok[:, c * NH:(c + 1) * NH], pb[7][0:128, 0:NH]),
                reads=[r_pb[7]], writes=[self.r_dtok], dbg=f"dtokev{c}")

        if "zxbcdt" in self.dbg:
            oz = self.dout("dbg_z", [T, I4], BF16)
            for tt in range(NTT):
                em.op("sp", lambda e, tt=tt, oz=oz: e.dma_start(
                    oz[tt * 128:(tt + 1) * 128, :],
                    self.zs_h[tt * 128:(tt + 1) * 128, :]),
                    reads=[self.r_zs[tt]], dma=True)
            ox = self.dout("dbg_xts", [NCH, I4, CH], BF16)
            em.op("sp", lambda e, ox=ox: e.dma_start(ox, self.xts_h),
                  reads=self.r_xts, dma=True)
            ob = self.dout("dbg_bct", [128, 2 * T], BF16)
            em.op("sp", lambda e, ob=ob: e.dma_start(ob, self.bct[:, :]),
                  reads=self.r_bct, dma=True)
            od = self.dout("dbg_dtv", [NH, T], F32)
            em.op("sp", lambda e, od=od: e.dma_start(od, dtv[:, :]),
                  reads=[r_dtv], dma=True)
        self.close_scope(pas)

    def phaseB(self):
        em = self.em
        import contextlib as _ctx
        pbs = _ctx.ExitStack()
        pb, r_pb = self.pbank, self.r_pb
        epsc = self.constcol(EPS_RMS)
        rcc = self.r_cconst

        def dup(mk, n=2):
            return [mk(i) for i in range(n)]

        xtc = dup(lambda i: self.sb(f"xtc{i}", [128, 32 * CH], BF16, pbs))
        r_xtc = dup(lambda i: em.res(f"xtc{i}"))
        zc0 = self.sb("zc", [128, I4], BF16, pbs)
        zcb = [zc0, zc0]
        r_zc0 = em.res("zc")
        r_zcb = [r_zc0, r_zc0]
        Xcb = dup(lambda i: self.sb(f"Xc{i}", [128, I4], BF16, pbs))
        r_Xcb = dup(lambda i: em.res(f"Xc{i}"))
        Xdb = dup(lambda i: self.sb(f"Xd{i}", [128, I4], BF16, pbs))
        r_Xdb = dup(lambda i: em.res(f"Xd{i}"))
        XDs = self.sb("XD", [128, I4], BF16, pbs)
        XDb = [XDs, XDs]
        r_XDs = em.res("XD")
        r_XDb = [r_XDs, r_XDs]
        P16 = self.sb("P16", [128, I4], BF16, pbs)
        Pg16 = self.sb("Pg16", [128, I4], BF16, pbs)
        r_P16h = [em.res("P16a"), em.res("P16b")]
        r_Pgh = [em.res("Pga"), em.res("Pgb")]
        y32 = self.sb("y32", [128, I4], BF16, pbs)
        r_y32 = em.res()
        yg0 = self.sb("yg16", [128, I4], BF16, pbs)
        yg16b = [yg0, yg0]
        r_yg0 = em.res("yg0")
        r_ygb = [r_yg0, r_yg0]
        Dq0 = self.sb("Dq0", [128, 16 * CH], F32, pbs)
        Dq = [Dq0, Dq0]
        r_Dq0 = em.res("Dq0")
        r_Dq = [r_Dq0, r_Dq0]
        Mq = dup(lambda i: self.sb(f"Mq{i}", [128, 16 * CH], BF16, pbs), 8)
        r_Mq = dup(lambda i: em.res(f"Mq{i}"), 8)
        dl1 = dup(lambda i: self.sb(f"dl1_{i}", [128, CH], BF16, pbs), 4)
        r_dl1 = dup(lambda i: em.res(f"dl1_{i}"), 4)
        dr1 = dup(lambda i: self.sb(f"dr1_{i}", [128, 16 * CH], BF16, pbs), 4)
        r_dr1 = dup(lambda i: em.res(f"dr1_{i}"), 4)
        Gmb = dup(lambda i: self.sb(f"Gm{i}", [128, CH], BF16, pbs))
        r_Gmb = dup(lambda i: em.res(f"Gm{i}"))
        stkb = dup(lambda i: self.sb(f"stk{i}", [128, CH], F32, pbs))
        r_stkb = dup(lambda i: em.res(f"stk{i}"))
        atokb = dup(lambda i: self.sb(f"atok{i}", [128, NH], F32, pbs))
        dstokb = dup(lambda i: self.sb(f"dstok{i}", [128, NH], BF16, pbs))
        eatokb = dup(lambda i: self.sb(f"eatok{i}", [128, NH], F32, pbs))
        r_atokb = dup(lambda i: em.res(f"atok{i}"))
        r_dstokb = dup(lambda i: em.res(f"dst{i}"))
        r_eatokb = dup(lambda i: em.res(f"eat{i}"))
        dtdsb = dup(lambda i: self.sb(f"dtds{i}", [128, NH], BF16, pbs))
        r_dtdsb = dup(lambda i: em.res(f"dtds{i}"))
        gbcb = dup(lambda i: self.sb(f"gbc{i}", [128, NH], F32, pbs))
        r_gbcb = dup(lambda i: em.res(f"gbc{i}"))
        asplitb = dup(lambda i: self.sb(f"asplit{i}", [NH, 3 * CH], BF16, pbs))
        r_aspb = dup(lambda i: em.res(f"asplit{i}"))
        rres = self.sb("rres", [NH, CH], F32, pbs)
        r_rres = em.res("rres")
        onesf = self.sb("onesf", [NH, CH], F32, pbs)
        r_onesf = em.res()
        dfull = self.sb("dfull", [128, I4], BF16, pbs)
        r_dfull = em.res("dfull")
        ygs0 = self.sb("ygs0", [128, 1024], BF16, pbs)
        ygstage = [ygs0, ygs0]
        r_ygs0 = em.res("ygs0")
        r_ygst = [r_ygs0, r_ygs0]
        btokb = dup(lambda i: self.sb(f"btok{i}", [128, NS], BF16, pbs))
        r_btokb = dup(lambda i: em.res(f"btok{i}"))
        self.gscr_h = self.dint("gscr", [NCH, NH], F32)
        r_gscr = [em.res(f"g{c}") for c in range(NCH)]
        self.negHE_h = self.din("negHE", [32, 32 * CH])

        em.op("dve", lambda e: e.memset(onesf[:, :], 1.0), writes=[r_onesf])
        em.op("dve", lambda e: e.memset(P16[:, :], 0.0), writes=r_P16h)
        em.op("dve", lambda e: e.memset(Pg16[:, :], 0.0), writes=r_Pgh)
        self.one3_h = self.din("one3", [32, CH])
        for i in range(4):
            em.op("dve", lambda e, i=i: e.memset(dl1[i][:, :], 0.0),
                  writes=[r_dl1[i]])
            em.op("pool", lambda e, i=i: e.dma_start(
                dl1[i][96:128, :], self.one3_h), writes=[r_dl1[i]], dma=True)
        for i in range(4):
            qp = i % 2
            em.op("dve", lambda e, i=i: e.memset(dr1[i][:, :], 0.0),
                  writes=[r_dr1[i]])
            for base in (0, 32, 64):
                em.op("pool", lambda e, i=i, qp=qp, base=base: e.dma_start(
                    dr1[i][base:base + 32, :],
                    self.negHE_h[:, qp * 2048:(qp + 1) * 2048]),
                    writes=[r_dr1[i]], dma=True)
        em.op("pool", lambda e: e.dma_start(
            dfull[:, :], bass.AP(self.drep_h.tensor, 0, [[0, 128], [1, I4]])),
            writes=[r_dfull], dma=True, dbg="dfull")
        em.op("sp", lambda e: e.dma_start(
            xtc[0][:, :].rearrange("p (j t) -> p j t", j=32),
            self.xts_h[0].rearrange("(j p) t -> p j t", j=32)),
            reads=self.r_xts, writes=[r_xtc[0]], dma=True, dbg="ldx0")

        def stageA(c):
            P = c % 2
            stk, r_stk = stkb[P], r_stkb[P]
            atok, dstok, eatok = atokb[P], dstokb[P], eatokb[P]
            r_atok, r_dstok, r_eatok = r_atokb[P], r_dstokb[P], r_eatokb[P]
            asplit, r_asp = asplitb[P], r_aspb[P]
            if c + 1 < NCH:
                em.op("sp", lambda e, c=c, P=P: e.dma_start(
                    xtc[1 - P][:, :].rearrange("p (j t) -> p j t", j=32),
                    self.xts_h[c + 1].rearrange("(j p) t -> p j t", j=32)),
                    reads=self.r_xts, writes=[r_xtc[1 - P]], dma=True,
                    dbg=f"ldx{c+1}")
            em.op("dve", lambda e, c=c, stk=stk: e.tensor_tensor_scan(
                stk[0:NH, :], onesf[:, :], self.aT[:, c * CH:(c + 1) * CH],
                0.0, ALU.mult, ALU.add),
                reads=[r_onesf, self.r_aT], writes=[r_stk], dbg=f"cums{c}")
            em.op("dve", lambda e, stk=stk: e.tensor_scalar(
                stk[64:128, :], stk[0:NH, :], stk[0:NH, CH - 1:CH], None,
                ALU.subtract),
                reads=[r_stk], writes=[r_stk], dbg="dsT")
            em.op("dve", lambda e, stk=stk, asplit=asplit: e.tensor_copy(
                asplit[:, 0:CH], stk[0:NH, :]),
                reads=[r_stk], writes=[r_asp], dbg=f"a1_{c}")
            em.op("dve", lambda e, stk=stk, asplit=asplit: e.tensor_tensor(
                rres[:, :], stk[0:NH, :], asplit[:, 0:CH], ALU.subtract),
                reads=[r_stk, r_asp], writes=[r_rres], dbg=f"r2_{c}")
            em.op("dve", lambda e, asplit=asplit: e.tensor_copy(
                asplit[:, CH:2 * CH], rres[:, :]),
                reads=[r_rres], writes=[r_asp], dbg=f"a2_{c}")
            em.op("dve", lambda e, asplit=asplit: e.tensor_tensor(
                rres[:, :], rres[:, :], asplit[:, CH:2 * CH], ALU.subtract),
                reads=[r_rres, r_asp], writes=[r_rres], dbg=f"r3_{c}")
            em.op("dve", lambda e, asplit=asplit: e.tensor_copy(
                asplit[:, 2 * CH:3 * CH], rres[:, :]),
                reads=[r_rres], writes=[r_asp], dbg=f"a3_{c}")
            em.op("sp", lambda e, c=c, asplit=asplit: e.dma_start(
                self.acr_h[c].rearrange("i (h l) -> h i l", h=NH),
                asplit[:, :].rearrange("p (i l) -> p i l", i=3)),
                reads=[r_asp], writes=[self.r_acr[c]], dma=True, dbg=f"acr{c}")
            em.op("pe", lambda e, stk=stk: e.transpose(
                pb[0][:, 0:128], stk[:, :], self.idf[:, :]),
                reads=[r_stk, self.r_idf], writes=[r_pb[0]], dbg=f"trstk{c}")
            em.op("act", lambda e, atok=atok: e.activation(
                atok[:, :], pb[0][:, 0:NH], AF.Copy),
                reads=[r_pb[0]], writes=[r_atok], dbg="atok")
            em.op("act", lambda e, dstok=dstok: e.activation(
                dstok[:, :], pb[0][:, NH:128], AF.Exp, scale=-1.0),
                reads=[r_pb[0], r_atok], writes=[r_dstok], dbg="dstok")
            em.op("act", lambda e, eatok=eatok, atok=atok: e.activation(
                eatok[:, :], atok[:, :], AF.Exp),
                reads=[r_atok], writes=[r_eatok], dbg="eatok")
            em.op("dve", lambda e, c=c, P=P, dstok=dstok: e.tensor_tensor(
                dtdsb[P][:, :], self.dt_tok[:, c * NH:(c + 1) * NH],
                dstok[:, :], ALU.mult),
                reads=[self.r_dtok, r_dstok], writes=[r_dtdsb[P]],
                dbg=f"dtds{c}")
            em.op("pool", lambda e, c=c, stk=stk: e.dma_start(
                self.gscr_h[c:c + 1, :].rearrange("o h -> h o"),
                stk[0:NH, CH - 1:CH]),
                reads=[r_stk], writes=[r_gscr[c]], dma=True, dbg=f"gst{c}")
            em.op("pool", lambda e, c=c, P=P: e.dma_start(
                gbcb[P][:, :],
                bass.AP(self.gscr_h.tensor, c * NH, [[0, 128], [1, NH]])),
                reads=[r_gscr[c]], writes=[r_gbcb[P]], dma=True, dbg=f"gbc{c}")
            em.op("act", lambda e, P=P: e.activation(
                gbcb[P][:, :], gbcb[P][:, :], AF.Exp),
                reads=[r_gbcb[P]], writes=[r_gbcb[P]], dbg=f"gexp{c}")
            em.op("pe", lambda e, c=c: e.transpose(
                pb[0][:, 0:128].bitcast(BF16)[:, 0:128],
                self.bct[:, c * CH:(c + 1) * CH], self.id16[:, :]),
                reads=[self.r_bct[0], self.r_id16], writes=[r_pb[0]],
                dbg=f"trB{c}")
            em.op("act", lambda e, P=P: e.activation(
                btokb[P][:, :], pb[0][:, 0:128].bitcast(BF16)[:, 0:128],
                AF.Copy),
                reads=[r_pb[0]], writes=[r_btokb[P]], dbg="btok")
            em.op("pe", lambda e, c=c: e.matmul(
                pb[1][:, 0:128], self.bct[:, c * CH:(c + 1) * CH],
                self.bct[:, T + c * CH:T + (c + 1) * CH],
                start=True, stop=True),
                reads=self.r_bct, writes=[r_pb[1]], dbg=f"G{c}")
            em.op("dve", lambda e, P=P: e.tensor_tensor(
                Gmb[P][:, :], pb[1][:, 0:128], self.tril[:, :], ALU.mult),
                reads=[r_pb[1], self.r_tril], writes=[r_Gmb[P]], dbg="Gm")

        def stageC(c):
            P = c % 2
            xb, r_xb = xtc[P], r_xtc[P]
            Xc, r_Xc = Xcb[P], r_Xcb[P]
            Xd, r_Xd = Xdb[P], r_Xdb[P]
            XD, r_XD = XDb[P], r_XDb[P]
            asplit, r_asp = asplitb[P], r_aspb[P]
            dstok, r_dstok = dstokb[P], r_dstokb[P]
            for gg in range(4):
                bp = (gg % 2) * 2
                for sub in range(2):
                    g = gg * 2 + sub
                    pv = pb[bp + sub][:, :].bitcast(BF16)

                    def trx(e, g=g, pv=pv, xb=xb):
                        last = None
                        for j in range(4):
                            last = e.transpose(
                                pv[:, j * 128:(j + 1) * 128],
                                xb[:, (g * 4 + j) * CH:(g * 4 + j + 1) * CH],
                                self.id16[:, :])
                        return last
                    em.op("pe", trx, reads=[r_xb, self.r_id16],
                          writes=[r_pb[bp + sub]], dbg=f"trx{c}_{g}")
                pvb = self.pview[:, :].bitcast(BF16).rearrange(
                    "p (b x) -> p b x", b=8)[:, bp:bp + 2, 0:512]
                wide4 = pvb.rearrange("p b (h q) -> p b h q", h=8)
                sl = slice(gg * 1024, (gg + 1) * 1024)
                em.op("dve", lambda e, gg=gg, sl=sl, wide4=wide4, c=c, Xc=Xc:
                      e.tensor_tensor(
                      Xc[:, sl].rearrange("p (b h q) -> p b h q", b=2, h=8),
                      wide4,
                      self.dt_tok[:, c * NH + gg * 16:c * NH + (gg + 1) * 16]
                      .rearrange("p (b h) -> p b h", b=2)
                      .unsqueeze(3).broadcast_to([128, 2, 8, PP]),
                      ALU.mult),
                      reads=[r_pb[bp], r_pb[bp + 1], self.r_dtok],
                      writes=[r_Xc], dbg=f"xc{c}_{gg}")
                em.op("dve", lambda e, sl=sl, pvb=pvb, XD=XD: e.tensor_tensor(
                    XD[:, sl].rearrange("p (b x) -> p b x", b=2),
                    pvb, dfull[:, sl].rearrange("p (b x) -> p b x", b=2),
                    ALU.mult),
                    reads=[r_pb[bp], r_pb[bp + 1], r_dfull], writes=[r_XD],
                    dbg=f"xD{c}_{gg}")
                em.op("dve", lambda e, gg=gg, sl=sl, Xd=Xd, Xc=Xc, P=P:
                      e.tensor_tensor(
                      Xd[:, sl].rearrange("p (h q) -> p h q", h=16),
                      Xc[:, sl].rearrange("p (h q) -> p h q", h=16),
                      dstokb[P][:, gg * 16:(gg + 1) * 16]
                      .unsqueeze(2).broadcast_to([128, 16, PP]),
                      ALU.mult),
                      reads=[r_Xc, r_dstokb[P]], writes=[r_Xd],
                      dbg=f"xd{c}_{gg}")


        def decay_chunk(c):
            P = c % 2
            asplit, r_asp = asplitb[P], r_aspb[P]
            for q in range(4):
                half = q // 2
                l1, r_l1 = dl1[P * 2 + half], r_dl1[P * 2 + half]
                if q % 2 == 0:
                    h0 = half * 32
                    for i3, base in ((0, 0), (1, 32), (2, 64)):
                        em.op("dve", lambda e, h0=h0, l1=l1, i3=i3, base=base,
                              asplit=asplit: e.tensor_copy(
                              l1[base:base + 32, :],
                              asplit[h0:h0 + 32, i3 * CH:(i3 + 1) * CH]),
                              reads=[r_asp], writes=[r_l1],
                              dbg=f"dl{i3}_{c}_{half}")
                di = (c % 2) * 2 + (q % 2)
                rb1, r_rb1 = dr1[di], r_dr1[di]
                for i3 in range(3):
                    em.op("pool", lambda e, c=c, q=q, rb1=rb1, i3=i3:
                          e.dma_start(
                          rb1[96 + i3:97 + i3, :],
                          self.acr_h[c, i3, q * 16 * CH:(q + 1) * 16 * CH]
                          .unsqueeze(0)),
                          reads=[self.r_acr[c]], writes=[r_rb1], dma=True,
                          dbg=f"ar{i3}_{c}_{q}")
                db_, r_db = Dq[q % 2], r_Dq[q % 2]
                mb = Mq[P * 4 + q]
                r_mb = r_Mq[P * 4 + q]
                for dd in range(2):
                    for sub in range(2):
                        ssl = dd * 2 + sub
                        bk = 2 + sub
                        cs = slice(ssl * 512, (ssl + 1) * 512)

                        def dmm(e, cs=cs, bk=bk, l1=l1, rb1=rb1):
                            return e.matmul(pb[bk][:, :], l1[:, :],
                                            rb1[:, cs], start=True, stop=True)
                        em.op("pe", dmm,
                              reads=[r_l1, r_rb1],
                              writes=[r_pb[bk]], dbg=f"dmm{c}_{q}_{ssl}")
                    em.op("act", lambda e, dd=dd, db_=db_: e.activation(
                        db_[:, dd * 1024:(dd + 1) * 1024],
                        self.pview[:, 1024:2048], AF.Relu, scale=-1.0),
                        reads=[r_pb[2], r_pb[3]], writes=[r_db],
                        dbg=f"dev{c}_{q}_{dd}")
                em.op("act", lambda e, db_=db_, mb=mb: e.activation(
                    mb[:, :], db_[:, :], AF.Exp, scale=-1.0),
                    reads=[r_db], writes=[r_mb], dbg=f"E{c}_{q}")
                em.op("dve",
                      lambda e, mb=mb, P=P: e.tensor_tensor(
                      mb[:, :].rearrange("p (h l) -> p h l", h=16),
                      mb[:, :].rearrange("p (h l) -> p h l", h=16),
                      Gmb[P][:, :].unsqueeze(1).broadcast_to([128, 16, CH]),
                      ALU.mult),
                      reads=[r_mb, r_Gmb[P]], writes=[r_mb], dbg=f"M{c}_{q}")

        def stageB(c):
            P = c % 2
            Xc, r_Xc = Xcb[P], r_Xcb[P]
            XD, r_XD = XDb[P], r_XDb[P]
            eatok, r_eatok = eatokb[P], r_eatokb[P]
            pbank_h = [pb[4 + i] for i in range(4)]
            r_pbh = [r_pb[4 + i] for i in range(4)]
            em.op("sp", lambda e, c=c: e.dma_start(
                zcb[0][:, :], self.zs_h[c * CH:(c + 1) * CH, :]),
                reads=[self.r_zs[c]], writes=[r_zcb[0]], dma=True,
                dbg=f"ldz{c}")
            for half in range(2):
                def yoff(e, c=c, half=half, pbank_h=pbank_h):
                    last = None
                    for n in range(4):
                        last = e.matmul(
                            pbank_h[n][:, :],
                            self.bct[:, T + c * CH:T + (c + 1) * CH],
                            P16[:, half * 2048 + n * 512:
                                half * 2048 + (n + 1) * 512],
                            start=True, stop=True)
                    return last
                em.op("pe", yoff, reads=[self.r_bct[1], r_P16h[half]],
                      writes=r_pbh, dbg=f"yoff{c}_{half}")
                em.op("dve", lambda e, half=half, P=P: e.tensor_tensor(
                    P16[:, half * 2048:(half + 1) * 2048]
                    .rearrange("p (h q) -> p h q", h=32),
                    P16[:, half * 2048:(half + 1) * 2048]
                    .rearrange("p (h q) -> p h q", h=32),
                    gbcb[P][:, half * 32:(half + 1) * 32].unsqueeze(2)
                    .broadcast_to([128, 32, PP]),
                    ALU.mult),
                    reads=[r_P16h[half], r_gbcb[P]], writes=[r_P16h[half]],
                    dbg=f"Pg{c}_{half}")
                em.op("dve", lambda e, half=half, eatok=eatok:
                      e.tensor_tensor(
                      y32[:, half * 2048:(half + 1) * 2048]
                      .rearrange("p (h q) -> p h q", h=32),
                      self.pview[:, 2048:4096]
                      .rearrange("p (h q) -> p h q", h=32),
                      eatok[:, half * 32:(half + 1) * 32]
                      .unsqueeze(2).broadcast_to([128, 32, PP]),
                      ALU.mult),
                      reads=r_pbh + [r_eatok], writes=[r_y32],
                      dbg=f"ev1_{c}_{half}")

                def yd(e, c=c, half=half, pbank_h=pbank_h, Xc=Xc, XD=XD, P=P):
                    last = None
                    for n in range(4):
                        last = e.matmul(
                            pbank_h[n][:, :], self.id16[:, :],
                            XD[:, half * 2048 + n * 512:
                               half * 2048 + (n + 1) * 512],
                            start=True, stop=False)
                    for hh in range(32):
                        h = half * 32 + hh
                        qq, hq = divmod(hh, 16)
                        n, off = divmod(hh * PP, 512)
                        last = e.matmul(
                            pbank_h[n][:, off:off + PP],
                            Mq[P * 4 + half * 2 + qq][:, hq * CH:(hq + 1) * CH],
                            Xc[:, h * PP:(h + 1) * PP],
                            start=False, stop=(off + PP == 512))
                    return last
                em.op("pe", yd,
                      reads=[self.r_id16, r_XD, r_Xc,
                             r_Mq[P * 4 + half * 2], r_Mq[P * 4 + half * 2 + 1]],
                      writes=r_pbh, dbg=f"yd{c}_{half}")
                em.op("dve", lambda e, half=half: e.scalar_tensor_tensor(
                    y32[:, half * 2048:(half + 1) * 2048],
                    self.pview[:, 2048:4096], 1.0,
                    y32[:, half * 2048:(half + 1) * 2048],
                    ALU.mult, ALU.add),
                    reads=r_pbh + [r_y32], writes=[r_y32],
                    dbg=f"ev2_{c}_{half}")
                stk_, r_stk_ = stkb[P], r_stkb[P]
                em.op("act", lambda e, half=half, stk_=stk_, P=P: e.activation(
                    yg16b[P][:, half * 2048:(half + 1) * 2048],
                    y32[:, half * 2048:(half + 1) * 2048], AF.Square,
                    accum_out=stk_[0:128, 2 + half:3 + half]),
                    reads=[r_y32], writes=[r_ygb[P], r_stk_],
                    dbg=f"ssh{c}_{half}")

        def stageD(c):
            P = c % 2
            Xd, r_Xd = Xdb[P], r_Xdb[P]
            stk, r_stk = stkb[P], r_stkb[P]
            yg16, r_yg = yg16b[P], r_ygb[P]
            pbank_h = [pb[4 + i] for i in range(4)]
            r_pbh = [r_pb[4 + i] for i in range(4)]
            for half in range(2):
                def stmm(e, half=half, pbank_h=pbank_h, Xd=Xd, P=P):
                    last = None
                    for n in range(4):
                        last = e.matmul(
                            pbank_h[n][:, :], btokb[P][:, :],
                            Xd[:, half * 2048 + n * 512:
                               half * 2048 + (n + 1) * 512],
                            start=True, stop=True)
                    return last
                em.op("pe", stmm, reads=[r_btokb[P], r_Xd], writes=r_pbh,
                      dbg=f"st{c}_{half}")
                em.op("dve", lambda e, half=half: e.scalar_tensor_tensor(
                    P16[:, half * 2048:(half + 1) * 2048],
                    self.pview[:, 2048:4096], 1.0,
                    P16[:, half * 2048:(half + 1) * 2048],
                    ALU.mult, ALU.add),
                    reads=r_pbh + [r_P16h[half]], writes=[r_P16h[half]],
                    dbg=f"P16{c}_{half}")

            em.op("dve", lambda e, stk=stk: e.tensor_scalar(
                stk[:, 1:2], stk[:, 2:3], stk[:, 3:4], 1.0 / I4, ALU.add,
                ALU.mult),
                reads=[r_stk], writes=[r_stk], dbg=f"sscomb{c}")
            em.op("act", lambda e, stk=stk: e.activation(
                stk[:, 1:2], stk[:, 1:2], AF.Ln, bias=epsc),
                reads=[r_stk, rcc], writes=[r_stk])
            em.op("act", lambda e, stk=stk: e.activation(
                stk[:, 1:2], stk[:, 1:2], AF.Exp, scale=-0.5),
                reads=[r_stk], writes=[r_stk], dbg=f"rstd{c}")
            em.op("dve", lambda e, yg16=yg16, stk=stk, P=P:
                  e.scalar_tensor_tensor(
                  yg16[:, :], y32[:, :], stk[:, 1:2], zcb[P][:, :],
                  ALU.mult, ALU.mult),
                  reads=[r_y32, r_stk, r_zcb[P]], writes=[r_yg],
                  dbg=f"ygfuse{c}")
            for gg in range(4):
                for sub in range(2):
                    g = gg * 2 + sub
                    pv = pb[2 + sub][:, :].bitcast(BF16)

                    def tryg(e, g=g, pv=pv, yg16=yg16):
                        last = None
                        for j in range(4):
                            last = e.transpose(
                                pv[:, j * 128:(j + 1) * 128],
                                yg16[:, (g * 4 + j) * 128:
                                     (g * 4 + j + 1) * 128],
                                self.id16[:, :])
                        return last
                    em.op("pe", tryg, reads=[r_yg, self.r_id16],
                          writes=[r_pb[2 + sub]], dbg=f"tryg{c}_{g}")
                pvb = self.pview[:, :].bitcast(BF16).rearrange(
                    "p (b x) -> p b x", b=8)[:, 2:4, 0:512]
                sb_, r_sb = ygstage[gg % 2], r_ygst[gg % 2]
                em.op("act", lambda e, pvb=pvb, sb_=sb_: e.activation(
                    sb_[:, :].rearrange("p (b x) -> p b x", b=2), pvb,
                    AF.Copy),
                    reads=[r_pb[2], r_pb[3]], writes=[r_sb],
                    dbg=f"evyg{c}_{gg}")
                em.op("sp", lambda e, c=c, gg=gg, sb_=sb_: e.dma_start(
                    self.ygt_h[c, gg * 1024:(gg + 1) * 1024, :]
                    .rearrange("(j p) t -> p j t", j=8),
                    sb_[:, :].rearrange("p (j t) -> p j t", j=8)),
                    reads=[r_sb], writes=[self.r_ygt[c]], dma=True,
                    dbg=f"styg{c}_{gg}")

        nch = NCH
        for d in self.dbg:
            if d.startswith("nch"):
                nch = int(d[3:])
        stageA(0)
        stageC(0)
        decay_chunk(0)
        for k in range(1, nch):
            stageA(k)
            stageB(k - 1)
            stageC(k)
            decay_chunk(k)
            stageD(k - 1)
        stageB(nch - 1)
        stageD(nch - 1)

        if "yg" in self.dbg:
            o = self.dout("dbg_ygt", [NCH, I4, CH], BF16)
            em.op("sp", lambda e, o=o: e.dma_start(o, self.ygt_h),
                  reads=self.r_ygt, dma=True)
        self.close_scope(pbs)

    def phaseD(self):
        em = self.em
        import contextlib as _ctx
        pds = _ctx.ExitStack()
        pb, r_pb = self.pbank, self.r_pb

        wo16 = self.sb("wo16", [128, 32 * D], BF16, pds)
        r_wo = [em.res("wo16a"), em.res("wo16b")]
        nwcol = self.sb("nwcol", [128, 32], F32, pds)
        r_nw = em.res("nw")
        ygcol = [self.sb(f"ygc{b}", [128, 32 * 128], BF16, pds) for b in range(2)]
        r_ygc = [em.res(f"ygc{b}") for b in range(2)]
        x_t = [self.sb(f"xd{b}", [128, D], F32, pds) for b in range(2)]
        r_xt = [em.res(f"xd{b}") for b in range(2)]
        f_t = [self.sb(f"fd{b}", [128, D], BF16, pds) for b in range(2)]
        r_ft = [em.res(f"fd{b}") for b in range(2)]
        o_t = [self.sb(f"od{b}", [128, D], F32, pds) for b in range(2)]
        r_ot = [em.res(f"od{b}") for b in range(2)]

        em.op("sp", lambda e: e.dma_start(nwcol[:, :], self.nw_h),
              writes=[r_nw], dma=True)
        for dh in range(2):
            em.op("pool", lambda e, dh=dh: e.dma_start(
                wo16[:, :].rearrange("p (j x) -> p j x", j=32)[
                    :, :, dh * 1024:(dh + 1) * 1024],
                self.woutT_h.rearrange("(j p) d -> p j d", j=32)[
                    :, :, dh * 1024:(dh + 1) * 1024]),
                writes=[r_wo[dh]], dma=True, dbg=f"ldwo{dh}")
        for j in range(32):
            for dh in range(2):
                em.op("dve", lambda e, j=j, dh=dh: e.tensor_scalar(
                    wo16[:, j * D + dh * 1024:j * D + (dh + 1) * 1024],
                    wo16[:, j * D + dh * 1024:j * D + (dh + 1) * 1024],
                    nwcol[:, j:j + 1], None, ALU.mult),
                    reads=[r_wo[dh], r_nw], writes=[r_wo[dh]],
                    dbg=f"nwfold{j}_{dh}")

        em.op("sp", lambda e: e.dma_start(
            ygcol[0][:, :].rearrange("p (j t) -> p j t", j=32),
            self.ygt_h[0].rearrange("(j p) t -> p j t", j=32)),
            reads=[self.r_ygt[0]], writes=[r_ygc[0]], dma=True)
        em.op("sp", lambda e: e.dma_start(x_t[0][:, :], self.x_h[0:128, :]),
              writes=[r_xt[0]], dma=True)
        em.op("sp", lambda e: e.dma_start(f_t[0][:, :], self.ffn_h[0:128, :]),
              reads=[self.r_ffn[0]], writes=[r_ft[0]], dma=True)

        for tt in range(NTT):
            bi = tt % 2
            yb, r_yb = ygcol[bi], r_ygc[bi]
            xb, r_xb = x_t[bi], r_xt[bi]
            fb, r_fb = f_t[bi], r_ft[bi]
            ob, r_ob = o_t[bi], r_ot[bi]
            if tt + 1 < NTT:
                em.op("sp", lambda e, tt=tt, bi=bi: e.dma_start(
                    ygcol[1 - bi][:, :].rearrange("p (j t) -> p j t", j=32),
                    self.ygt_h[tt + 1].rearrange("(j p) t -> p j t", j=32)),
                    reads=[self.r_ygt[tt + 1]], writes=[r_ygc[1 - bi]],
                    dma=True)
                em.op("sp", lambda e, tt=tt, bi=bi: e.dma_start(
                    x_t[1 - bi][:, :],
                    self.x_h[(tt + 1) * 128:(tt + 2) * 128, :]),
                    writes=[r_xt[1 - bi]], dma=True)
                em.op("sp", lambda e, tt=tt, bi=bi: e.dma_start(
                    f_t[1 - bi][:, :],
                    self.ffn_h[(tt + 1) * 128:(tt + 2) * 128, :]),
                    reads=[self.r_ffn[tt + 1]], writes=[r_ft[1 - bi]],
                    dma=True)
            for dsl in range(4):
                bk = dsl
                dh = dsl // 2
                def omm(e, yb=yb, dsl=dsl, bk=bk):
                    last = None
                    for j in range(32):
                        last = e.matmul(
                            pb[bk][:, :],
                            yb[:, j * 128:(j + 1) * 128],
                            wo16[:, :].rearrange("p (j d) -> p j d", j=32)[
                                :, j, dsl * 512:(dsl + 1) * 512],
                            start=(j == 0), stop=(j == 31))
                    return last
                em.op("pe", omm, reads=[r_yb, r_wo[dh]], writes=[r_pb[bk]],
                      dbg=f"omm{tt}_{dsl}")
                sl = slice(dsl * 512, (dsl + 1) * 512)
                em.op("dve", lambda e, sl=sl, bk=bk, ob=ob, xb=xb:
                      e.scalar_tensor_tensor(
                          ob[:, sl], pb[bk][:, :], 1.0, xb[:, sl],
                          ALU.mult, ALU.add),
                      reads=[r_pb[bk], r_xb], writes=[r_ob],
                      dbg=f"oev{tt}_{dsl}")
            em.op("dve", lambda e, ob=ob, fb=fb: e.tensor_tensor(
                ob[:, :], ob[:, :], fb[:, :], ALU.add),
                reads=[r_ob, r_fb], writes=[r_ob], dbg=f"offn{tt}")
            em.op("sp", lambda e, tt=tt, ob=ob: e.dma_start(
                self.out_h[tt * 128:(tt + 1) * 128, :], ob[:, :]),
                reads=[r_ob], writes=[self.r_out], dma=True, dbg=f"out{tt}")
        self.close_scope(pds)

    def phaseF(self):
        em = self.em
        import contextlib as _ctx
        pfs = _ctx.ExitStack()
        pb, r_pb = self.pbank, self.r_pb
        HT = T // 2
        NTH = NTT // 2

        acc16 = self.sb("acc16", [128, NTH * D], BF16, pfs)
        r_acc = [em.res(f"acc{t}") for t in range(NTH)]
        h16b = [self.sb(f"h16_{b}", [128, 8 * HT], BF16, pfs) for b in range(2)]
        r_h16b = [[em.res(f"h16_{b}_{j}") for j in range(8)] for b in range(2)]
        srowb = [self.sb(f"srow{b}", [128, HT], BF16, pfs) for b in range(2)]
        r_srowb = [em.res(f"srow{b}") for b in range(2)]
        wd16 = self.sb("wd16", [128, 8 * D], BF16, pfs)
        r_wd = em.res("wd16")
        wupb = [self.sb(f"wupb{b}", [128, KT * 128], BF16, pfs) for b in range(2)]
        r_wupb = [em.res(f"wupb{b}") for b in range(2)]
        bupc = self.sb("bupc_s", [128, EXP * 8], F32, pfs)
        r_bup = em.res("bupc")
        bdn16 = self.sb("bdn16", [EXP, D], BF16, pfs)
        r_bdn = em.res("bdn16")

        self.bupc_h = self.din("bupc", [128, EXP * 8])
        self.bdn_h = self.din("bdn", [EXP, D])
        em.op("sp", lambda e: e.dma_start(bupc[:, :], self.bupc_h),
              writes=[r_bup], dma=True)
        em.op("pool", lambda e: e.dma_start(bdn16[:, :], self.bdn_h),
              writes=[r_bdn], dma=True)

        for th in range(2):
            t_base = th * HT
            for e_i in range(EXP):
                h16 = h16b[e_i % 2]
                r_h16 = r_h16b[e_i % 2]
                srow, r_srow = srowb[e_i % 2], r_srowb[e_i % 2]
                em.op("sp", lambda e, e_i=e_i, t_base=t_base, srow=srow:
                    e.dma_start(
                    srow[:, :],
                    bass.AP(self.scoh_h.tensor, e_i * T + t_base,
                            [[0, 128], [1, HT]])),
                    reads=[self.r_scoh], writes=[r_srow], dma=True,
                    dbg=f"srow{th}_{e_i}")
                for ht in range(8):
                    wb, r_wb = wupb[ht % 2], r_wupb[ht % 2]
                    c0 = e_i * HE + ht * 128
                    em.op("pool", lambda e, c0=c0, wb=wb: e.dma_start(
                        wb[:, :].rearrange("p (k c) -> p k c", k=KT),
                        self.wupT_h.rearrange("(k p) c -> p k c", k=KT)[
                            :, :, c0:c0 + 128]),
                        writes=[r_wb], dma=True, dbg=f"wup{th}_{e_i}_{ht}")
                    if ht == 2:
                        em.op("pool", lambda e, e_i=e_i: e.dma_start(
                            wd16[:, :].rearrange("p (j d) -> p j d", j=8),
                            self.wdT_h[e_i].rearrange("(j p) d -> p j d", j=8)),
                            writes=[r_wd], dma=True, dbg=f"wd{th}_{e_i}")
                    for n in range(2):
                        bi = 6 + n % 2
                        def up_fn(e, wb=wb, n=n, bi=bi, t_base=t_base):
                            last = None
                            for k in range(KT):
                                last = e.matmul(
                                    pb[bi][:, :],
                                    wb[:, :].rearrange(
                                        "p (k c) -> p k c", k=KT)[:, k],
                                    self.xnTb[:, :].rearrange(
                                        "p (k t) -> p k t", k=KT)[
                                        :, k,
                                        t_base + n * 512:t_base + (n + 1) * 512],
                                    start=(k == 0), stop=(k == KT - 1))
                            return last
                        em.op("pe", up_fn, reads=[r_wb] + self.r_xnT,
                              writes=[r_pb[bi]], dbg=f"up{th}_{e_i}_{ht}_{n}")
                        em.op("act", lambda e, ht=ht, n=n, e_i=e_i, bi=bi,
                              h16=h16: e.activation(
                                  h16[:, ht * HT + n * 512:
                                      ht * HT + (n + 1) * 512],
                                  pb[bi][:, :], AF.Silu,
                                  bias=bupc[:, e_i * 8 + ht:e_i * 8 + ht + 1]),
                              reads=[r_pb[bi], r_bup], writes=[r_h16[ht]],
                              dbg=f"silu{th}_{e_i}_{ht}_{n}")
                    em.op("dve", lambda e, ht=ht, h16=h16, srow=srow:
                        e.tensor_tensor(
                        h16[:, ht * HT:(ht + 1) * HT],
                        h16[:, ht * HT:(ht + 1) * HT],
                        srow[:, :], ALU.mult),
                        reads=[r_h16[ht], r_srow], writes=[r_h16[ht]],
                        dbg=f"hsc{th}_{e_i}_{ht}")

                for lt in range(NTH):
                    tt = th * NTH + lt
                    for pr in range(2):
                        for sub in range(2):
                            dsl = pr * 2 + sub
                            bi = 2 + pr * 2 + sub
                            def dn_fn(e, lt=lt, tt=tt, dsl=dsl, e_i=e_i,
                                      bi=bi, h16=h16):
                                last = None
                                if e_i == 0:
                                    last = e.matmul(
                                        pb[bi][:, :],
                                        self.scoT[0:8, tt * 128:(tt + 1) * 128],
                                        bdn16[0:8, dsl * 512:(dsl + 1) * 512],
                                        start=True, stop=False)
                                for j in range(8):
                                    last = e.matmul(
                                        pb[bi][:, :],
                                        h16[:, j * HT + lt * 128:
                                            j * HT + (lt + 1) * 128],
                                        wd16[:, :].rearrange(
                                            "p (j d) -> p j d", j=8)[
                                            :, j, dsl * 512:(dsl + 1) * 512],
                                        start=(e_i != 0 and j == 0),
                                        stop=(j == 7))
                                return last
                            em.op("pe", dn_fn,
                                  reads=[r_wd, self.r_scoT, r_bdn] + r_h16,
                                  writes=[r_pb[bi]],
                                  dbg=f"dn{th}_{e_i}_{lt}_{dsl}")
                        dst = acc16[:, lt * D + pr * 1024:
                                    lt * D + (pr + 1) * 1024]
                        rpair = [r_pb[2 + pr * 2], r_pb[3 + pr * 2]]
                        src = self.pview[:, 1024 + pr * 1024:
                                         2048 + pr * 1024]
                        if e_i == 0:
                            em.op("dve", lambda e, dst=dst, src=src:
                                  e.tensor_copy(dst, src),
                                  reads=rpair, writes=[r_acc[lt]],
                                  dbg=f"ev0_{th}_{lt}_{pr}")
                        else:
                            em.op("dve", lambda e, dst=dst, src=src:
                                  e.scalar_tensor_tensor(
                                      dst, src, 1.0, dst, ALU.mult, ALU.add),
                                  reads=rpair + [r_acc[lt]],
                                  writes=[r_acc[lt]],
                                  dbg=f"ev{e_i}_{th}_{lt}_{pr}")

            for lt in range(NTH):
                tt = th * NTH + lt
                em.op("sp", lambda e, lt=lt, tt=tt: e.dma_start(
                    self.ffn_h[tt * 128:(tt + 1) * 128, :],
                    acc16[:, lt * D:(lt + 1) * D]),
                    reads=[r_acc[lt]], writes=[self.r_ffn[tt]], dma=True,
                    dbg=f"stffn{tt}")

        if "ffn" in self.dbg:
            o = self.dout("dbg_ffn", [T, D], BF16)
            for tt in range(NTT):
                em.op("sp", lambda e, tt=tt, o=o: e.dma_start(
                    o[tt * 128:(tt + 1) * 128, :],
                    self.ffn_h[tt * 128:(tt + 1) * 128, :]),
                    reads=[self.r_ffn[tt]], dma=True)
        self.close_scope(pfs)

    def phase0(self):
        em = self.em
        import contextlib as _ctx
        p0s = _ctx.ExitStack()
        self.xnTb = self.sb("xnTb", [128, KT * T], BF16, self.xns)
        self.r_xnT = [em.res(f"xnT{k}") for k in range(KT)]
        self.scoT = self.sb("scoT", [EXP, T], BF16, self.xns)
        self.r_scoT = em.res("scoT")

        def xnT(k):
            return self.xnTb[:, k * T:(k + 1) * T]

        self.xnT = xnT

        x_tile = [self.sb(f"xt{b}", [128, D], F32, p0s) for b in range(2)]
        r_xtile = [em.res(f"xt{b}") for b in range(2)]
        sq_scr = self.sb("sqscr", [128, D], BF16, p0s)
        r_sq = em.res("sq")
        statsb = [self.sb(f"stats{b}", [128, 8], F32, p0s) for b in range(2)]
        r_statsb = [em.res(f"stats{b}") for b in range(2)]
        xn_stageb = [self.sb(f"xn_stage{b}", [128, D], BF16, p0s)
                     for b in range(2)]
        lo_stageb = [self.sb(f"lo_stage{b}", [128, D], BF16, p0s)
                     for b in range(2)]
        loTb = [self.sb(f"loT{b}", [128, KT * 128], BF16, p0s)
                for b in range(2)]
        r_xnstb = [em.res(f"xnst{b}") for b in range(2)]
        r_lostb = [em.res(f"lost{b}") for b in range(2)]
        r_loTb = [em.res(f"loT{b}") for b in range(2)]

        wg1f = self.sb("wg1f", [128, KT * EXP], F32, p0s)
        wg1hi = self.sb("wg1hi", [128, KT * EXP], BF16, p0s)
        wg1lo = self.sb("wg1lo", [128, KT * EXP], BF16, p0s)
        r_wg1, r_wg1h, r_wg1l = em.res(), em.res(), em.res()
        wg2f = self.sb("wg2f", [EXP, EXP], F32, p0s)
        r_wg2 = em.res()
        em.op("sp", lambda e: e.dma_start(
            wg1f[:, :].rearrange("p (k e) -> p k e", k=KT),
            self.wg1T_h.rearrange("(k p) e -> p k e", k=KT)),
            writes=[r_wg1], dma=True)
        em.op("sp", lambda e: e.dma_start(wg2f[:, :], self.wg2T_h),
              writes=[r_wg2], dma=True)
        em.op("dve", lambda e: e.tensor_copy(wg1hi[:, :], wg1f[:, :]),
              reads=[r_wg1], writes=[r_wg1h])
        em.op("dve", lambda e: e.tensor_tensor(
            wg1lo[:, :], wg1f[:, :], wg1hi[:, :], ALU.subtract),
            reads=[r_wg1, r_wg1h], writes=[r_wg1l])

        l1T = self.sb("l1T", [EXP, T], F32, p0s)
        r_l1 = em.res("l1T")
        l2T = self.sb("l2T", [EXP, T], F32, p0s)
        r_l2 = em.res("l2T")
        lgt = self.sb("lgt", [128, 8], F32, p0s)
        r_lgt = em.res("lgt")
        tkb = self.sb("tkb", [128, 40], F32, p0s)
        r_tk = em.res("tk")

        pb, r_pb = self.pbank, self.r_pb
        epsc = self.constcol(EPS_LN)
        rcc = self.r_cconst

        for tt in range(NTT):
            xb, r_xb = x_tile[tt % 2], r_xtile[tt % 2]
            stats, r_stats = statsb[tt % 2], r_statsb[tt % 2]
            xn_stage, r_xnst = xn_stageb[tt % 2], r_xnstb[tt % 2]
            lo_stage, r_lost = lo_stageb[tt % 2], r_lostb[tt % 2]
            loT, r_loT = loTb[tt % 2], r_loTb[tt % 2]
            em.op("sp", lambda e, tt=tt, xb=xb: e.dma_start(
                xb[:, :], self.x_h[tt * 128:(tt + 1) * 128, :]),
                writes=[r_xb], dma=True, dbg=f"ldx{tt}")
            em.op("dve", lambda e, xb=xb, stats=stats: e.tensor_reduce(
                stats[:, 0:1], xb[:, :], AX.X, ALU.add),
                reads=[r_xb], writes=[r_stats], dbg="lnsum")
            em.op("act", lambda e, xb=xb, stats=stats: e.activation(
                sq_scr[:, :], xb[:, :], AF.Square, accum_out=stats[:, 2:3]),
                reads=[r_xb], writes=[r_sq, r_stats], dbg="lnsq")
            em.op("dve", lambda e, stats=stats: e.tensor_scalar(
                stats[:, 1:2], stats[:, 0:1], 1.0 / D, None, ALU.mult),
                reads=[r_stats], writes=[r_stats])
            em.op("act", lambda e, stats=stats: e.activation(
                stats[:, 3:4], stats[:, 1:2], AF.Square),
                reads=[r_stats], writes=[r_stats])
            em.op("dve", lambda e, stats=stats: e.tensor_scalar(
                stats[:, 3:4], stats[:, 2:3], 1.0 / D, stats[:, 3:4], ALU.mult,
                ALU.subtract), reads=[r_stats], writes=[r_stats], dbg="var")
            em.op("act", lambda e, stats=stats: e.activation(
                stats[:, 4:5], stats[:, 3:4], AF.Ln, bias=epsc),
                reads=[r_stats, rcc], writes=[r_stats])
            em.op("act", lambda e, stats=stats: e.activation(
                stats[:, 4:5], stats[:, 4:5], AF.Exp, scale=-0.5),
                reads=[r_stats], writes=[r_stats], dbg="rstd")
            em.op("dve", lambda e, xb=xb, stats=stats, xn_stage=xn_stage:
                  e.tensor_scalar(
                  xn_stage[:, :], xb[:, :], stats[:, 1:2], stats[:, 4:5],
                  ALU.subtract, ALU.mult),
                  reads=[r_xb, r_stats], writes=[r_xnst], dbg="xn16")
            em.op("dve", lambda e, xb=xb, stats=stats: e.tensor_scalar(
                xb[:, :], xb[:, :], stats[:, 1:2], stats[:, 4:5],
                ALU.subtract, ALU.mult),
                reads=[r_xb, r_stats], writes=[r_xb], dbg="xnf32")
            em.op("dve", lambda e, xb=xb, xn_stage=xn_stage, lo_stage=lo_stage:
                  e.tensor_tensor(
                  lo_stage[:, :], xb[:, :], xn_stage[:, :], ALU.subtract),
                  reads=[r_xb, r_xnst], writes=[r_lost], dbg="lo")

            for g in range(4):
                bi = g % 2
                pv = pb[bi][:, :].bitcast(BF16)

                def tr_fn(e, g=g, pv=pv, xn_stage=xn_stage):
                    last = None
                    for j in range(4):
                        k = g * 4 + j
                        last = e.transpose(pv[:, j * 128:(j + 1) * 128],
                                           xn_stage[:, k * 128:(k + 1) * 128],
                                           self.id16[:, :])
                    return last
                em.op("pe", tr_fn, reads=[r_xnst, self.r_id16], writes=[r_pb[bi]],
                      dbg=f"trxn{tt}_{g}")

                def ev_fn(e, g=g, tt=tt, pv=pv):
                    dst = self.xnTb[:, :].rearrange(
                        "p (k t) -> p k t", k=KT)[:, g * 4:(g + 1) * 4,
                                                  tt * 128:(tt + 1) * 128]
                    return e.tensor_copy(
                        dst, pv[:, 0:512].rearrange("p (j c) -> p j c", j=4))
                em.op("dve", ev_fn, reads=[r_pb[bi]],
                      writes=[self.r_xnT[g * 4 + j] for j in range(4)],
                      dbg=f"evxn{tt}_{g}")

                bi2 = 2 + g % 2
                pv2 = pb[bi2][:, :].bitcast(BF16)

                def trlo_fn(e, g=g, pv2=pv2, lo_stage=lo_stage):
                    last = None
                    for j in range(4):
                        k = g * 4 + j
                        last = e.transpose(pv2[:, j * 128:(j + 1) * 128],
                                           lo_stage[:, k * 128:(k + 1) * 128],
                                           self.id16[:, :])
                    return last
                em.op("pe", trlo_fn, reads=[r_lost, self.r_id16],
                      writes=[r_pb[bi2]], dbg=f"trlo{tt}_{g}")

                def evlo_fn(e, g=g, pv2=pv2, loT=loT):
                    dst = loT[:, :].rearrange("p (k t) -> p k t", k=KT)[
                        :, g * 4:(g + 1) * 4, :]
                    return e.tensor_copy(
                        dst, pv2[:, 0:512].rearrange("p (j c) -> p j c", j=4))
                em.op("dve", evlo_fn, reads=[r_pb[bi2]], writes=[r_loT],
                      dbg=f"evlo{tt}_{g}")

            def gmm_fn(e, tt=tt, loT=loT):
                w_hi = wg1hi[:, :].rearrange("p (k e) -> p k e", k=KT)
                w_lo = wg1lo[:, :].rearrange("p (k e) -> p k e", k=KT)
                lov = loT[:, :].rearrange("p (k t) -> p k t", k=KT)
                last = None
                for k in range(KT):
                    xcol = self.xnTb[:, :].rearrange("p (k t) -> p k t", k=KT)[
                        :, k, tt * 128:(tt + 1) * 128]
                    last = e.matmul(pb[4][0:8, 0:128], w_hi[:, k], xcol,
                                    start=(k == 0), stop=False)
                    last = e.matmul(pb[4][0:8, 0:128], w_lo[:, k], xcol,
                                    start=False, stop=False)
                    last = e.matmul(pb[4][0:8, 0:128], w_hi[:, k], lov[:, k],
                                    start=False, stop=(k == KT - 1))
                return last
            em.op("pe", gmm_fn,
                  reads=[r_wg1h, r_wg1l, r_loT] + self.r_xnT, writes=[r_pb[4]],
                  dbg=f"gmm{tt}")
            em.op("act", lambda e, tt=tt: e.activation(
                l1T[:, tt * 128:(tt + 1) * 128], pb[4][0:8, 0:128], AF.Tanh),
                reads=[r_pb[4]], writes=[r_l1], dbg=f"tanh{tt}")

        for j in range(4):
            em.op("pe", lambda e, j=j: e.matmul(
                pb[5][0:8, 0:512], wg2f[:, :], l1T[:, j * 512:(j + 1) * 512],
                start=True, stop=True),
                reads=[r_wg2, r_l1], writes=[r_pb[5]], dbg=f"g2_{j}")
            em.op("dve", lambda e, j=j: e.tensor_copy(
                l2T[:, j * 512:(j + 1) * 512], pb[5][0:8, 0:512]),
                reads=[r_pb[5]], writes=[r_l2], dbg=f"g2ev{j}")

        for tt in range(NTT):
            em.op("pe", lambda e, tt=tt: e.transpose(
                self.pbank[6][0:128, 0:8], l2T[0:8, tt * 128:(tt + 1) * 128],
                self.idf[0:8, 0:8]),
                reads=[r_l2, self.r_idf], writes=[r_pb[6]], dbg=f"trl2{tt}")
            em.op("dve", lambda e: e.tensor_copy(lgt[:, 0:8],
                                                 self.pbank[6][0:128, 0:8]),
                  reads=[r_pb[6]], writes=[r_lgt])
            em.op("dve", lambda e: e.reduce_max(tkb[:, 0:1], lgt[:, 0:8], AX.X),
                  reads=[r_lgt], writes=[r_tk])
            em.op("dve", lambda e: e.tensor_scalar(
                tkb[:, 8:16], lgt[:, 0:8], tkb[:, 0:1], None, ALU.is_ge),
                reads=[r_lgt, r_tk], writes=[r_tk], dbg="m1")
            em.op("dve", lambda e: e.scalar_tensor_tensor(
                tkb[:, 16:24], tkb[:, 8:16], -1e30, lgt[:, 0:8], ALU.mult,
                ALU.add), reads=[r_tk, r_lgt], writes=[r_tk], dbg="l2m")
            em.op("dve", lambda e: e.reduce_max(tkb[:, 1:2], tkb[:, 16:24], AX.X),
                  reads=[r_tk], writes=[r_tk])
            em.op("dve", lambda e: e.tensor_scalar(
                tkb[:, 24:32], tkb[:, 16:24], tkb[:, 1:2], None, ALU.is_ge),
                reads=[r_tk], writes=[r_tk], dbg="m2")
            em.op("dve", lambda e: e.tensor_tensor(
                tkb[:, 2:3], tkb[:, 0:1], tkb[:, 1:2], ALU.subtract),
                reads=[r_tk], writes=[r_tk])
            em.op("act", lambda e: e.activation(tkb[:, 3:4], tkb[:, 2:3],
                                                AF.Sigmoid),
                  reads=[r_tk], writes=[r_tk], dbg="sig")
            em.op("dve", lambda e: e.tensor_scalar(
                tkb[:, 4:5], tkb[:, 3:4], 1.0, -1.0, ALU.subtract, ALU.mult),
                reads=[r_tk], writes=[r_tk], dbg="p2")
            em.op("dve", lambda e: e.tensor_scalar(
                tkb[:, 8:16], tkb[:, 8:16], tkb[:, 3:4], None, ALU.mult),
                reads=[r_tk], writes=[r_tk])
            em.op("dve", lambda e: e.scalar_tensor_tensor(
                tkb[:, 32:40], tkb[:, 24:32], tkb[:, 4:5], tkb[:, 8:16],
                ALU.mult, ALU.add),
                reads=[r_tk], writes=[r_tk], dbg="scores")
            em.op("pe", lambda e: e.transpose(
                self.pbank[7][0:8, 0:128], tkb[:, 32:40], self.idf[:, :]),
                reads=[r_tk, self.r_idf], writes=[r_pb[7]], dbg=f"trsc{tt}")
            em.op("dve", lambda e, tt=tt: e.tensor_copy(
                self.scoT[0:8, tt * 128:(tt + 1) * 128],
                self.pbank[7][0:8, 0:128]),
                reads=[r_pb[7]], writes=[self.r_scoT])

        self.scoh_h = self.dint("scoh", [EXP, T], BF16)
        self.r_scoh = em.res("scoh")
        em.op("sp", lambda e: e.dma_start(self.scoh_h, self.scoT[:, :]),
              reads=[self.r_scoT], writes=[self.r_scoh], dma=True)

        if "xnT" in self.dbg:
            o = self.dout("dbg_xnT", [KT * 128, T], BF16)
            for k in range(KT):
                em.op("sp", lambda e, k=k, o=o: e.dma_start(
                    o[k * 128:(k + 1) * 128, :], self.xnT(k)),
                    reads=[self.r_xnT[k]], dma=True)
        if "scoT" in self.dbg:
            o = self.dout("dbg_scoT", [EXP, T], BF16)
            em.op("sp", lambda e, o=o: e.dma_start(o, self.scoT[:, :]),
                  reads=[self.r_scoT], dma=True)
        if "l2T" in self.dbg:
            o = self.dout("dbg_l2T", [EXP, T], F32)
            em.op("sp", lambda e, o=o: e.dma_start(o, l2T[:, :]),
                  reads=[r_l2], dma=True)
        self.close_scope(p0s)

    def stub_out(self):
        em = self.em
        import contextlib as _ctx
        sos = _ctx.ExitStack()
        x_tile = [self.sb(f"so{b}", [128, D], F32, sos) for b in range(2)]
        r_xtile = [em.res(f"so{b}") for b in range(2)]
        for tt in range(NTT):
            xb, r_xb = x_tile[tt % 2], r_xtile[tt % 2]
            em.op("sp", lambda e, tt=tt, xb=xb: e.dma_start(
                xb[:, :], self.x_h[tt * 128:(tt + 1) * 128, :]),
                writes=[r_xb], dma=True)
            em.op("sp", lambda e, tt=tt, xb=xb: e.dma_start(
                self.out_h[tt * 128:(tt + 1) * 128, :], xb[:, :]),
                reads=[r_xb], writes=[self.r_out], dma=True)
        self.close_scope(sos)

    def finish(self):
        st = self.em.finalize(self.nc, self.sems)
        self.stack.close()
        return st


def host_inputs(core_x, w):
    f32 = np.float32
    d = {
        "x": np.ascontiguousarray(core_x, dtype=f32),
        "winT": w["winT"], "woutT": w["woutT"], "wupT": w["wupT"],
        "wdT": w["wdT"], "wg1T": w["wg1T"], "wg2T": w["wg2T"],
        "convw": w["convw"], "convb": w["convb"], "dtb": w["dtb"],
        "alog": w["alog"], "drep": w["drep"], "nw": w["nw"],
        "idf": w["idf"], "tril": w["tril"], "onesrow": w["onesrow"],
        "bupc": w["bupc"], "bdn": w["bdn"], "negHE": w["negHE"],
        "one3": w["one3"],
    }
    return d


def shared_weights(inputs):
    f32 = np.float32
    asc = np.ascontiguousarray
    w = {}
    w["winT"] = asc(np.asarray(inputs["L0_attn_in_proj_w"], f32).T)
    w["woutT"] = asc(np.asarray(inputs["L0_attn_out_proj_w"], f32).T)
    wup = np.asarray(inputs["L0_ffn_Wup"], f32)
    w["wupT"] = asc(wup.transpose(2, 0, 1).reshape(D, EXP * HE))
    wdn = np.asarray(inputs["L0_ffn_Wdown"], f32)
    w["wdT"] = asc(wdn.transpose(0, 2, 1))
    w["wg1T"] = asc(np.asarray(inputs["L0_ffn_Wg1"], f32).T)
    w["wg2T"] = asc(np.asarray(inputs["L0_ffn_Wg2"], f32).T)
    cw = np.asarray(inputs["L0_attn_conv_w"], f32)[:, 0, :]
    w["convw"] = asc(cw.reshape(NBC // 128, 128, 4).transpose(1, 0, 2)
                     .reshape(128, (NBC // 128) * 4))
    cb = np.asarray(inputs["L0_attn_conv_b"], f32)
    w["convb"] = asc(cb.reshape(NBC // 128, 128).T)
    w["dtb"] = asc(np.asarray(inputs["L0_attn_dt_bias"], f32).reshape(NH, 1))
    w["alog"] = asc(np.asarray(inputs["L0_attn_A_log"], f32).reshape(NH, 1))
    w["drep"] = asc(np.repeat(np.asarray(inputs["L0_attn_D"], f32), PP)
                    .reshape(1, I4))
    nw = np.asarray(inputs["L0_attn_norm_w"], f32)
    w["nw"] = asc(nw.reshape(I4 // 128, 128).T)
    bup = np.asarray(inputs["L0_ffn_bup"], f32)
    w["bupc"] = asc(np.concatenate(
        [bup[e].reshape(8, 128).T for e in range(EXP)], axis=1))
    w["bdn"] = asc(np.asarray(inputs["L0_ffn_bdown"], f32))
    he = np.zeros((32, 32, 128), f32)
    for h in range(32):
        he[h, h, :] = -1.0
    w["negHE"] = asc(he.reshape(32, 32 * 128))
    o3 = np.zeros((32, 128), f32)
    o3[0:3, :] = 1.0
    w["one3"] = o3
    w["idf"] = np.eye(128, dtype=f32)
    w["tril"] = np.triu(np.ones((128, 128), f32))
    w["onesrow"] = np.ones((1, 128), f32)
    return w


def kernel(**inputs):
    kb = KB().build()
    st = kb.finish()
    w = shared_weights(inputs)
    x = np.asarray(inputs["L0_attn_hidden_states"], np.float32)
    in_maps = [host_inputs(x[i], w) for i in range(8)]
    res = run_bass_kernel_spmd(kb.nc, in_maps, core_ids=list(range(8)))
    out = np.stack([np.asarray(r["out"]) for r in res.results], axis=0)
    return out


# version 20 (final)
# speedup vs baseline: 1.2176x; 1.0087x over previous
import contextlib
import numpy as np
import ml_dtypes
import concourse.bass as bass
import concourse.mybir as mybir
from concourse.bass_utils import run_bass_kernel_spmd

F32 = mybir.dt.float32
BF16 = mybir.dt.bfloat16
AF = mybir.ActivationFunctionType
ALU = mybir.AluOpType
AX = mybir.AxisListType

T = 2048
D = 2048
FOUT = 8512
I4 = 4096
NBC = 4352
NH = 64
PP = 64
NS = 128
CH = 128
NCH = T // CH
EXP = 8
HE = 1024
KT = D // 128
NTT = T // 128
EPS_LN = 1e-5
EPS_RMS = 1e-5

ENGS = ("pe", "dve", "act", "pool", "sp")
NLANES = 8
QSEMS = tuple(ENGS) + tuple(
    f"{e}q{i}" for e in ("pool", "sp", "act") for i in range(NLANES))


class Res:
    __slots__ = ("name", "writer", "readers")

    def __init__(self, name):
        self.name = name
        self.writer = None
        self.readers = []


class Op:
    __slots__ = ("fn", "eng", "qsem", "inc_amt", "waits", "need_inc",
                 "sem_val_after", "dbg")

    def __init__(self, fn, eng, qsem, inc_amt, dbg):
        self.fn = fn
        self.eng = eng
        self.qsem = qsem
        self.inc_amt = inc_amt
        self.waits = []
        self.need_inc = False
        self.sem_val_after = 0
        self.dbg = dbg


class EM:

    def __init__(self):
        self.streams = {e: [] for e in ENGS}
        self.nres = 0
        self.dma_count = {e: 0 for e in ("pool", "sp", "act")}
        self.dma_lane_prev = {e: [None] * NLANES for e in ("pool", "sp", "act")}

    def res(self, name=None):
        self.nres += 1
        return Res(name or f"r{self.nres}")

    def op(self, eng, fn, reads=(), writes=(), dma=False, dbg=""):
        lane_prev = None
        if dma:
            j = self.dma_count[eng]
            self.dma_count[eng] = j + 1
            lane = j % NLANES
            qsem = f"{eng}q{lane}"
            lane_prev = self.dma_lane_prev[eng][lane]
        else:
            qsem = eng
        o = Op(fn, eng, qsem, 16 if dma else 1, dbg)
        if dma:
            self.dma_lane_prev[eng][lane] = o
        tgt = []
        if lane_prev is not None:
            tgt.append(lane_prev)
        for r in reads:
            if r.writer is not None:
                tgt.append(r.writer)
        for r in writes:
            if r.writer is not None:
                tgt.append(r.writer)
            tgt.extend(r.readers)
        seen = set()
        for t in tgt:
            if t is o or id(t) in seen:
                continue
            seen.add(id(t))
            if o.qsem == "pe" and t.qsem == "pe":
                continue
            o.waits.append(t)
        for r in reads:
            r.readers.append(o)
        for r in writes:
            r.writer = o
            r.readers = []
        self.streams[eng].append(o)
        return o

    def barrier(self):
        lasts = []
        for s in self.streams.values():
            for t in reversed(s):
                if t.fn is not None and not t.qsem[-1].isdigit():
                    lasts.append(t)
                    break
        for e in ("pool", "sp", "act"):
            for prev in self.dma_lane_prev[e]:
                if prev is not None:
                    lasts.append(prev)
        for eng in ENGS:
            o = Op(None, eng, eng, 1, "barrier")
            o.waits.extend(lasts)
            self.streams[eng].append(o)

    def finalize(self, nc, sems):
        for s in self.streams.values():
            for o in s:
                if o.qsem.endswith("q"):
                    o.need_inc = True
                for t in o.waits:
                    t.need_inc = True
        cum = {q: 0 for q in QSEMS}
        for eng in ENGS:
            for o in self.streams[eng]:
                if o.need_inc:
                    cum[o.qsem] += o.inc_amt
                o.sem_val_after = cum[o.qsem]
        knows = {e: {q: 0 for q in QSEMS} for e in ENGS}
        stats = {"waits": 0, "incs": 0, "ops": 0}

        def emit_stream(engname):
            def run(eh):
                kn = knows[engname]
                for o in self.streams[engname]:
                    stats["ops"] += 1
                    need = {}
                    for t in o.waits:
                        v = t.sem_val_after
                        if v > need.get(t.qsem, 0):
                            need[t.qsem] = v
                    for q in QSEMS:
                        if q in need and kn[q] < need[q]:
                            eh.wait_ge(sems[q], need[q])
                            kn[q] = need[q]
                            stats["waits"] += 1
                    if o.fn is not None:
                        inst = o.fn(eh)
                        if o.need_inc:
                            assert inst is not None, (o.eng, o.dbg)
                            inst.then_inc(sems[o.qsem], o.inc_amt)
                            stats["incs"] += 1
            return run

        with nc.Block() as block:
            block.tensor(emit_stream("pe"))
            block.vector(emit_stream("dve"))
            block.scalar(emit_stream("act"))
            block.gpsimd(emit_stream("pool"))
            block.sync(emit_stream("sp"))
        return stats


class KB:

    def __init__(self, dbg=()):
        self.dbg = set(dbg)
        self.nc = bass.Bass("TRN2", target_bir_lowering=False)
        self.em = EM()
        self.stack = contextlib.ExitStack()
        self.sems = {}
        for q in QSEMS:
            self.sems[q] = self.stack.enter_context(self.nc.semaphore("sem_" + q))
        self.dr = {}

    def din(self, name, shape, dt=F32):
        ap = self.nc.dram_tensor(name, list(shape), dt, kind="ExternalInput").ap()
        self.dr[name] = ap
        return ap

    def dout(self, name, shape, dt=F32):
        ap = self.nc.dram_tensor(name, list(shape), dt, kind="ExternalOutput").ap()
        self.dr[name] = ap
        return ap

    def dint(self, name, shape, dt):
        ap = self.nc.dram_tensor(name, list(shape), dt, kind="Internal").ap()
        self.dr[name] = ap
        return ap

    def sb(self, name, shape, dt, scope=None, side=None):
        st = scope if scope is not None else self.stack
        if side is None:
            return st.enter_context(self.nc.sbuf_tensor(name, list(shape), dt))
        return st.enter_context(
            self.nc.sbuf_tensor(name, list(shape), dt, side=side))

    def close_scope(self, scope):
        self.em.barrier()
        scope.close()

    def ps(self, name, shape, dt=F32):
        return self.stack.enter_context(self.nc.psum_tensor(name, list(shape), dt))

    def build(self):
        em = self.em
        self.x_h = self.din("x", [T, D])
        self.winT_h = self.din("winT", [D, FOUT])
        self.woutT_h = self.din("woutT", [I4, D])
        self.wupT_h = self.din("wupT", [D, EXP * HE])
        self.wdT_h = self.din("wdT", [EXP, HE, D])
        self.wg1T_h = self.din("wg1T", [D, EXP])
        self.wg2T_h = self.din("wg2T", [EXP, EXP])
        self.convw_h = self.din("convw", [128, (NBC // 128) * 4])
        self.convb_h = self.din("convb", [128, NBC // 128])
        self.dtb_h = self.din("dtb", [NH, 1])
        self.alog_h = self.din("alog", [NH, 1])
        self.drep_h = self.din("drep", [1, I4])
        self.nw_h = self.din("nw", [128, I4 // 128])
        self.idf_h = self.din("idf", [128, 128])
        self.tril_h = self.din("tril", [128, 128])
        self.ones_h = self.din("onesrow", [1, 128])
        self.out_h = self.dout("out", [T, D], F32)
        self.zs_h = self.dint("zs", [T, I4], BF16)
        self.xts_h = self.dint("xts", [NCH, I4, CH], BF16)
        self.ffn_h = self.dint("ffn", [T, D], BF16)
        self.ygt_h = self.dint("ygt", [NCH, I4, CH], BF16)
        self.acr_h = self.dint("acr", [NCH, 3, NH * CH], BF16)

        self.r_out = em.res("out")
        self.r_zs = [em.res(f"zs{t}") for t in range(NTT)]
        self.r_xts = [em.res(f"xts{c}") for c in range(NCH)]
        self.r_ffn = [em.res(f"ffn{t}") for t in range(NTT)]
        self.r_ygt = [em.res(f"ygt{c}") for c in range(NCH)]
        self.r_acr = [em.res(f"acr{c}") for c in range(NCH)]

        self.idf = self.sb("idf_s", [128, 128], F32)
        self.id16 = self.sb("id16", [128, 128], BF16)
        self.tril = self.sb("tril_s", [128, 128], F32)
        self.ones1 = self.sb("ones1", [1, 128], F32)
        self.ones16 = self.sb("ones16", [1, 128], BF16)
        self.r_idf = em.res("idf")
        self.r_id16 = em.res("id16")
        self.r_tril = em.res("tril")
        self.r_ones1 = em.res("ones1")
        self.r_ones16 = em.res("ones16")
        em.op("sp", lambda e: e.dma_start(self.idf[:, :], self.idf_h),
              writes=[self.r_idf], dma=True)
        em.op("pool", lambda e: e.dma_start(self.id16[:, :], self.idf_h),
              writes=[self.r_id16], dma=True)
        em.op("sp", lambda e: e.dma_start(self.tril[:, :], self.tril_h),
              writes=[self.r_tril], dma=True)
        em.op("sp", lambda e: e.dma_start(self.ones1[:, :], self.ones_h),
              writes=[self.r_ones1], dma=True)
        em.op("pool", lambda e: e.dma_start(self.ones16[:, :], self.ones_h),
              writes=[self.r_ones16], dma=True)

        self.psumAll = self.ps("psumAll", [128, 4096], F32)
        self.pbank = [self.psumAll[:, i * 512:(i + 1) * 512] for i in range(8)]
        self.r_pb = [em.res(f"pb{i}") for i in range(8)]
        self.pview = self.psumAll

        self.cconst = self.sb("cconst", [128, 8], F32)
        self.r_cconst = em.res("cconst")
        self._cmap = {}
        return self.build_phases()

    def constcol(self, val):
        if val not in self._cmap:
            i = len(self._cmap)
            assert i < 8
            self._cmap[val] = i
            self.em.op("dve", lambda e, i=i, val=val: e.memset(
                self.cconst[:, i:i + 1], float(val)),
                writes=[self.r_cconst], dbg=f"const{val}")
        i = self._cmap[val]
        return self.cconst[:, i:i + 1]

    def build_phases(self):
        import contextlib as _ctx
        self.xns = _ctx.ExitStack()
        self.rscope = _ctx.ExitStack()
        self.phase0()
        if "skipF" not in self.dbg:
            self.phaseF()
        self.phaseA()
        self.close_scope(self.xns)
        if "stopA" in self.dbg:
            self.stub_out()
            self.em.barrier()
            return self
        self.phaseB()
        self.close_scope(self.rscope)
        if "stopB" in self.dbg:
            self.stub_out()
            self.em.barrier()
            return self
        self.phaseD()
        self.em.barrier()
        return self

    def phaseA(self):
        em = self.em
        import contextlib as _ctx
        pas = _ctx.ExitStack()
        pb, r_pb = self.pbank, self.r_pb

        self.bct = self.sb("bct", [128, 2 * T], BF16, self.rscope, side="right")
        self.r_bct = [em.res("bT"), em.res("cT")]
        self.aT = self.sb("aT", [NH, T], F32, self.rscope, side="right")
        self.r_aT = em.res("aT")
        self.dt_tok = self.sb("dt_tok", [128, NCH * NH], BF16, self.rscope,
                              side="right")
        self.r_dtok = em.res("dt_tok")

        wz = [self.sb(f"wz{b}", [128, KT * 512], BF16, pas) for b in range(2)]
        r_wz = [em.res(f"wz{b}") for b in range(2)]
        wx = [self.sb(f"wx{b}", [128, KT * 128], BF16, pas) for b in range(2)]
        r_wx = [em.res(f"wx{b}") for b in range(2)]
        z16 = [self.sb(f"z16_{b}", [128, 512], BF16, pas) for b in range(2)]
        r_z16 = [em.res(f"z16{b}") for b in range(2)]
        cacc = [self.sb(f"cacc{b}", [128, T], F32, pas) for b in range(2)]
        r_cacc = [em.res(f"cacc{b}") for b in range(2)]
        xt16 = [self.sb(f"xt16_{b}", [128, T], BF16, pas) for b in range(2)]
        r_xt16 = [em.res(f"xt16{b}") for b in range(2)]
        convw = self.sb("convw_s", [128, 34 * 4], F32, pas)
        convb = self.sb("convb_s", [128, 34], F32, pas)
        r_cw, r_cb = em.res(), em.res()
        dtr = self.sb("dtr", [NH, T], F32, pas)
        dtt = self.sb("dtt", [NH, T], F32, pas)
        dtv = self.sb("dtv", [NH, T], F32, pas)
        r_dtr, r_dtt, r_dtv = em.res(), em.res(), em.res()
        acol = self.sb("acol", [NH, 2], F32, pas)
        r_acol = em.res()

        em.op("sp", lambda e: e.dma_start(convw[:, :], self.convw_h),
              writes=[r_cw], dma=True)
        em.op("sp", lambda e: e.dma_start(convb[:, :], self.convb_h),
              writes=[r_cb], dma=True)

        for fs in range(8):
            wb, r_wb = wz[fs % 2], r_wz[fs % 2]
            em.op("pool", lambda e, fs=fs, wb=wb: e.dma_start(
                wb[:, :].rearrange("p (k c) -> p k c", k=KT),
                self.winT_h.rearrange("(k p) f -> p k f", k=KT)[
                    :, :, fs * 512:(fs + 1) * 512]),
                writes=[r_wb], dma=True, dbg=f"wz{fs}")
            for tt in range(NTT):
                bi = tt % 2
                def zmm(e, wb=wb, tt=tt, bi=bi):
                    last = None
                    for k in range(KT):
                        last = e.matmul(
                            pb[bi][:, :],
                            self.xnTb[:, :].rearrange("p (k t) -> p k t", k=KT)[
                                :, k, tt * 128:(tt + 1) * 128],
                            wb[:, :].rearrange("p (k c) -> p k c", k=KT)[:, k],
                            start=(k == 0), stop=(k == KT - 1))
                    return last
                em.op("pe", zmm, reads=[r_wb] + self.r_xnT, writes=[r_pb[bi]],
                      dbg=f"zmm{fs}_{tt}")
                zb, r_zb = z16[tt % 2], r_z16[tt % 2]
                em.op("act", lambda e, bi=bi, zb=zb: e.activation(
                    zb[:, :], pb[bi][:, :], AF.Silu),
                    reads=[r_pb[bi]], writes=[r_zb], dbg=f"zsilu{fs}_{tt}")
                em.op("sp", lambda e, fs=fs, tt=tt, zb=zb: e.dma_start(
                    self.zs_h[tt * 128:(tt + 1) * 128,
                              fs * 512:(fs + 1) * 512], zb[:, :]),
                    reads=[r_zb], writes=[self.r_zs[tt]], dma=True,
                    dbg=f"zst{fs}_{tt}")

        for j in range(34):
            wb, r_wb = wx[j % 2], r_wx[j % 2]
            f0 = I4 + j * 128
            em.op("pool", lambda e, f0=f0, wb=wb: e.dma_start(
                wb[:, :].rearrange("p (k c) -> p k c", k=KT),
                self.winT_h.rearrange("(k p) f -> p k f", k=KT)[
                    :, :, f0:f0 + 128]),
                writes=[r_wb], dma=True, dbg=f"wx{j}")
            qoff = (j % 2) * 4
            r_quad = [r_pb[qoff + n] for n in range(4)]
            for n in range(4):
                bi = qoff + n
                def xmm(e, wb=wb, n=n, bi=bi):
                    last = None
                    for k in range(KT):
                        last = e.matmul(
                            pb[bi][:, :],
                            wb[:, :].rearrange("p (k c) -> p k c", k=KT)[:, k],
                            self.xnTb[:, :].rearrange("p (k t) -> p k t", k=KT)[
                                :, k, n * 512:(n + 1) * 512],
                            start=(k == 0), stop=(k == KT - 1))
                    return last
                em.op("pe", xmm, reads=[r_wb] + self.r_xnT, writes=[r_pb[bi]],
                      dbg=f"xmm{j}_{n}")
            cb_, r_cc = cacc[j % 2], r_cacc[j % 2]
            pvw = self.pview[:, qoff * 512:(qoff + 4) * 512]
            em.op("act", lambda e, j=j, cb_=cb_, pvw=pvw: e.activation(
                cb_[:, :], pvw, AF.Copy, scale=convw[:, j * 4 + 3:j * 4 + 4]),
                reads=r_quad + [r_cw], writes=[r_cc], dbg=f"cv3_{j}")
            for kk in (2, 1, 0):
                sft = 3 - kk
                em.op("dve", lambda e, j=j, kk=kk, sft=sft, cb_=cb_, pvw=pvw:
                      e.scalar_tensor_tensor(
                          cb_[:, sft:T], pvw[:, 0:T - sft],
                          convw[:, j * 4 + kk:j * 4 + kk + 1],
                          cb_[:, sft:T], ALU.mult, ALU.add),
                      reads=r_quad + [r_cw, r_cc], writes=[r_cc],
                      dbg=f"cv{kk}_{j}")
            if j < 32:
                ob, r_ob = xt16[j % 2], r_xt16[j % 2]
                em.op("act", lambda e, j=j, cb_=cb_, ob=ob: e.activation(
                    ob[:, :], cb_[:, :], AF.Silu,
                    bias=convb[:, j:j + 1]),
                    reads=[r_cc, r_cb], writes=[r_ob], dbg=f"cvs{j}")
                em.op("sp", lambda e, j=j, ob=ob: e.dma_start(
                    self.xts_h[:, j * 128:(j + 1) * 128, :].transpose([1, 0, 2]),
                    ob[:, :].rearrange("p (c t) -> p c t", c=NCH)),
                    reads=[r_ob], writes=self.r_xts, dma=True, dbg=f"xst{j}")
            else:
                bi2 = j - 32
                em.op("act", lambda e, j=j, bi2=bi2, cb_=cb_: e.activation(
                    self.bct[:, bi2 * T:(bi2 + 1) * T], cb_[:, :], AF.Silu,
                    bias=convb[:, j:j + 1]),
                    reads=[r_cc, r_cb], writes=[self.r_bct[bi2]], dbg=f"cvs{j}")

        def dtmm(e, n=None):
            last = None
            for k in range(KT):
                last = e.matmul(
                    pb[6][0:NH, 0:512],
                    self.winT_h_dt[:, :].rearrange("p (k c) -> p k c", k=KT)[:, k],
                    self.xnTb[:, :].rearrange("p (k t) -> p k t", k=KT)[
                        :, k, n * 512:(n + 1) * 512],
                    start=(k == 0), stop=(k == KT - 1))
            return last
        wdt = self.sb("wdt", [128, KT * NH], BF16, pas)
        r_wdt = em.res()
        self.winT_h_dt = wdt
        em.op("pool", lambda e: e.dma_start(
            wdt[:, :].rearrange("p (k c) -> p k c", k=KT),
            self.winT_h.rearrange("(k p) f -> p k f", k=KT)[
                :, :, FOUT - NH:FOUT]),
            writes=[r_wdt], dma=True, dbg="wdt")
        for n in range(4):
            em.op("pe", lambda e, n=n: dtmm(e, n=n),
                  reads=[r_wdt] + self.r_xnT, writes=[r_pb[6]], dbg=f"dtmm{n}")
            em.op("dve", lambda e, n=n: e.tensor_copy(
                dtr[:, n * 512:(n + 1) * 512], pb[6][0:NH, 0:512]),
                reads=[r_pb[6]], writes=[r_dtr], dbg=f"dtev{n}")

        dtb = self.sb("dtb_s", [NH, 1], F32, pas)
        r_dtb = em.res()
        em.op("sp", lambda e: e.dma_start(dtb[:, :], self.dtb_h),
              writes=[r_dtb], dma=True)
        em.op("sp", lambda e: e.dma_start(acol[:, 0:1], self.alog_h),
              writes=[r_acol], dma=True)
        onec = self.constcol(1.0)
        rcc = self.r_cconst
        em.op("dve", lambda e: e.tensor_scalar(
            dtr[:, :], dtr[:, :], dtb[:, :], None, ALU.add),
            reads=[r_dtr, r_dtb], writes=[r_dtr])
        em.op("act", lambda e: e.activation(dtt[:, :], dtr[:, :], AF.Abs),
              reads=[r_dtr], writes=[r_dtt], dbg="abs")
        em.op("act", lambda e: e.activation(dtt[:, :], dtt[:, :], AF.Exp,
                                            scale=-1.0),
              reads=[r_dtt], writes=[r_dtt])
        em.op("act", lambda e: e.activation(dtt[:, :], dtt[:, :], AF.Ln,
                                            bias=onec[0:NH, :]),
              reads=[r_dtt, rcc], writes=[r_dtt])
        em.op("dve", lambda e: e.tensor_scalar(
            dtv[:, :], dtr[:, :], 0.0, None, ALU.max),
            reads=[r_dtr], writes=[r_dtv])
        em.op("dve", lambda e: e.tensor_tensor(
            dtv[:, :], dtv[:, :], dtt[:, :], ALU.add),
            reads=[r_dtv, r_dtt], writes=[r_dtv])
        em.op("dve", lambda e: e.tensor_scalar(
            dtv[:, :], dtv[:, :], 100.0, None, ALU.min),
            reads=[r_dtv], writes=[r_dtv], dbg="clip")
        em.op("act", lambda e: e.activation(acol[:, 1:2], acol[:, 0:1], AF.Exp),
              reads=[r_acol], writes=[r_acol])
        em.op("dve", lambda e: e.tensor_scalar(
            acol[:, 1:2], acol[:, 1:2], -1.0, None, ALU.mult),
            reads=[r_acol], writes=[r_acol])
        em.op("dve", lambda e: e.tensor_scalar(
            self.aT[:, :], dtv[:, :], acol[:, 1:2], None, ALU.mult),
            reads=[r_dtv, r_acol], writes=[self.r_aT], dbg="aT")
        for c in range(NCH):
            em.op("pe", lambda e, c=c: e.transpose(
                pb[7][0:128, 0:NH], dtv[:, c * CH:(c + 1) * CH],
                self.idf[0:NH, 0:NH]),
                reads=[r_dtv, self.r_idf], writes=[r_pb[7]], dbg=f"trdt{c}")
            em.op("dve", lambda e, c=c: e.tensor_copy(
                self.dt_tok[:, c * NH:(c + 1) * NH], pb[7][0:128, 0:NH]),
                reads=[r_pb[7]], writes=[self.r_dtok], dbg=f"dtokev{c}")

        if "zxbcdt" in self.dbg:
            oz = self.dout("dbg_z", [T, I4], BF16)
            for tt in range(NTT):
                em.op("sp", lambda e, tt=tt, oz=oz: e.dma_start(
                    oz[tt * 128:(tt + 1) * 128, :],
                    self.zs_h[tt * 128:(tt + 1) * 128, :]),
                    reads=[self.r_zs[tt]], dma=True)
            ox = self.dout("dbg_xts", [NCH, I4, CH], BF16)
            em.op("sp", lambda e, ox=ox: e.dma_start(ox, self.xts_h),
                  reads=self.r_xts, dma=True)
            ob = self.dout("dbg_bct", [128, 2 * T], BF16)
            em.op("sp", lambda e, ob=ob: e.dma_start(ob, self.bct[:, :]),
                  reads=self.r_bct, dma=True)
            od = self.dout("dbg_dtv", [NH, T], F32)
            em.op("sp", lambda e, od=od: e.dma_start(od, dtv[:, :]),
                  reads=[r_dtv], dma=True)
        self.close_scope(pas)

    def phaseB(self):
        em = self.em
        import contextlib as _ctx
        pbs = _ctx.ExitStack()
        pb, r_pb = self.pbank, self.r_pb
        epsc = self.constcol(EPS_RMS)
        rcc = self.r_cconst

        def dup(mk, n=2):
            return [mk(i) for i in range(n)]

        xtc = dup(lambda i: self.sb(f"xtc{i}", [128, 32 * CH], BF16, pbs))
        r_xtc = dup(lambda i: em.res(f"xtc{i}"))
        zc0 = self.sb("zc", [128, I4], BF16, pbs)
        zcb = [zc0, zc0]
        r_zc0 = em.res("zc")
        r_zcb = [r_zc0, r_zc0]
        Xcb = dup(lambda i: self.sb(f"Xc{i}", [128, I4], BF16, pbs))
        r_Xcb = dup(lambda i: em.res(f"Xc{i}"))
        Xdb = dup(lambda i: self.sb(f"Xd{i}", [128, I4], BF16, pbs))
        r_Xdb = dup(lambda i: em.res(f"Xd{i}"))
        XDs = self.sb("XD", [128, I4], BF16, pbs)
        XDb = [XDs, XDs]
        r_XDs = em.res("XD")
        r_XDb = [r_XDs, r_XDs]
        P32 = self.sb("P32", [128, I4], F32, pbs)
        P16 = self.sb("P16", [128, I4], BF16, pbs)
        r_P32h = [em.res("P32a"), em.res("P32b")]
        r_P16h = [em.res("P16a"), em.res("P16b")]
        y32 = self.sb("y32", [128, I4], BF16, pbs)
        r_y32 = em.res()
        yg0 = self.sb("yg16", [128, I4], BF16, pbs)
        yg16b = [yg0, yg0]
        r_yg0 = em.res("yg0")
        r_ygb = [r_yg0, r_yg0]
        Dq0 = self.sb("Dq0", [128, 16 * CH], F32, pbs)
        Dq = [Dq0, Dq0]
        r_Dq0 = em.res("Dq0")
        r_Dq = [r_Dq0, r_Dq0]
        Mq = dup(lambda i: self.sb(f"Mq{i}", [128, 16 * CH], BF16, pbs), 8)
        r_Mq = dup(lambda i: em.res(f"Mq{i}"), 8)
        dl1 = dup(lambda i: self.sb(f"dl1_{i}", [128, CH], BF16, pbs), 4)
        r_dl1 = dup(lambda i: em.res(f"dl1_{i}"), 4)
        dr1 = dup(lambda i: self.sb(f"dr1_{i}", [128, 16 * CH], BF16, pbs), 4)
        r_dr1 = dup(lambda i: em.res(f"dr1_{i}"), 4)
        Gmb = dup(lambda i: self.sb(f"Gm{i}", [128, CH], BF16, pbs))
        r_Gmb = dup(lambda i: em.res(f"Gm{i}"))
        stkb = dup(lambda i: self.sb(f"stk{i}", [128, CH], F32, pbs))
        r_stkb = dup(lambda i: em.res(f"stk{i}"))
        atokb = dup(lambda i: self.sb(f"atok{i}", [128, NH], F32, pbs))
        dstokb = dup(lambda i: self.sb(f"dstok{i}", [128, NH], BF16, pbs))
        eatokb = dup(lambda i: self.sb(f"eatok{i}", [128, NH], F32, pbs))
        r_atokb = dup(lambda i: em.res(f"atok{i}"))
        r_dstokb = dup(lambda i: em.res(f"dst{i}"))
        r_eatokb = dup(lambda i: em.res(f"eat{i}"))
        dtdsb = dup(lambda i: self.sb(f"dtds{i}", [128, NH], BF16, pbs))
        r_dtdsb = dup(lambda i: em.res(f"dtds{i}"))
        gbcb = dup(lambda i: self.sb(f"gbc{i}", [128, NH], F32, pbs))
        r_gbcb = dup(lambda i: em.res(f"gbc{i}"))
        asplitb = dup(lambda i: self.sb(f"asplit{i}", [NH, 3 * CH], BF16, pbs))
        r_aspb = dup(lambda i: em.res(f"asplit{i}"))
        rres = self.sb("rres", [NH, CH], F32, pbs)
        r_rres = em.res("rres")
        onesf = self.sb("onesf", [NH, CH], F32, pbs)
        r_onesf = em.res()
        dfull = self.sb("dfull", [128, I4], BF16, pbs)
        r_dfull = em.res("dfull")
        ygs0 = self.sb("ygs0", [128, 1024], BF16, pbs)
        ygstage = [ygs0, ygs0]
        r_ygs0 = em.res("ygs0")
        r_ygst = [r_ygs0, r_ygs0]
        btokb = dup(lambda i: self.sb(f"btok{i}", [128, NS], BF16, pbs))
        r_btokb = dup(lambda i: em.res(f"btok{i}"))
        self.gscr_h = self.dint("gscr", [NCH, NH], F32)
        r_gscr = [em.res(f"g{c}") for c in range(NCH)]
        self.negHE_h = self.din("negHE", [32, 32 * CH])

        em.op("dve", lambda e: e.memset(onesf[:, :], 1.0), writes=[r_onesf])
        em.op("dve", lambda e: e.memset(P32[:, :], 0.0), writes=r_P32h)
        em.op("dve", lambda e: e.memset(P16[:, :], 0.0), writes=r_P16h)
        self.one3_h = self.din("one3", [32, CH])
        for i in range(4):
            em.op("dve", lambda e, i=i: e.memset(dl1[i][:, :], 0.0),
                  writes=[r_dl1[i]])
            em.op("pool", lambda e, i=i: e.dma_start(
                dl1[i][96:128, :], self.one3_h), writes=[r_dl1[i]], dma=True)
        for i in range(4):
            qp = i % 2
            em.op("dve", lambda e, i=i: e.memset(dr1[i][:, :], 0.0),
                  writes=[r_dr1[i]])
            for base in (0, 32, 64):
                em.op("pool", lambda e, i=i, qp=qp, base=base: e.dma_start(
                    dr1[i][base:base + 32, :],
                    self.negHE_h[:, qp * 2048:(qp + 1) * 2048]),
                    writes=[r_dr1[i]], dma=True)
        em.op("pool", lambda e: e.dma_start(
            dfull[:, :], bass.AP(self.drep_h.tensor, 0, [[0, 128], [1, I4]])),
            writes=[r_dfull], dma=True, dbg="dfull")
        em.op("sp", lambda e: e.dma_start(
            xtc[0][:, :].rearrange("p (j t) -> p j t", j=32),
            self.xts_h[0].rearrange("(j p) t -> p j t", j=32)),
            reads=self.r_xts, writes=[r_xtc[0]], dma=True, dbg="ldx0")

        def stageA(c):
            P = c % 2
            stk, r_stk = stkb[P], r_stkb[P]
            atok, dstok, eatok = atokb[P], dstokb[P], eatokb[P]
            r_atok, r_dstok, r_eatok = r_atokb[P], r_dstokb[P], r_eatokb[P]
            asplit, r_asp = asplitb[P], r_aspb[P]
            if c + 1 < NCH:
                em.op("sp", lambda e, c=c, P=P: e.dma_start(
                    xtc[1 - P][:, :].rearrange("p (j t) -> p j t", j=32),
                    self.xts_h[c + 1].rearrange("(j p) t -> p j t", j=32)),
                    reads=self.r_xts, writes=[r_xtc[1 - P]], dma=True,
                    dbg=f"ldx{c+1}")
            em.op("dve", lambda e, c=c, stk=stk: e.tensor_tensor_scan(
                stk[0:NH, :], onesf[:, :], self.aT[:, c * CH:(c + 1) * CH],
                0.0, ALU.mult, ALU.add),
                reads=[r_onesf, self.r_aT], writes=[r_stk], dbg=f"cums{c}")
            em.op("dve", lambda e, stk=stk: e.tensor_scalar(
                stk[64:128, :], stk[0:NH, :], stk[0:NH, CH - 1:CH], None,
                ALU.subtract),
                reads=[r_stk], writes=[r_stk], dbg="dsT")
            em.op("dve", lambda e, stk=stk, asplit=asplit: e.tensor_copy(
                asplit[:, 0:CH], stk[0:NH, :]),
                reads=[r_stk], writes=[r_asp], dbg=f"a1_{c}")
            em.op("dve", lambda e, stk=stk, asplit=asplit: e.tensor_tensor(
                rres[:, :], stk[0:NH, :], asplit[:, 0:CH], ALU.subtract),
                reads=[r_stk, r_asp], writes=[r_rres], dbg=f"r2_{c}")
            em.op("dve", lambda e, asplit=asplit: e.tensor_copy(
                asplit[:, CH:2 * CH], rres[:, :]),
                reads=[r_rres], writes=[r_asp], dbg=f"a2_{c}")
            em.op("dve", lambda e, asplit=asplit: e.tensor_tensor(
                rres[:, :], rres[:, :], asplit[:, CH:2 * CH], ALU.subtract),
                reads=[r_rres, r_asp], writes=[r_rres], dbg=f"r3_{c}")
            em.op("dve", lambda e, asplit=asplit: e.tensor_copy(
                asplit[:, 2 * CH:3 * CH], rres[:, :]),
                reads=[r_rres], writes=[r_asp], dbg=f"a3_{c}")
            em.op("sp", lambda e, c=c, asplit=asplit: e.dma_start(
                self.acr_h[c].rearrange("i (h l) -> h i l", h=NH),
                asplit[:, :].rearrange("p (i l) -> p i l", i=3)),
                reads=[r_asp], writes=[self.r_acr[c]], dma=True, dbg=f"acr{c}")
            em.op("pe", lambda e, stk=stk: e.transpose(
                pb[0][:, 0:128], stk[:, :], self.idf[:, :]),
                reads=[r_stk, self.r_idf], writes=[r_pb[0]], dbg=f"trstk{c}")
            em.op("act", lambda e, atok=atok: e.activation(
                atok[:, :], pb[0][:, 0:NH], AF.Copy),
                reads=[r_pb[0]], writes=[r_atok], dbg="atok")
            em.op("act", lambda e, dstok=dstok: e.activation(
                dstok[:, :], pb[0][:, NH:128], AF.Exp, scale=-1.0),
                reads=[r_pb[0], r_atok], writes=[r_dstok], dbg="dstok")
            em.op("act", lambda e, eatok=eatok, atok=atok: e.activation(
                eatok[:, :], atok[:, :], AF.Exp),
                reads=[r_atok], writes=[r_eatok], dbg="eatok")
            em.op("dve", lambda e, c=c, P=P, dstok=dstok: e.tensor_tensor(
                dtdsb[P][:, :], self.dt_tok[:, c * NH:(c + 1) * NH],
                dstok[:, :], ALU.mult),
                reads=[self.r_dtok, r_dstok], writes=[r_dtdsb[P]],
                dbg=f"dtds{c}")
            em.op("pool", lambda e, c=c, stk=stk: e.dma_start(
                self.gscr_h[c:c + 1, :].rearrange("o h -> h o"),
                stk[0:NH, CH - 1:CH]),
                reads=[r_stk], writes=[r_gscr[c]], dma=True, dbg=f"gst{c}")
            em.op("pool", lambda e, c=c, P=P: e.dma_start(
                gbcb[P][:, :],
                bass.AP(self.gscr_h.tensor, c * NH, [[0, 128], [1, NH]])),
                reads=[r_gscr[c]], writes=[r_gbcb[P]], dma=True, dbg=f"gbc{c}")
            em.op("act", lambda e, P=P: e.activation(
                gbcb[P][:, :], gbcb[P][:, :], AF.Exp),
                reads=[r_gbcb[P]], writes=[r_gbcb[P]], dbg=f"gexp{c}")
            em.op("pe", lambda e, c=c: e.transpose(
                pb[0][:, 0:128].bitcast(BF16)[:, 0:128],
                self.bct[:, c * CH:(c + 1) * CH], self.id16[:, :]),
                reads=[self.r_bct[0], self.r_id16], writes=[r_pb[0]],
                dbg=f"trB{c}")
            em.op("act", lambda e, P=P: e.activation(
                btokb[P][:, :], pb[0][:, 0:128].bitcast(BF16)[:, 0:128],
                AF.Copy),
                reads=[r_pb[0]], writes=[r_btokb[P]], dbg="btok")
            em.op("pe", lambda e, c=c: e.matmul(
                pb[1][:, 0:128], self.bct[:, c * CH:(c + 1) * CH],
                self.bct[:, T + c * CH:T + (c + 1) * CH],
                start=True, stop=True),
                reads=self.r_bct, writes=[r_pb[1]], dbg=f"G{c}")
            em.op("dve", lambda e, P=P: e.tensor_tensor(
                Gmb[P][:, :], pb[1][:, 0:128], self.tril[:, :], ALU.mult),
                reads=[r_pb[1], self.r_tril], writes=[r_Gmb[P]], dbg="Gm")

        def stageC(c):
            P = c % 2
            xb, r_xb = xtc[P], r_xtc[P]
            Xc, r_Xc = Xcb[P], r_Xcb[P]
            Xd, r_Xd = Xdb[P], r_Xdb[P]
            XD, r_XD = XDb[P], r_XDb[P]
            asplit, r_asp = asplitb[P], r_aspb[P]
            dstok, r_dstok = dstokb[P], r_dstokb[P]
            for gg in range(4):
                bp = (gg % 2) * 2
                for sub in range(2):
                    g = gg * 2 + sub
                    pv = pb[bp + sub][:, :].bitcast(BF16)

                    def trx(e, g=g, pv=pv, xb=xb):
                        last = None
                        for j in range(4):
                            last = e.transpose(
                                pv[:, j * 128:(j + 1) * 128],
                                xb[:, (g * 4 + j) * CH:(g * 4 + j + 1) * CH],
                                self.id16[:, :])
                        return last
                    em.op("pe", trx, reads=[r_xb, self.r_id16],
                          writes=[r_pb[bp + sub]], dbg=f"trx{c}_{g}")
                pvb = self.pview[:, :].bitcast(BF16).rearrange(
                    "p (b x) -> p b x", b=8)[:, bp:bp + 2, 0:512]
                wide4 = pvb.rearrange("p b (h q) -> p b h q", h=8)
                sl = slice(gg * 1024, (gg + 1) * 1024)
                em.op("dve", lambda e, gg=gg, sl=sl, wide4=wide4, c=c, Xc=Xc:
                      e.tensor_tensor(
                      Xc[:, sl].rearrange("p (b h q) -> p b h q", b=2, h=8),
                      wide4,
                      self.dt_tok[:, c * NH + gg * 16:c * NH + (gg + 1) * 16]
                      .rearrange("p (b h) -> p b h", b=2)
                      .unsqueeze(3).broadcast_to([128, 2, 8, PP]),
                      ALU.mult),
                      reads=[r_pb[bp], r_pb[bp + 1], self.r_dtok],
                      writes=[r_Xc], dbg=f"xc{c}_{gg}")
                em.op("dve", lambda e, sl=sl, pvb=pvb, XD=XD: e.tensor_tensor(
                    XD[:, sl].rearrange("p (b x) -> p b x", b=2),
                    pvb, dfull[:, sl].rearrange("p (b x) -> p b x", b=2),
                    ALU.mult),
                    reads=[r_pb[bp], r_pb[bp + 1], r_dfull], writes=[r_XD],
                    dbg=f"xD{c}_{gg}")
                em.op("dve", lambda e, gg=gg, sl=sl, Xd=Xd, Xc=Xc, P=P:
                      e.tensor_tensor(
                      Xd[:, sl].rearrange("p (h q) -> p h q", h=16),
                      Xc[:, sl].rearrange("p (h q) -> p h q", h=16),
                      dstokb[P][:, gg * 16:(gg + 1) * 16]
                      .unsqueeze(2).broadcast_to([128, 16, PP]),
                      ALU.mult),
                      reads=[r_Xc, r_dstokb[P]], writes=[r_Xd],
                      dbg=f"xd{c}_{gg}")


        def decay_chunk(c):
            P = c % 2
            asplit, r_asp = asplitb[P], r_aspb[P]
            for q in range(4):
                half = q // 2
                l1, r_l1 = dl1[P * 2 + half], r_dl1[P * 2 + half]
                if q % 2 == 0:
                    h0 = half * 32
                    for i3, base in ((0, 0), (1, 32), (2, 64)):
                        em.op("dve", lambda e, h0=h0, l1=l1, i3=i3, base=base,
                              asplit=asplit: e.tensor_copy(
                              l1[base:base + 32, :],
                              asplit[h0:h0 + 32, i3 * CH:(i3 + 1) * CH]),
                              reads=[r_asp], writes=[r_l1],
                              dbg=f"dl{i3}_{c}_{half}")
                di = (c % 2) * 2 + (q % 2)
                rb1, r_rb1 = dr1[di], r_dr1[di]
                for i3 in range(3):
                    em.op("pool", lambda e, c=c, q=q, rb1=rb1, i3=i3:
                          e.dma_start(
                          rb1[96 + i3:97 + i3, :],
                          self.acr_h[c, i3, q * 16 * CH:(q + 1) * 16 * CH]
                          .unsqueeze(0)),
                          reads=[self.r_acr[c]], writes=[r_rb1], dma=True,
                          dbg=f"ar{i3}_{c}_{q}")
                db_, r_db = Dq[q % 2], r_Dq[q % 2]
                mb = Mq[P * 4 + q]
                r_mb = r_Mq[P * 4 + q]
                for dd in range(2):
                    for sub in range(2):
                        ssl = dd * 2 + sub
                        bk = 2 + sub
                        cs = slice(ssl * 512, (ssl + 1) * 512)

                        def dmm(e, cs=cs, bk=bk, l1=l1, rb1=rb1):
                            return e.matmul(pb[bk][:, :], l1[:, :],
                                            rb1[:, cs], start=True, stop=True)
                        em.op("pe", dmm,
                              reads=[r_l1, r_rb1],
                              writes=[r_pb[bk]], dbg=f"dmm{c}_{q}_{ssl}")
                    em.op("act", lambda e, dd=dd, db_=db_: e.activation(
                        db_[:, dd * 1024:(dd + 1) * 1024],
                        self.pview[:, 1024:2048], AF.Relu, scale=-1.0),
                        reads=[r_pb[2], r_pb[3]], writes=[r_db],
                        dbg=f"dev{c}_{q}_{dd}")
                em.op("act", lambda e, db_=db_, mb=mb: e.activation(
                    mb[:, :], db_[:, :], AF.Exp, scale=-1.0),
                    reads=[r_db], writes=[r_mb], dbg=f"E{c}_{q}")
                em.op("dve",
                      lambda e, mb=mb, P=P: e.tensor_tensor(
                      mb[:, :].rearrange("p (h l) -> p h l", h=16),
                      mb[:, :].rearrange("p (h l) -> p h l", h=16),
                      Gmb[P][:, :].unsqueeze(1).broadcast_to([128, 16, CH]),
                      ALU.mult),
                      reads=[r_mb, r_Gmb[P]], writes=[r_mb], dbg=f"M{c}_{q}")

        def stageB(c):
            P = c % 2
            Xc, r_Xc = Xcb[P], r_Xcb[P]
            XD, r_XD = XDb[P], r_XDb[P]
            eatok, r_eatok = eatokb[P], r_eatokb[P]
            pbank_h = [pb[4 + i] for i in range(4)]
            r_pbh = [r_pb[4 + i] for i in range(4)]
            em.op("sp", lambda e, c=c: e.dma_start(
                zcb[0][:, :], self.zs_h[c * CH:(c + 1) * CH, :]),
                reads=[self.r_zs[c]], writes=[r_zcb[0]], dma=True,
                dbg=f"ldz{c}")
            for half in range(2):
                def yoff(e, c=c, half=half, pbank_h=pbank_h):
                    last = None
                    for n in range(4):
                        last = e.matmul(
                            pbank_h[n][:, :],
                            self.bct[:, T + c * CH:T + (c + 1) * CH],
                            P16[:, half * 2048 + n * 512:
                                half * 2048 + (n + 1) * 512],
                            start=True, stop=True)
                    return last
                em.op("pe", yoff, reads=[self.r_bct[1], r_P16h[half]],
                      writes=r_pbh, dbg=f"yoff{c}_{half}")
                em.op("dve", lambda e, half=half, P=P: e.tensor_tensor(
                    P32[:, half * 2048:(half + 1) * 2048]
                    .rearrange("p (h q) -> p h q", h=32),
                    P32[:, half * 2048:(half + 1) * 2048]
                    .rearrange("p (h q) -> p h q", h=32),
                    gbcb[P][:, half * 32:(half + 1) * 32].unsqueeze(2)
                    .broadcast_to([128, 32, PP]),
                    ALU.mult),
                    reads=[r_P32h[half], r_gbcb[P]], writes=[r_P32h[half]],
                    dbg=f"Pg{c}_{half}")
                em.op("dve", lambda e, half=half, eatok=eatok:
                      e.tensor_tensor(
                      y32[:, half * 2048:(half + 1) * 2048]
                      .rearrange("p (h q) -> p h q", h=32),
                      self.pview[:, 2048:4096]
                      .rearrange("p (h q) -> p h q", h=32),
                      eatok[:, half * 32:(half + 1) * 32]
                      .unsqueeze(2).broadcast_to([128, 32, PP]),
                      ALU.mult),
                      reads=r_pbh + [r_eatok], writes=[r_y32],
                      dbg=f"ev1_{c}_{half}")

                def yd(e, c=c, half=half, pbank_h=pbank_h, Xc=Xc, XD=XD, P=P):
                    last = None
                    for n in range(4):
                        last = e.matmul(
                            pbank_h[n][:, :], self.id16[:, :],
                            XD[:, half * 2048 + n * 512:
                               half * 2048 + (n + 1) * 512],
                            start=True, stop=False)
                    for hh in range(32):
                        h = half * 32 + hh
                        qq, hq = divmod(hh, 16)
                        n, off = divmod(hh * PP, 512)
                        last = e.matmul(
                            pbank_h[n][:, off:off + PP],
                            Mq[P * 4 + half * 2 + qq][:, hq * CH:(hq + 1) * CH],
                            Xc[:, h * PP:(h + 1) * PP],
                            start=False, stop=(off + PP == 512))
                    return last
                em.op("pe", yd,
                      reads=[self.r_id16, r_XD, r_Xc,
                             r_Mq[P * 4 + half * 2], r_Mq[P * 4 + half * 2 + 1]],
                      writes=r_pbh, dbg=f"yd{c}_{half}")
                em.op("dve", lambda e, half=half: e.scalar_tensor_tensor(
                    y32[:, half * 2048:(half + 1) * 2048],
                    self.pview[:, 2048:4096], 1.0,
                    y32[:, half * 2048:(half + 1) * 2048],
                    ALU.mult, ALU.add),
                    reads=r_pbh + [r_y32], writes=[r_y32],
                    dbg=f"ev2_{c}_{half}")
                stk_, r_stk_ = stkb[P], r_stkb[P]
                em.op("act", lambda e, half=half, stk_=stk_, P=P: e.activation(
                    yg16b[P][:, half * 2048:(half + 1) * 2048],
                    y32[:, half * 2048:(half + 1) * 2048], AF.Square,
                    accum_out=stk_[0:128, 2 + half:3 + half]),
                    reads=[r_y32], writes=[r_ygb[P], r_stk_],
                    dbg=f"ssh{c}_{half}")

        def stageD(c):
            P = c % 2
            Xd, r_Xd = Xdb[P], r_Xdb[P]
            stk, r_stk = stkb[P], r_stkb[P]
            yg16, r_yg = yg16b[P], r_ygb[P]
            pbank_h = [pb[4 + i] for i in range(4)]
            r_pbh = [r_pb[4 + i] for i in range(4)]
            for half in range(2):
                def stmm(e, half=half, pbank_h=pbank_h, Xd=Xd, P=P):
                    last = None
                    for n in range(4):
                        last = e.matmul(
                            pbank_h[n][:, :], btokb[P][:, :],
                            Xd[:, half * 2048 + n * 512:
                               half * 2048 + (n + 1) * 512],
                            start=True, stop=True)
                    return last
                em.op("pe", stmm, reads=[r_btokb[P], r_Xd], writes=r_pbh,
                      dbg=f"st{c}_{half}")
                em.op("dve", lambda e, half=half: e.scalar_tensor_tensor(
                    P16[:, half * 2048:(half + 1) * 2048],
                    self.pview[:, 2048:4096], 1.0,
                    P32[:, half * 2048:(half + 1) * 2048],
                    ALU.mult, ALU.add),
                    reads=r_pbh + [r_P32h[half]], writes=[r_P16h[half]],
                    dbg=f"P16{c}_{half}")
                em.op("dve", lambda e, half=half: e.tensor_tensor(
                    P32[:, half * 2048:(half + 1) * 2048],
                    P32[:, half * 2048:(half + 1) * 2048],
                    self.pview[:, 2048:4096], ALU.add),
                    reads=r_pbh + [r_P32h[half]], writes=[r_P32h[half]],
                    dbg=f"Pa{c}_{half}")

            em.op("dve", lambda e, stk=stk: e.tensor_scalar(
                stk[:, 1:2], stk[:, 2:3], stk[:, 3:4], 1.0 / I4, ALU.add,
                ALU.mult),
                reads=[r_stk], writes=[r_stk], dbg=f"sscomb{c}")
            em.op("act", lambda e, stk=stk: e.activation(
                stk[:, 1:2], stk[:, 1:2], AF.Ln, bias=epsc),
                reads=[r_stk, rcc], writes=[r_stk])
            em.op("act", lambda e, stk=stk: e.activation(
                stk[:, 1:2], stk[:, 1:2], AF.Exp, scale=-0.5),
                reads=[r_stk], writes=[r_stk], dbg=f"rstd{c}")
            em.op("dve", lambda e, yg16=yg16, stk=stk, P=P:
                  e.scalar_tensor_tensor(
                  yg16[:, :], y32[:, :], stk[:, 1:2], zcb[P][:, :],
                  ALU.mult, ALU.mult),
                  reads=[r_y32, r_stk, r_zcb[P]], writes=[r_yg],
                  dbg=f"ygfuse{c}")
            for gg in range(4):
                for sub in range(2):
                    g = gg * 2 + sub
                    pv = pb[2 + sub][:, :].bitcast(BF16)

                    def tryg(e, g=g, pv=pv, yg16=yg16):
                        last = None
                        for j in range(4):
                            last = e.transpose(
                                pv[:, j * 128:(j + 1) * 128],
                                yg16[:, (g * 4 + j) * 128:
                                     (g * 4 + j + 1) * 128],
                                self.id16[:, :])
                        return last
                    em.op("pe", tryg, reads=[r_yg, self.r_id16],
                          writes=[r_pb[2 + sub]], dbg=f"tryg{c}_{g}")
                pvb = self.pview[:, :].bitcast(BF16).rearrange(
                    "p (b x) -> p b x", b=8)[:, 2:4, 0:512]
                sb_, r_sb = ygstage[gg % 2], r_ygst[gg % 2]
                em.op("act", lambda e, pvb=pvb, sb_=sb_: e.activation(
                    sb_[:, :].rearrange("p (b x) -> p b x", b=2), pvb,
                    AF.Copy),
                    reads=[r_pb[2], r_pb[3]], writes=[r_sb],
                    dbg=f"evyg{c}_{gg}")
                em.op("sp", lambda e, c=c, gg=gg, sb_=sb_: e.dma_start(
                    self.ygt_h[c, gg * 1024:(gg + 1) * 1024, :]
                    .rearrange("(j p) t -> p j t", j=8),
                    sb_[:, :].rearrange("p (j t) -> p j t", j=8)),
                    reads=[r_sb], writes=[self.r_ygt[c]], dma=True,
                    dbg=f"styg{c}_{gg}")

        nch = NCH
        for d in self.dbg:
            if d.startswith("nch"):
                nch = int(d[3:])
        stageA(0)
        stageC(0)
        decay_chunk(0)
        for k in range(1, nch):
            stageA(k)
            stageB(k - 1)
            stageC(k)
            decay_chunk(k)
            stageD(k - 1)
        stageB(nch - 1)
        stageD(nch - 1)

        if "yg" in self.dbg:
            o = self.dout("dbg_ygt", [NCH, I4, CH], BF16)
            em.op("sp", lambda e, o=o: e.dma_start(o, self.ygt_h),
                  reads=self.r_ygt, dma=True)
        self.close_scope(pbs)

    def phaseD(self):
        em = self.em
        import contextlib as _ctx
        pds = _ctx.ExitStack()
        pb, r_pb = self.pbank, self.r_pb

        wo16 = self.sb("wo16", [128, 32 * D], BF16, pds)
        r_wo = [em.res("wo16a"), em.res("wo16b")]
        nwcol = self.sb("nwcol", [128, 32], F32, pds)
        r_nw = em.res("nw")
        ygcol = [self.sb(f"ygc{b}", [128, 32 * 128], BF16, pds) for b in range(2)]
        r_ygc = [em.res(f"ygc{b}") for b in range(2)]
        x_t = [self.sb(f"xd{b}", [128, D], F32, pds) for b in range(2)]
        r_xt = [em.res(f"xd{b}") for b in range(2)]
        f_t = [self.sb(f"fd{b}", [128, D], BF16, pds) for b in range(2)]
        r_ft = [em.res(f"fd{b}") for b in range(2)]
        o_t = [self.sb(f"od{b}", [128, D], F32, pds) for b in range(2)]
        r_ot = [em.res(f"od{b}") for b in range(2)]

        em.op("sp", lambda e: e.dma_start(nwcol[:, :], self.nw_h),
              writes=[r_nw], dma=True)
        for dh in range(2):
            em.op("pool", lambda e, dh=dh: e.dma_start(
                wo16[:, :].rearrange("p (j x) -> p j x", j=32)[
                    :, :, dh * 1024:(dh + 1) * 1024],
                self.woutT_h.rearrange("(j p) d -> p j d", j=32)[
                    :, :, dh * 1024:(dh + 1) * 1024]),
                writes=[r_wo[dh]], dma=True, dbg=f"ldwo{dh}")
        for j in range(32):
            for dh in range(2):
                em.op("dve", lambda e, j=j, dh=dh: e.tensor_scalar(
                    wo16[:, j * D + dh * 1024:j * D + (dh + 1) * 1024],
                    wo16[:, j * D + dh * 1024:j * D + (dh + 1) * 1024],
                    nwcol[:, j:j + 1], None, ALU.mult),
                    reads=[r_wo[dh], r_nw], writes=[r_wo[dh]],
                    dbg=f"nwfold{j}_{dh}")

        em.op("sp", lambda e: e.dma_start(
            ygcol[0][:, :].rearrange("p (j t) -> p j t", j=32),
            self.ygt_h[0].rearrange("(j p) t -> p j t", j=32)),
            reads=[self.r_ygt[0]], writes=[r_ygc[0]], dma=True)
        em.op("sp", lambda e: e.dma_start(x_t[0][:, :], self.x_h[0:128, :]),
              writes=[r_xt[0]], dma=True)
        em.op("sp", lambda e: e.dma_start(f_t[0][:, :], self.ffn_h[0:128, :]),
              reads=[self.r_ffn[0]], writes=[r_ft[0]], dma=True)

        for tt in range(NTT):
            bi = tt % 2
            yb, r_yb = ygcol[bi], r_ygc[bi]
            xb, r_xb = x_t[bi], r_xt[bi]
            fb, r_fb = f_t[bi], r_ft[bi]
            ob, r_ob = o_t[bi], r_ot[bi]
            if tt + 1 < NTT:
                em.op("sp", lambda e, tt=tt, bi=bi: e.dma_start(
                    ygcol[1 - bi][:, :].rearrange("p (j t) -> p j t", j=32),
                    self.ygt_h[tt + 1].rearrange("(j p) t -> p j t", j=32)),
                    reads=[self.r_ygt[tt + 1]], writes=[r_ygc[1 - bi]],
                    dma=True)
                em.op("sp", lambda e, tt=tt, bi=bi: e.dma_start(
                    x_t[1 - bi][:, :],
                    self.x_h[(tt + 1) * 128:(tt + 2) * 128, :]),
                    writes=[r_xt[1 - bi]], dma=True)
                em.op("sp", lambda e, tt=tt, bi=bi: e.dma_start(
                    f_t[1 - bi][:, :],
                    self.ffn_h[(tt + 1) * 128:(tt + 2) * 128, :]),
                    reads=[self.r_ffn[tt + 1]], writes=[r_ft[1 - bi]],
                    dma=True)
            for dsl in range(4):
                bk = dsl
                dh = dsl // 2
                def omm(e, yb=yb, dsl=dsl, bk=bk):
                    last = None
                    for j in range(32):
                        last = e.matmul(
                            pb[bk][:, :],
                            yb[:, j * 128:(j + 1) * 128],
                            wo16[:, :].rearrange("p (j d) -> p j d", j=32)[
                                :, j, dsl * 512:(dsl + 1) * 512],
                            start=(j == 0), stop=(j == 31))
                    return last
                em.op("pe", omm, reads=[r_yb, r_wo[dh]], writes=[r_pb[bk]],
                      dbg=f"omm{tt}_{dsl}")
                sl = slice(dsl * 512, (dsl + 1) * 512)
                em.op("dve", lambda e, sl=sl, bk=bk, ob=ob, xb=xb:
                      e.scalar_tensor_tensor(
                          ob[:, sl], pb[bk][:, :], 1.0, xb[:, sl],
                          ALU.mult, ALU.add),
                      reads=[r_pb[bk], r_xb], writes=[r_ob],
                      dbg=f"oev{tt}_{dsl}")
            em.op("dve", lambda e, ob=ob, fb=fb: e.tensor_tensor(
                ob[:, :], ob[:, :], fb[:, :], ALU.add),
                reads=[r_ob, r_fb], writes=[r_ob], dbg=f"offn{tt}")
            em.op("sp", lambda e, tt=tt, ob=ob: e.dma_start(
                self.out_h[tt * 128:(tt + 1) * 128, :], ob[:, :]),
                reads=[r_ob], writes=[self.r_out], dma=True, dbg=f"out{tt}")
        self.close_scope(pds)

    def phaseF(self):
        em = self.em
        import contextlib as _ctx
        pfs = _ctx.ExitStack()
        pb, r_pb = self.pbank, self.r_pb
        HT = T // 2
        NTH = NTT // 2

        acc16 = self.sb("acc16", [128, NTH * D], BF16, pfs)
        r_acc = [em.res(f"acc{t}") for t in range(NTH)]
        h16b = [self.sb(f"h16_{b}", [128, 8 * HT], BF16, pfs) for b in range(2)]
        r_h16b = [[em.res(f"h16_{b}_{j}") for j in range(8)] for b in range(2)]
        srowb = [self.sb(f"srow{b}", [128, HT], BF16, pfs) for b in range(2)]
        r_srowb = [em.res(f"srow{b}") for b in range(2)]
        wd16 = self.sb("wd16", [128, 8 * D], BF16, pfs)
        r_wd = em.res("wd16")
        wupb = [self.sb(f"wupb{b}", [128, KT * 128], BF16, pfs) for b in range(2)]
        r_wupb = [em.res(f"wupb{b}") for b in range(2)]
        bupc = self.sb("bupc_s", [128, EXP * 8], F32, pfs)
        r_bup = em.res("bupc")
        bdn16 = self.sb("bdn16", [EXP, D], BF16, pfs)
        r_bdn = em.res("bdn16")

        self.bupc_h = self.din("bupc", [128, EXP * 8])
        self.bdn_h = self.din("bdn", [EXP, D])
        em.op("sp", lambda e: e.dma_start(bupc[:, :], self.bupc_h),
              writes=[r_bup], dma=True)
        em.op("pool", lambda e: e.dma_start(bdn16[:, :], self.bdn_h),
              writes=[r_bdn], dma=True)

        for th in range(2):
            t_base = th * HT
            for e_i in range(EXP):
                h16 = h16b[e_i % 2]
                r_h16 = r_h16b[e_i % 2]
                srow, r_srow = srowb[e_i % 2], r_srowb[e_i % 2]
                em.op("sp", lambda e, e_i=e_i, t_base=t_base, srow=srow:
                    e.dma_start(
                    srow[:, :],
                    bass.AP(self.scoh_h.tensor, e_i * T + t_base,
                            [[0, 128], [1, HT]])),
                    reads=[self.r_scoh], writes=[r_srow], dma=True,
                    dbg=f"srow{th}_{e_i}")
                for ht in range(8):
                    wb, r_wb = wupb[ht % 2], r_wupb[ht % 2]
                    c0 = e_i * HE + ht * 128
                    em.op("pool", lambda e, c0=c0, wb=wb: e.dma_start(
                        wb[:, :].rearrange("p (k c) -> p k c", k=KT),
                        self.wupT_h.rearrange("(k p) c -> p k c", k=KT)[
                            :, :, c0:c0 + 128]),
                        writes=[r_wb], dma=True, dbg=f"wup{th}_{e_i}_{ht}")
                    if ht == 2:
                        em.op("pool", lambda e, e_i=e_i: e.dma_start(
                            wd16[:, :].rearrange("p (j d) -> p j d", j=8),
                            self.wdT_h[e_i].rearrange("(j p) d -> p j d", j=8)),
                            writes=[r_wd], dma=True, dbg=f"wd{th}_{e_i}")
                    for n in range(2):
                        bi = 6 + n % 2
                        def up_fn(e, wb=wb, n=n, bi=bi, t_base=t_base):
                            last = None
                            for k in range(KT):
                                last = e.matmul(
                                    pb[bi][:, :],
                                    wb[:, :].rearrange(
                                        "p (k c) -> p k c", k=KT)[:, k],
                                    self.xnTb[:, :].rearrange(
                                        "p (k t) -> p k t", k=KT)[
                                        :, k,
                                        t_base + n * 512:t_base + (n + 1) * 512],
                                    start=(k == 0), stop=(k == KT - 1))
                            return last
                        em.op("pe", up_fn, reads=[r_wb] + self.r_xnT,
                              writes=[r_pb[bi]], dbg=f"up{th}_{e_i}_{ht}_{n}")
                        em.op("act", lambda e, ht=ht, n=n, e_i=e_i, bi=bi,
                              h16=h16: e.activation(
                                  h16[:, ht * HT + n * 512:
                                      ht * HT + (n + 1) * 512],
                                  pb[bi][:, :], AF.Silu,
                                  bias=bupc[:, e_i * 8 + ht:e_i * 8 + ht + 1]),
                              reads=[r_pb[bi], r_bup], writes=[r_h16[ht]],
                              dbg=f"silu{th}_{e_i}_{ht}_{n}")
                    em.op("dve", lambda e, ht=ht, h16=h16, srow=srow:
                        e.tensor_tensor(
                        h16[:, ht * HT:(ht + 1) * HT],
                        h16[:, ht * HT:(ht + 1) * HT],
                        srow[:, :], ALU.mult),
                        reads=[r_h16[ht], r_srow], writes=[r_h16[ht]],
                        dbg=f"hsc{th}_{e_i}_{ht}")

                for lt in range(NTH):
                    tt = th * NTH + lt
                    for pr in range(2):
                        for sub in range(2):
                            dsl = pr * 2 + sub
                            bi = 2 + pr * 2 + sub
                            def dn_fn(e, lt=lt, tt=tt, dsl=dsl, e_i=e_i,
                                      bi=bi, h16=h16):
                                last = None
                                if e_i == 0:
                                    last = e.matmul(
                                        pb[bi][:, :],
                                        self.scoT[0:8, tt * 128:(tt + 1) * 128],
                                        bdn16[0:8, dsl * 512:(dsl + 1) * 512],
                                        start=True, stop=False)
                                for j in range(8):
                                    last = e.matmul(
                                        pb[bi][:, :],
                                        h16[:, j * HT + lt * 128:
                                            j * HT + (lt + 1) * 128],
                                        wd16[:, :].rearrange(
                                            "p (j d) -> p j d", j=8)[
                                            :, j, dsl * 512:(dsl + 1) * 512],
                                        start=(e_i != 0 and j == 0),
                                        stop=(j == 7))
                                return last
                            em.op("pe", dn_fn,
                                  reads=[r_wd, self.r_scoT, r_bdn] + r_h16,
                                  writes=[r_pb[bi]],
                                  dbg=f"dn{th}_{e_i}_{lt}_{dsl}")
                        dst = acc16[:, lt * D + pr * 1024:
                                    lt * D + (pr + 1) * 1024]
                        rpair = [r_pb[2 + pr * 2], r_pb[3 + pr * 2]]
                        src = self.pview[:, 1024 + pr * 1024:
                                         2048 + pr * 1024]
                        if e_i == 0:
                            em.op("dve", lambda e, dst=dst, src=src:
                                  e.tensor_copy(dst, src),
                                  reads=rpair, writes=[r_acc[lt]],
                                  dbg=f"ev0_{th}_{lt}_{pr}")
                        else:
                            em.op("dve", lambda e, dst=dst, src=src:
                                  e.scalar_tensor_tensor(
                                      dst, src, 1.0, dst, ALU.mult, ALU.add),
                                  reads=rpair + [r_acc[lt]],
                                  writes=[r_acc[lt]],
                                  dbg=f"ev{e_i}_{th}_{lt}_{pr}")

            for lt in range(NTH):
                tt = th * NTH + lt
                em.op("sp", lambda e, lt=lt, tt=tt: e.dma_start(
                    self.ffn_h[tt * 128:(tt + 1) * 128, :],
                    acc16[:, lt * D:(lt + 1) * D]),
                    reads=[r_acc[lt]], writes=[self.r_ffn[tt]], dma=True,
                    dbg=f"stffn{tt}")

        if "ffn" in self.dbg:
            o = self.dout("dbg_ffn", [T, D], BF16)
            for tt in range(NTT):
                em.op("sp", lambda e, tt=tt, o=o: e.dma_start(
                    o[tt * 128:(tt + 1) * 128, :],
                    self.ffn_h[tt * 128:(tt + 1) * 128, :]),
                    reads=[self.r_ffn[tt]], dma=True)
        self.close_scope(pfs)

    def phase0(self):
        em = self.em
        import contextlib as _ctx
        p0s = _ctx.ExitStack()
        self.xnTb = self.sb("xnTb", [128, KT * T], BF16, self.xns)
        self.r_xnT = [em.res(f"xnT{k}") for k in range(KT)]
        self.scoT = self.sb("scoT", [EXP, T], BF16, self.xns)
        self.r_scoT = em.res("scoT")

        def xnT(k):
            return self.xnTb[:, k * T:(k + 1) * T]

        self.xnT = xnT

        x_tile = [self.sb(f"xt{b}", [128, D], F32, p0s) for b in range(2)]
        r_xtile = [em.res(f"xt{b}") for b in range(2)]
        sq_scr = self.sb("sqscr", [128, D], BF16, p0s)
        r_sq = em.res("sq")
        statsb = [self.sb(f"stats{b}", [128, 8], F32, p0s) for b in range(2)]
        r_statsb = [em.res(f"stats{b}") for b in range(2)]
        xn_stageb = [self.sb(f"xn_stage{b}", [128, D], BF16, p0s)
                     for b in range(2)]
        lo_stageb = [self.sb(f"lo_stage{b}", [128, D], BF16, p0s)
                     for b in range(2)]
        loTb = [self.sb(f"loT{b}", [128, KT * 128], BF16, p0s)
                for b in range(2)]
        r_xnstb = [em.res(f"xnst{b}") for b in range(2)]
        r_lostb = [em.res(f"lost{b}") for b in range(2)]
        r_loTb = [em.res(f"loT{b}") for b in range(2)]

        wg1f = self.sb("wg1f", [128, KT * EXP], F32, p0s)
        wg1hi = self.sb("wg1hi", [128, KT * EXP], BF16, p0s)
        wg1lo = self.sb("wg1lo", [128, KT * EXP], BF16, p0s)
        r_wg1, r_wg1h, r_wg1l = em.res(), em.res(), em.res()
        wg2f = self.sb("wg2f", [EXP, EXP], F32, p0s)
        r_wg2 = em.res()
        em.op("sp", lambda e: e.dma_start(
            wg1f[:, :].rearrange("p (k e) -> p k e", k=KT),
            self.wg1T_h.rearrange("(k p) e -> p k e", k=KT)),
            writes=[r_wg1], dma=True)
        em.op("sp", lambda e: e.dma_start(wg2f[:, :], self.wg2T_h),
              writes=[r_wg2], dma=True)
        em.op("dve", lambda e: e.tensor_copy(wg1hi[:, :], wg1f[:, :]),
              reads=[r_wg1], writes=[r_wg1h])
        em.op("dve", lambda e: e.tensor_tensor(
            wg1lo[:, :], wg1f[:, :], wg1hi[:, :], ALU.subtract),
            reads=[r_wg1, r_wg1h], writes=[r_wg1l])

        l1T = self.sb("l1T", [EXP, T], F32, p0s)
        r_l1 = em.res("l1T")
        l2T = self.sb("l2T", [EXP, T], F32, p0s)
        r_l2 = em.res("l2T")
        lgt = self.sb("lgt", [128, 8], F32, p0s)
        r_lgt = em.res("lgt")
        tkb = self.sb("tkb", [128, 40], F32, p0s)
        r_tk = em.res("tk")

        pb, r_pb = self.pbank, self.r_pb
        epsc = self.constcol(EPS_LN)
        rcc = self.r_cconst

        for tt in range(NTT):
            xb, r_xb = x_tile[tt % 2], r_xtile[tt % 2]
            stats, r_stats = statsb[tt % 2], r_statsb[tt % 2]
            xn_stage, r_xnst = xn_stageb[tt % 2], r_xnstb[tt % 2]
            lo_stage, r_lost = lo_stageb[tt % 2], r_lostb[tt % 2]
            loT, r_loT = loTb[tt % 2], r_loTb[tt % 2]
            em.op("sp", lambda e, tt=tt, xb=xb: e.dma_start(
                xb[:, :], self.x_h[tt * 128:(tt + 1) * 128, :]),
                writes=[r_xb], dma=True, dbg=f"ldx{tt}")
            em.op("dve", lambda e, xb=xb, stats=stats: e.tensor_reduce(
                stats[:, 0:1], xb[:, :], AX.X, ALU.add),
                reads=[r_xb], writes=[r_stats], dbg="lnsum")
            em.op("act", lambda e, xb=xb, stats=stats: e.activation(
                sq_scr[:, :], xb[:, :], AF.Square, accum_out=stats[:, 2:3]),
                reads=[r_xb], writes=[r_sq, r_stats], dbg="lnsq")
            em.op("dve", lambda e, stats=stats: e.tensor_scalar(
                stats[:, 1:2], stats[:, 0:1], 1.0 / D, None, ALU.mult),
                reads=[r_stats], writes=[r_stats])
            em.op("act", lambda e, stats=stats: e.activation(
                stats[:, 3:4], stats[:, 1:2], AF.Square),
                reads=[r_stats], writes=[r_stats])
            em.op("dve", lambda e, stats=stats: e.tensor_scalar(
                stats[:, 3:4], stats[:, 2:3], 1.0 / D, stats[:, 3:4], ALU.mult,
                ALU.subtract), reads=[r_stats], writes=[r_stats], dbg="var")
            em.op("act", lambda e, stats=stats: e.activation(
                stats[:, 4:5], stats[:, 3:4], AF.Ln, bias=epsc),
                reads=[r_stats, rcc], writes=[r_stats])
            em.op("act", lambda e, stats=stats: e.activation(
                stats[:, 4:5], stats[:, 4:5], AF.Exp, scale=-0.5),
                reads=[r_stats], writes=[r_stats], dbg="rstd")
            em.op("dve", lambda e, stats=stats: e.tensor_scalar(
                stats[:, 5:6], stats[:, 1:2], stats[:, 4:5], -1.0, ALU.mult,
                ALU.mult),
                reads=[r_stats], writes=[r_stats], dbg="mb")
            em.op("act", lambda e, xb=xb, stats=stats, xn_stage=xn_stage:
                  e.activation(
                  xn_stage[:, :], xb[:, :], AF.Identity, scale=stats[:, 4:5],
                  bias=stats[:, 5:6]),
                  reads=[r_xb, r_stats], writes=[r_xnst], dbg="xn16")
            em.op("act", lambda e, xb=xb, stats=stats: e.activation(
                xb[:, :], xb[:, :], AF.Identity, scale=stats[:, 4:5],
                bias=stats[:, 5:6]),
                reads=[r_xb, r_stats], writes=[r_xb], dbg="xnf32")
            em.op("dve", lambda e, xb=xb, xn_stage=xn_stage, lo_stage=lo_stage:
                  e.tensor_tensor(
                  lo_stage[:, :], xb[:, :], xn_stage[:, :], ALU.subtract),
                  reads=[r_xb, r_xnst], writes=[r_lost], dbg="lo")

            for g in range(4):
                bi = g % 2
                pv = pb[bi][:, :].bitcast(BF16)

                def tr_fn(e, g=g, pv=pv, xn_stage=xn_stage):
                    last = None
                    for j in range(4):
                        k = g * 4 + j
                        last = e.transpose(pv[:, j * 128:(j + 1) * 128],
                                           xn_stage[:, k * 128:(k + 1) * 128],
                                           self.id16[:, :])
                    return last
                em.op("pe", tr_fn, reads=[r_xnst, self.r_id16], writes=[r_pb[bi]],
                      dbg=f"trxn{tt}_{g}")

                def ev_fn(e, g=g, tt=tt, pv=pv):
                    dst = self.xnTb[:, :].rearrange(
                        "p (k t) -> p k t", k=KT)[:, g * 4:(g + 1) * 4,
                                                  tt * 128:(tt + 1) * 128]
                    return e.tensor_copy(
                        dst, pv[:, 0:512].rearrange("p (j c) -> p j c", j=4))
                em.op("dve", ev_fn, reads=[r_pb[bi]],
                      writes=[self.r_xnT[g * 4 + j] for j in range(4)],
                      dbg=f"evxn{tt}_{g}")

                bi2 = 2 + g % 2
                pv2 = pb[bi2][:, :].bitcast(BF16)

                def trlo_fn(e, g=g, pv2=pv2, lo_stage=lo_stage):
                    last = None
                    for j in range(4):
                        k = g * 4 + j
                        last = e.transpose(pv2[:, j * 128:(j + 1) * 128],
                                           lo_stage[:, k * 128:(k + 1) * 128],
                                           self.id16[:, :])
                    return last
                em.op("pe", trlo_fn, reads=[r_lost, self.r_id16],
                      writes=[r_pb[bi2]], dbg=f"trlo{tt}_{g}")

                def evlo_fn(e, g=g, pv2=pv2, loT=loT):
                    dst = loT[:, :].rearrange("p (k t) -> p k t", k=KT)[
                        :, g * 4:(g + 1) * 4, :]
                    return e.tensor_copy(
                        dst, pv2[:, 0:512].rearrange("p (j c) -> p j c", j=4))
                em.op("dve", evlo_fn, reads=[r_pb[bi2]], writes=[r_loT],
                      dbg=f"evlo{tt}_{g}")

            def gmm_fn(e, tt=tt, loT=loT):
                w_hi = wg1hi[:, :].rearrange("p (k e) -> p k e", k=KT)
                w_lo = wg1lo[:, :].rearrange("p (k e) -> p k e", k=KT)
                lov = loT[:, :].rearrange("p (k t) -> p k t", k=KT)
                last = None
                for k in range(KT):
                    xcol = self.xnTb[:, :].rearrange("p (k t) -> p k t", k=KT)[
                        :, k, tt * 128:(tt + 1) * 128]
                    last = e.matmul(pb[4][0:8, 0:128], w_hi[:, k], xcol,
                                    start=(k == 0), stop=False)
                    last = e.matmul(pb[4][0:8, 0:128], w_lo[:, k], xcol,
                                    start=False, stop=False)
                    last = e.matmul(pb[4][0:8, 0:128], w_hi[:, k], lov[:, k],
                                    start=False, stop=(k == KT - 1))
                return last
            em.op("pe", gmm_fn,
                  reads=[r_wg1h, r_wg1l, r_loT] + self.r_xnT, writes=[r_pb[4]],
                  dbg=f"gmm{tt}")
            em.op("act", lambda e, tt=tt: e.activation(
                l1T[:, tt * 128:(tt + 1) * 128], pb[4][0:8, 0:128], AF.Tanh),
                reads=[r_pb[4]], writes=[r_l1], dbg=f"tanh{tt}")

        for j in range(4):
            em.op("pe", lambda e, j=j: e.matmul(
                pb[5][0:8, 0:512], wg2f[:, :], l1T[:, j * 512:(j + 1) * 512],
                start=True, stop=True),
                reads=[r_wg2, r_l1], writes=[r_pb[5]], dbg=f"g2_{j}")
            em.op("dve", lambda e, j=j: e.tensor_copy(
                l2T[:, j * 512:(j + 1) * 512], pb[5][0:8, 0:512]),
                reads=[r_pb[5]], writes=[r_l2], dbg=f"g2ev{j}")

        for tt in range(NTT):
            em.op("pe", lambda e, tt=tt: e.transpose(
                self.pbank[6][0:128, 0:8], l2T[0:8, tt * 128:(tt + 1) * 128],
                self.idf[0:8, 0:8]),
                reads=[r_l2, self.r_idf], writes=[r_pb[6]], dbg=f"trl2{tt}")
            em.op("dve", lambda e: e.tensor_copy(lgt[:, 0:8],
                                                 self.pbank[6][0:128, 0:8]),
                  reads=[r_pb[6]], writes=[r_lgt])
            em.op("dve", lambda e: e.reduce_max(tkb[:, 0:1], lgt[:, 0:8], AX.X),
                  reads=[r_lgt], writes=[r_tk])
            em.op("dve", lambda e: e.tensor_scalar(
                tkb[:, 8:16], lgt[:, 0:8], tkb[:, 0:1], None, ALU.is_ge),
                reads=[r_lgt, r_tk], writes=[r_tk], dbg="m1")
            em.op("dve", lambda e: e.scalar_tensor_tensor(
                tkb[:, 16:24], tkb[:, 8:16], -1e30, lgt[:, 0:8], ALU.mult,
                ALU.add), reads=[r_tk, r_lgt], writes=[r_tk], dbg="l2m")
            em.op("dve", lambda e: e.reduce_max(tkb[:, 1:2], tkb[:, 16:24], AX.X),
                  reads=[r_tk], writes=[r_tk])
            em.op("dve", lambda e: e.tensor_scalar(
                tkb[:, 24:32], tkb[:, 16:24], tkb[:, 1:2], None, ALU.is_ge),
                reads=[r_tk], writes=[r_tk], dbg="m2")
            em.op("dve", lambda e: e.tensor_tensor(
                tkb[:, 2:3], tkb[:, 0:1], tkb[:, 1:2], ALU.subtract),
                reads=[r_tk], writes=[r_tk])
            em.op("act", lambda e: e.activation(tkb[:, 3:4], tkb[:, 2:3],
                                                AF.Sigmoid),
                  reads=[r_tk], writes=[r_tk], dbg="sig")
            em.op("dve", lambda e: e.tensor_scalar(
                tkb[:, 4:5], tkb[:, 3:4], 1.0, -1.0, ALU.subtract, ALU.mult),
                reads=[r_tk], writes=[r_tk], dbg="p2")
            em.op("dve", lambda e: e.tensor_scalar(
                tkb[:, 8:16], tkb[:, 8:16], tkb[:, 3:4], None, ALU.mult),
                reads=[r_tk], writes=[r_tk])
            em.op("dve", lambda e: e.scalar_tensor_tensor(
                tkb[:, 32:40], tkb[:, 24:32], tkb[:, 4:5], tkb[:, 8:16],
                ALU.mult, ALU.add),
                reads=[r_tk], writes=[r_tk], dbg="scores")
            em.op("pe", lambda e: e.transpose(
                self.pbank[7][0:8, 0:128], tkb[:, 32:40], self.idf[:, :]),
                reads=[r_tk, self.r_idf], writes=[r_pb[7]], dbg=f"trsc{tt}")
            em.op("dve", lambda e, tt=tt: e.tensor_copy(
                self.scoT[0:8, tt * 128:(tt + 1) * 128],
                self.pbank[7][0:8, 0:128]),
                reads=[r_pb[7]], writes=[self.r_scoT])

        self.scoh_h = self.dint("scoh", [EXP, T], BF16)
        self.r_scoh = em.res("scoh")
        em.op("sp", lambda e: e.dma_start(self.scoh_h, self.scoT[:, :]),
              reads=[self.r_scoT], writes=[self.r_scoh], dma=True)

        if "xnT" in self.dbg:
            o = self.dout("dbg_xnT", [KT * 128, T], BF16)
            for k in range(KT):
                em.op("sp", lambda e, k=k, o=o: e.dma_start(
                    o[k * 128:(k + 1) * 128, :], self.xnT(k)),
                    reads=[self.r_xnT[k]], dma=True)
        if "scoT" in self.dbg:
            o = self.dout("dbg_scoT", [EXP, T], BF16)
            em.op("sp", lambda e, o=o: e.dma_start(o, self.scoT[:, :]),
                  reads=[self.r_scoT], dma=True)
        if "l2T" in self.dbg:
            o = self.dout("dbg_l2T", [EXP, T], F32)
            em.op("sp", lambda e, o=o: e.dma_start(o, l2T[:, :]),
                  reads=[r_l2], dma=True)
        self.close_scope(p0s)

    def stub_out(self):
        em = self.em
        import contextlib as _ctx
        sos = _ctx.ExitStack()
        x_tile = [self.sb(f"so{b}", [128, D], F32, sos) for b in range(2)]
        r_xtile = [em.res(f"so{b}") for b in range(2)]
        for tt in range(NTT):
            xb, r_xb = x_tile[tt % 2], r_xtile[tt % 2]
            em.op("sp", lambda e, tt=tt, xb=xb: e.dma_start(
                xb[:, :], self.x_h[tt * 128:(tt + 1) * 128, :]),
                writes=[r_xb], dma=True)
            em.op("sp", lambda e, tt=tt, xb=xb: e.dma_start(
                self.out_h[tt * 128:(tt + 1) * 128, :], xb[:, :]),
                reads=[r_xb], writes=[self.r_out], dma=True)
        self.close_scope(sos)

    def finish(self):
        st = self.em.finalize(self.nc, self.sems)
        self.stack.close()
        return st


def host_inputs(core_x, w):
    f32 = np.float32
    d = {
        "x": np.ascontiguousarray(core_x, dtype=f32),
        "winT": w["winT"], "woutT": w["woutT"], "wupT": w["wupT"],
        "wdT": w["wdT"], "wg1T": w["wg1T"], "wg2T": w["wg2T"],
        "convw": w["convw"], "convb": w["convb"], "dtb": w["dtb"],
        "alog": w["alog"], "drep": w["drep"], "nw": w["nw"],
        "idf": w["idf"], "tril": w["tril"], "onesrow": w["onesrow"],
        "bupc": w["bupc"], "bdn": w["bdn"], "negHE": w["negHE"],
        "one3": w["one3"],
    }
    return d


def shared_weights(inputs):
    f32 = np.float32
    asc = np.ascontiguousarray
    w = {}
    w["winT"] = asc(np.asarray(inputs["L0_attn_in_proj_w"], f32).T)
    w["woutT"] = asc(np.asarray(inputs["L0_attn_out_proj_w"], f32).T)
    wup = np.asarray(inputs["L0_ffn_Wup"], f32)
    w["wupT"] = asc(wup.transpose(2, 0, 1).reshape(D, EXP * HE))
    wdn = np.asarray(inputs["L0_ffn_Wdown"], f32)
    w["wdT"] = asc(wdn.transpose(0, 2, 1))
    w["wg1T"] = asc(np.asarray(inputs["L0_ffn_Wg1"], f32).T)
    w["wg2T"] = asc(np.asarray(inputs["L0_ffn_Wg2"], f32).T)
    cw = np.asarray(inputs["L0_attn_conv_w"], f32)[:, 0, :]
    w["convw"] = asc(cw.reshape(NBC // 128, 128, 4).transpose(1, 0, 2)
                     .reshape(128, (NBC // 128) * 4))
    cb = np.asarray(inputs["L0_attn_conv_b"], f32)
    w["convb"] = asc(cb.reshape(NBC // 128, 128).T)
    w["dtb"] = asc(np.asarray(inputs["L0_attn_dt_bias"], f32).reshape(NH, 1))
    w["alog"] = asc(np.asarray(inputs["L0_attn_A_log"], f32).reshape(NH, 1))
    w["drep"] = asc(np.repeat(np.asarray(inputs["L0_attn_D"], f32), PP)
                    .reshape(1, I4))
    nw = np.asarray(inputs["L0_attn_norm_w"], f32)
    w["nw"] = asc(nw.reshape(I4 // 128, 128).T)
    bup = np.asarray(inputs["L0_ffn_bup"], f32)
    w["bupc"] = asc(np.concatenate(
        [bup[e].reshape(8, 128).T for e in range(EXP)], axis=1))
    w["bdn"] = asc(np.asarray(inputs["L0_ffn_bdown"], f32))
    he = np.zeros((32, 32, 128), f32)
    for h in range(32):
        he[h, h, :] = -1.0
    w["negHE"] = asc(he.reshape(32, 32 * 128))
    o3 = np.zeros((32, 128), f32)
    o3[0:3, :] = 1.0
    w["one3"] = o3
    w["idf"] = np.eye(128, dtype=f32)
    w["tril"] = np.triu(np.ones((128, 128), f32))
    w["onesrow"] = np.ones((1, 128), f32)
    return w


def kernel(**inputs):
    kb = KB().build()
    st = kb.finish()
    w = shared_weights(inputs)
    x = np.asarray(inputs["L0_attn_hidden_states"], np.float32)
    in_maps = [host_inputs(x[i], w) for i in range(8)]
    res = run_bass_kernel_spmd(kb.nc, in_maps, core_ids=list(range(8)))
    out = np.stack([np.asarray(r["out"]) for r in res.results], axis=0)
    return out
